# version 03
# baseline (speedup 1.0000x reference)
import contextlib
import numpy as np
import ml_dtypes

import concourse.bass as bass
import concourse.mybir as mybir
from concourse.bass_utils import run_bass_kernel_spmd

F32 = mybir.dt.float32
F32R = mybir.dt.float32r
BF16 = mybir.dt.bfloat16
FP8E4 = mybir.dt.float8e4
U8 = mybir.dt.uint8
U32 = mybir.dt.uint32
AL = mybir.AluOpType
AF = mybir.ActivationFunctionType
AX = mybir.AxisListType

from concourse.dve_spec import Spec, Src0, Src1, C0, C1, C2, maxx, minn, lower
from concourse.dve_op import DveOp
import concourse.dve_ops as dve_ops

MAGIC = float(1.5 * 2 ** 23)


def _register_op(name, spec, reference):
    for op in dve_ops.OPS:
        if op.name == name:
            return op
    shas = {}
    for ver in ("v3", "v4"):
        try:
            shas[ver] = lower(spec, ver=ver, name=name).sha(ver)
        except Exception:
            pass
    op = DveOp(name, spec, subdim=False, uops_sha=shas, reference=reference)
    dve_ops.OPS.append(op)
    idx = len(dve_ops.OPS) - 1
    dve_ops._SUB_OPCODE_FOR_NAME[name] = dve_ops._CUSTOM_DVE_ROW_BASE + idx
    assert dve_ops._SUB_OPCODE_FOR_NAME[name] < 0x20
    dve_ops.CUSTOM_DVE_SPECS[name] = spec
    dve_ops.CUSTOM_DVE_REFS[name] = reference
    return op


def _compand_spec():
    d1 = Src0 + Src0
    d2 = Src0 + C1
    m1 = minn(d1, d2)
    e1 = Src0 * C0
    d3 = e1 + (C1 + C1)
    m2 = minn(m1, d3)
    z = m2 + C2
    return Spec(body=z - C2)


def _compand_ref(in0, in1, s0, s1, imm2):
    G = np.minimum(np.minimum(2.0 * in0, in0 + 2.0), 0.5 * in0 + 4.0)
    return np.round(G)


def _uncompand_spec():
    w1 = Src0 + Src0
    w1b = w1 - C0
    v1 = maxx(Src0, w1b)
    w2 = w1 + w1
    w2b = w2 - C1
    v2 = maxx(v1, w2b)
    capped = minn(v2, C2)
    return Spec(body=capped * Src1)


def _uncompand_ref(in0, in1, s0, s1, imm2):
    v = np.maximum(np.maximum(in0, 2.0 * in0 - 4.0), 4.0 * in0 - 16.0)
    return np.minimum(v, 12.0) * in1


COMPAND = _register_op("NVFP4_COMPAND", _compand_spec(), _compand_ref)
UNCOMPAND = _register_op("NVFP4_UNCOMPAND", _uncompand_spec(), _uncompand_ref)

ENGS = ("pe", "dve", "act", "gp", "sync")


class Buf:
    __slots__ = ("w", "rs")

    def __init__(self):
        self.w = None
        self.rs = []


class Prog:
    def __init__(self):
        self.items = []
        self.count = {e: 0 for e in ENGS}
        self.pending = {e: [] for e in ENGS}
        self.level = 1
        self.maxlev = 99
        self.spacer_fns = {}
        self.serialize = False

    def barrier(self):
        snap = [(e, self.count[e]) for e in ENGS if self.count[e] > 0]
        for e in ENGS:
            self.pending[e] = list(snap)

    def emit(self, eng, fn, reads=(), writes=(), deps=(), dma=False):
        if self.level > self.maxlev:
            return None
        if self.serialize or (getattr(self, "cur_tag", "") in getattr(self, "ser_tags", set())):
            self.barrier()
        dd = {}

        def add(d, keep_same=False):
            if d is not None and (keep_same or d[0] != eng):
                dd[d[0]] = max(dd.get(d[0], 0), d[1])
        for d in deps:
            add(d)
        for d in self.pending[eng]:
            add(d, keep_same=True)
        for b in reads:
            add(b.w)
        for b in writes:
            for d in (b.rs + [b.w]):
                add(d)
        self.pending[eng] = []
        inc = 16 if dma else 1
        hazard = self.serialize or getattr(self, "allspace", False)
        if eng in ("dve", "act", "gp"):
            prev_tok = (eng, self.count[eng])
            for b in reads:
                if b.w == prev_tok:
                    hazard = True
            for b in writes:
                if b.w == prev_tok or (b.rs and b.rs[-1] == prev_tok):
                    hazard = True
        self.items.append((eng, fn, sorted(dd.items()), inc, hazard))
        self.count[eng] += inc
        tok = (eng, self.count[eng])
        for b in reads:
            b.rs.append(tok)
        for b in writes:
            b.w = tok
            b.rs = []
        return tok

    def play(self, engname, obj, sems):
        sp = self.spacer_fns.get(engname)
        for item in self.items:
            eng, fn, deps, inc, hazard = item
            if eng != engname:
                continue
            for de, dc in deps:
                obj.wait_ge(sems[de], dc)
            if hazard and sp is not None:
                sp(obj)
            last = fn(obj)
            if last is None:
                last = obj.nop()
            last.then_inc(sems[eng], inc)


class Arena:

    def __init__(self, tensor, nbytes):
        self.t = tensor
        self.nbytes = nbytes
        self.off = 0
        self.marks = []

    def push(self):
        self.marks.append(self.off)

    def pop(self):
        self.off = self.marks.pop()

    def alloc(self, shape, dt=F32):
        p, n = shape
        assert p <= 128
        esz = {F32: 4, BF16: 2, FP8E4: 1, U32: 4, U8: 1}[dt]
        nb = n * esz
        off = (self.off + 511) & ~511
        assert off + nb <= self.nbytes, f"arena overflow: {off + nb} > {self.nbytes}"
        self.off = off + nb
        view = self.t[:, off:off + nb]
        if dt != U8:
            view = view.bitcast(dt)
        return view[0:p, :], Buf()


def build_program(B=2, T=4096, TC=256, phases="abc", maxlev=99):
    D, I = 1024, 4096
    NG = D // 128
    NCHUNK = T // TC
    NTT = (B * T) // 128
    NIC = I // 128
    EPS = 1e-5

    nc = bass.Bass("TRN2", target_bir_lowering=False,
                   dynamic_dma_scratch_size=4096)

    d_x = nc.dram_tensor("x_fm", [B, D, T], F32, kind="ExternalInput").ap()
    d_w = [nc.dram_tensor(n, [D, D], F32, kind="ExternalInput").ap()
           for n in ("wkT", "wvT", "wrT", "woT")]
    d_gw = nc.dram_tensor("gate_w", [I, D], F32, kind="ExternalInput").ap()
    d_uw = nc.dram_tensor("up_w", [I, D], F32, kind="ExternalInput").ap()
    d_dw = nc.dram_tensor("down_w", [D, I], F32, kind="ExternalInput").ap()
    d_vec = nc.dram_tensor("vecs", [6, 128, NG], F32, kind="ExternalInput").ap()
    d_sin = nc.dram_tensor("s_in2", [1, 2], F32, kind="ExternalInput").ap()
    d_id = nc.dram_tensor("ident", [128, 129], F32, kind="ExternalInput").ap()
    d_orow = nc.dram_tensor("onesrow", [1, 128], F32, kind="ExternalInput").ap()
    d_om = nc.dram_tensor("onesmat", [128, 128], F32, kind="ExternalInput").ap()
    d_out = nc.dram_tensor("out", [B * T, D], F32, kind="ExternalOutput").ap()
    import os as _os
    _dbg = _os.environ.get("KDBG", "")
    d_x2tm = nc.dram_tensor("x2tm", [B * T, D], F32,
                            kind="ExternalOutput" if "x2" in _dbg else "Internal").ap()
    d_hq_kind = "ExternalOutput" if "hq" in _dbg else "Internal"
    d_hq = nc.dram_tensor("hq", [NIC, 128, B * T], BF16, kind=d_hq_kind).ap()

    P = Prog()
    import os as _os2
    _kser = _os2.environ.get("KSER", "")
    P.serialize = _kser == "1"
    P.allspace = _os2.environ.get("KSPACE", "0") == "1"
    P.ser_tags = set(_kser.split(",")) if _kser not in ("", "0", "1") else set()
    P.ser_tags.add("su")
    P.cur_tag = ""
    P.maxlev = maxlev
    es = contextlib.ExitStack()

    ARENA_BYTES = 214 * 1024
    arena_t = es.enter_context(nc.sbuf_tensor("arena", [128, ARENA_BYTES], U8))
    A = Arena(arena_t, ARENA_BYTES)

    def SBP(name, shape, dt=F32):
        return es.enter_context(nc.sbuf_tensor(name, shape, dt)), Buf()

    t_id, b_id = SBP("t_id", [128, 129])
    t_idbf, b_idbf = SBP("t_idbf", [128, 128], BF16)
    t_onebf, b_onebf = SBP("t_onebf", [128, 1], BF16)
    t_or, b_or = SBP("t_or", [1, 128])
    t_om, b_om = SBP("t_om", [128, 128])
    t_ombf, b_ombf = SBP("t_ombf", [128, 128], BF16)
    t_vec, b_vec = SBP("t_vec", [128, 6 * NG])
    t_decay, b_decay = SBP("t_decay", [128, NG])
    t_gs, b_gs = SBP("t_gs", [128, 8])
    car_sh, b_csh = SBP("car_sh", [128, NG])
    car_sc, b_csc = SBP("car_sc", [128, NG])
    wmaxt, b_wmax = SBP("wmaxt", [128, 128])
    gsm, b_gsm = SBP("gsm", [1, 4])

    ones_col = t_id[:, 0:1]
    ident = t_id[:, 1:129]
    t_sp, _bsp = SBP("t_sp", [128, 512])
    P.spacer_fns["dve"] = lambda v: v.tensor_copy(out=t_sp[:, 0:256], in_=t_sp[:, 256:512])
    P.spacer_fns["act"] = lambda a: a.copy(out=t_sp[:, 0:256], in_=t_sp[:, 256:512])
    P.spacer_fns["gp"] = lambda g: g.tensor_copy(out=t_sp[:, 0:256], in_=t_sp[:, 256:512])

    pb = []
    bpb = []
    for i in range(8):
        t = es.enter_context(nc.psum_tensor(f"pb{i}", [128, 512], F32))
        pb.append(t)
        bpb.append(Buf())

    P.cur_tag = "su"
    P.emit("sync", lambda s: s.dma_start(out=t_vec[:].rearrange("p (k g) -> p k g", g=NG), in_=d_vec.transpose([1, 0, 2])),
           writes=[b_vec], dma=True)
    P.emit("sync", lambda s: s.dma_start(out=t_gs[0:1, 0:2], in_=d_sin[:]), writes=[b_gs], dma=True)
    P.emit("sync", lambda s: s.dma_start(out=t_id[:], in_=d_id[:]), writes=[b_id], dma=True)
    P.emit("sync", lambda s: s.dma_start(out=t_or[:], in_=d_orow[:]), writes=[b_or], dma=True)
    P.emit("sync", lambda s: s.dma_start(out=t_om[:], in_=d_om[:]), writes=[b_om], dma=True)

    P.emit("act", lambda a: a.activation(out=t_decay[:], in_=t_vec[:, 5 * NG:6 * NG], func=AF.Sigmoid),
           reads=[b_vec], writes=[b_decay])
    P.emit("act", lambda a: a.copy(out=t_idbf[:], in_=ident), reads=[b_id], writes=[b_idbf])
    P.emit("act", lambda a: a.copy(out=t_ombf[:], in_=t_om[:]), reads=[b_om], writes=[b_ombf])
    P.emit("act", lambda a: a.copy(out=t_onebf[:], in_=ones_col), reads=[b_id], writes=[b_onebf])

    P.emit("pe", lambda t: t.matmul(pb[0][:, 0:2], t_or[:], t_gs[0:1, 0:2],
                                    start=True, stop=True),
           reads=[b_or, b_gs], writes=[bpb[0]])

    def f_gs(v):
        v.tensor_scalar(out=t_gs[:, 0:1], in0=pb[0][:, 0:1], scalar1=1.0 / 6.0, scalar2=None, op0=AL.mult)
        v.tensor_scalar(out=t_gs[:, 2:3], in0=pb[0][:, 1:2], scalar1=1.0 / 6.0, scalar2=None, op0=AL.mult)
        v.reciprocal(out=t_gs[:, 1:2], in_=pb[0][:, 0:1])
        return v.reciprocal(out=t_gs[:, 3:4], in_=pb[0][:, 1:2])
    P.emit("dve", f_gs, reads=[bpb[0]], writes=[b_gs])

    def emit_quant(src_ap, b_src, out_ap, b_out, gs6_ap, rgs_ap, NN, q):
        nb = NN // 16
        amax, b_amax = q["amax"]
        scz, b_scz = q["scz"]
        sgn, b_sgn = q["sgn"]
        asg, b_asg = q["asg"]
        fp8t, b_fp8 = q["fp8"]
        msku, b_msk = q["msk"]

        P.emit("dve", lambda v: v.tensor_reduce(
            out=amax[:, 0:nb][:, :, None], in_=src_ap.rearrange("p (nb w) -> p nb w", w=16),
            axis=AX.X, op=AL.max, apply_absolute_value=True),
            reads=[b_src], writes=[b_amax])

        def f_sc(v):
            v.tensor_scalar(out=scz[:, 0:nb], in0=amax[:, 0:nb], scalar1=gs6_ap, scalar2=None, op0=AL.mult)
            v.tensor_scalar(out=scz[:, 0:nb], in0=scz[:, 0:nb], scalar1=0.0, scalar2=448.0,
                            op0=AL.max, op1=AL.min)
            return v.tensor_scalar(out=scz[:, nb:2 * nb], in0=scz[:, 0:nb],
                                   scalar1=float(2 ** 14), scalar2=float(2 ** 14),
                                   op0=AL.add, op1=AL.subtract)
        P.emit("dve", f_sc, reads=[b_amax, b_gs], writes=[b_scz])

        def f_fp8(a):
            a.mul(out=scz[:, 2 * nb:3 * nb], in_=scz[:, 0:nb], mul=0.5)
            a.copy(out=fp8t[:, 0:nb], in_=scz[:, 2 * nb:3 * nb])
            a.activation(out=scz[:, 2 * nb:3 * nb], in_=fp8t[:, 0:nb], func=AF.Copy, scale=2.0)
            a.activation(out=asg[:, 0:NN], in_=src_ap, func=AF.Abs)
            return a.sign(out=sgn[:, 0:NN], in_=src_ap)
        P.emit("act", f_fp8, reads=[b_scz, b_src], writes=[b_scz, b_fp8, b_sgn, b_asg])

        def f_sel(v):
            v.tensor_scalar(out=msku[:, 0:nb], in0=scz[:, 0:nb], scalar1=float(2.0 ** -5),
                            scalar2=None, op0=AL.is_lt)
            v.select(out=scz[:, 0:nb], mask=msku[:, 0:nb], on_true=scz[:, nb:2 * nb],
                     on_false=scz[:, 2 * nb:3 * nb])
            v.tensor_scalar(out=scz[:, 3 * nb:4 * nb], in0=scz[:, 0:nb], scalar1=rgs_ap, scalar2=None, op0=AL.mult)
            v.tensor_scalar(out=scz[:, 4 * nb:5 * nb], in0=scz[:, 3 * nb:4 * nb], scalar1=0.5, scalar2=None, op0=AL.mult)
            v.tensor_scalar(out=scz[:, 5 * nb:6 * nb], in0=scz[:, 3 * nb:4 * nb], scalar1=1e-38, scalar2=None, op0=AL.max)
            return v.reciprocal(out=scz[:, 5 * nb:6 * nb], in_=scz[:, 5 * nb:6 * nb])
        P.emit("dve", f_sel, reads=[b_scz, b_fp8, b_gs], writes=[b_scz, b_msk])

        def rv(ap_):
            return ap_.rearrange("p (nb w) -> p w nb", w=16)

        def bv(col0):
            return scz[:, col0:col0 + nb][:, None, :].broadcast_to([128, 16, nb])

        def f_ratio(v):
            v.tensor_tensor(out=rv(asg[:, 0:NN]), in0=rv(asg[:, 0:NN]), in1=bv(5 * nb), op=AL.mult)
            return v.tensor_tensor(out=rv(sgn[:, 0:NN]), in0=rv(sgn[:, 0:NN]), in1=bv(4 * nb), op=AL.mult)
        P.emit("dve", f_ratio, reads=[b_scz, b_sgn, b_asg], writes=[b_asg, b_sgn])

        def f_cq(v):
            v._custom_dve(COMPAND, out=asg[:, 0:NN], in0=asg[:, 0:NN], s0=0.5, s1=2.0, imm2=MAGIC)
            return v._custom_dve(UNCOMPAND, out=out_ap, in0=asg[:, 0:NN], in1=sgn[:, 0:NN],
                                 s0=4.0, s1=16.0, imm2=12.0)
        P.emit("dve", f_cq, reads=[b_asg, b_sgn], writes=[b_out, b_asg])

    def make_qscratch(NNmax):
        nbm = NNmax // 16
        return {
            "amax": A.alloc([128, nbm]),
            "scz": A.alloc([128, 6 * nbm]),
            "sgn": A.alloc([128, NNmax]),
            "asg": A.alloc([128, NNmax]),
            "fp8": A.alloc([128, nbm], FP8E4),
            "msk": A.alloc([128, nbm], U32),
        }

    def quant_weight(dram_w, nrt, ncols, gs6_ap, rgs_ap, dest_fn, qs, wstage, b_ws,
                     wqt, b_wqt, p_qv, b_pq, b_dest):
        P.emit("dve", lambda v: v.memset(wmaxt[:, 0:1], 0.0), writes=[b_wmax])
        for rt in range(nrt):
            par = rt % 2
            sl = wstage[:, par * ncols:(par + 1) * ncols]
            P.emit("sync", (lambda sl=sl, src=dram_w[rt * 128:(rt + 1) * 128, :]:
                            lambda s: s.dma_start(out=sl, in_=src))(),
                   writes=[b_ws[par]], dma=True)
            P.emit("dve", (lambda sl=sl: lambda v: v.tensor_reduce(
                out=wmaxt[:, 1:2], in_=sl, axis=AX.X, op=AL.max, apply_absolute_value=True))(),
                reads=[b_ws[par]], writes=[b_wmax])
            P.emit("dve", lambda v: v.tensor_tensor(out=wmaxt[:, 0:1], in0=wmaxt[:, 0:1],
                                                    in1=wmaxt[:, 1:2], op=AL.max),
                   reads=[b_wmax], writes=[b_wmax])
        P.emit("pe", lambda t: t.transpose(pb[7][:, 0:128], wmaxt[:], ident),
               reads=[b_wmax, b_id], writes=[bpb[7]])
        P.emit("act", lambda a: a.copy(out=wmaxt[:], in_=pb[7][:, 0:128]),
               reads=[bpb[7]], writes=[b_wmax])
        P.emit("dve", lambda v: v.reduce_max(out=gsm[:, 0:1], in_=wmaxt[0:1, 0:128], axis=AX.X),
               reads=[b_wmax], writes=[b_gsm])
        P.emit("pe", lambda t: t.matmul(pb[7][:, 0:1], t_or[:],
                                        gsm[:, 0:1], start=True, stop=True),
               reads=[b_gsm, b_or], writes=[bpb[7]])

        def f_gsw(v):
            v.reciprocal(out=wmaxt[:, 2:3], in_=pb[7][:, 0:1])
            v.tensor_copy(out=t_sp[:, 0:256], in_=t_sp[:, 256:512])
            v.tensor_scalar(out=gs6_ap, in0=wmaxt[:, 2:3], scalar1=448.0, scalar2=None, op0=AL.mult)
            return v.tensor_scalar(out=rgs_ap, in0=pb[7][:, 0:1], scalar1=1.0 / 2688.0, scalar2=None, op0=AL.mult)
        P.emit("dve", f_gsw, reads=[bpb[7]], writes=[b_gs, b_wmax])
        if "wd" in _dbg and not getattr(P, "_wd2_done", False):
            P._wd2_done = True
            d_gsm = nc.dram_tensor("gsmd", [1, 4], F32, kind="ExternalOutput").ap()
            d_p7 = nc.dram_tensor("p7d", [128, 4], F32, kind="ExternalOutput").ap()
            d_wmx = nc.dram_tensor("wmxd", [128, 128], F32, kind="ExternalOutput").ap()
            P.emit("sync", lambda s: s.dma_start(out=d_gsm, in_=gsm[:]), reads=[b_gsm], dma=True)
            P.emit("act", lambda a: a.copy(out=wmaxt[:, 4:8], in_=pb[7][:, 0:4]), reads=[bpb[7]], writes=[b_wmax])
            P.emit("sync", lambda s: s.dma_start(out=d_p7, in_=wmaxt[:, 4:8]), reads=[b_wmax], dma=True)
            P.emit("sync", lambda s: s.dma_start(out=d_wmx, in_=wmaxt[:]), reads=[b_wmax], dma=True)

        for rt in range(nrt):
            par = rt % 2
            sl = wstage[:, par * ncols:(par + 1) * ncols]
            P.emit("sync", (lambda sl=sl, src=dram_w[rt * 128:(rt + 1) * 128, :]:
                            lambda s: s.dma_start(out=sl, in_=src))(),
                   writes=[b_ws[par]], dma=True)
            emit_quant(sl, b_ws[par], wqt[:, 0:ncols], b_wqt, gs6_ap, rgs_ap, ncols, qs)
            if "wd" in _dbg and rt == 0 and not getattr(P, "_wd_done", False):
                P._wd_done = True
                d_wq0 = nc.dram_tensor("wq0", [128, ncols], BF16, kind="ExternalOutput").ap()
                d_sz0 = nc.dram_tensor("sz0", [128, 6 * (ncols // 16)], F32, kind="ExternalOutput").ap()
                d_am0 = nc.dram_tensor("am0", [128, ncols // 16], F32, kind="ExternalOutput").ap()
                d_src0 = nc.dram_tensor("src0", [128, ncols], F32, kind="ExternalOutput").ap()
                P.emit("sync", (lambda sl=sl: lambda s: s.dma_start(out=d_src0, in_=sl))(), reads=[b_ws[par]], dma=True)
                P.emit("sync", lambda s: s.dma_start(out=d_wq0, in_=wqt[:, 0:ncols]), reads=[b_wqt], dma=True)
                P.emit("sync", lambda s: s.dma_start(out=d_sz0, in_=qs["scz"][0][:, 0:6 * (ncols // 16)]), reads=[qs["scz"][1]], dma=True)
                P.emit("sync", lambda s: s.dma_start(out=d_am0, in_=qs["amax"][0][:, 0:ncols // 16]), reads=[qs["amax"][1]], dma=True)
            for j0 in range(0, ncols // 128, 8):
                jn = min(j0 + 8, ncols // 128)

                def f_trq(t, j0=j0, jn=jn):
                    last = None
                    for j in range(j0, jn):
                        last = t.transpose(p_qv[:, (j - j0) * 128:(j - j0 + 1) * 128],
                                           wqt[:, j * 128:(j + 1) * 128], t_idbf[:])
                    return last
                P.emit("pe", f_trq, reads=[b_wqt, b_idbf], writes=[b_pq])

                def f_pl(a, rt=rt, j0=j0, jn=jn):
                    last = None
                    for j in range(j0, jn):
                        last = a.copy(out=dest_fn(rt, j), in_=p_qv[:, (j - j0) * 128:(j - j0 + 1) * 128])
                    return last
                tok = P.emit("act", f_pl, reads=[b_pq], writes=[b_dest])
                b_dest.rs = []

    if "a" in phases:
        A.push()
        w_attn, b_wattn = A.alloc([128, 4 * NG * D], BF16)

        def wslot(mi, g):
            return w_attn[:, (mi * NG + g) * D:(mi * NG + g + 1) * D]

        A.push()
        stage_w, _ = A.alloc([128, 2 * D])
        b_stage = [Buf(), Buf()]
        for mi in range(4):
            for g in range(NG):
                par = (mi * NG + g) % 2
                sl = stage_w[:, par * D:(par + 1) * D]
                P.emit("sync", (lambda sl=sl, src=d_w[mi][g * 128:(g + 1) * 128, :]:
                                lambda s: s.dma_start(out=sl, in_=src))(),
                       writes=[b_stage[par]], dma=True)
                P.emit("act", (lambda sl=sl, dst=wslot(mi, g): lambda a: a.copy(out=dst, in_=sl))(),
                       reads=[b_stage[par]], writes=[b_wattn])
        A.pop()

        def pair(shape, dt=F32):
            t0, bb0 = A.alloc(shape, dt)
            t1, bb1 = A.alloc(shape, dt)
            return [t0, t1], [bb0, bb1]

        X2p, b_X2p = pair([128, NG * TC])
        XSQ2, b_XSQ2 = pair([128, NG * TC])
        Y2, b_Y2 = pair([128, NG * TC])
        XN2, b_XN2 = pair([128, NG * TC])
        MIX2, b_MIX2 = pair([128, 3 * NG * TC], BF16)
        RSB2, b_RSB2 = pair([128, NG * TC])
        MSB2, b_MSB2 = pair([128, TC])
        RSTDa2, b_RSTDa2 = pair([128, TC])
        RSTDb2, b_RSTDb2 = pair([128, TC])
        SCR, b_SCR = A.alloc([128, TC])
        SF, b_SF = A.alloc([128, NG * TC])
        SBF, b_SBF = A.alloc([128, NG * TC], BF16)
        VSB2, b_VSB2 = pair([128, TC])
        KVSB2, b_KVSB2 = pair([128, TC])
        STG, b_STG = A.alloc([128, D])

        def xg(buf, g):
            return buf[:, g * TC:(g + 1) * TC]

        def g3(buf):
            return buf[:, 0:NG * TC].rearrange("p (g t) -> p g t", t=TC)

        def gt(buf):
            return buf[:, 0:NG * TC].rearrange("p (g t) -> p t g", t=TC)

        def pv(col0):
            return t_vec[:, col0:col0 + NG][:, None, :].broadcast_to([128, TC, NG])

        def bc3(tile):
            return tile[:, 0:TC][:, None, :].broadcast_to([128, NG, TC])

        def ln_stats(src_t, b_src, MSB, b_MSB, RSTD, b_RSTD, R2=None, b_R2=None, XSQ=None, b_XSQ=None):
            P.emit("act", lambda a: a.activation(out=XSQ[:], in_=src_t[:], func=AF.Square),
                   reads=[b_src], writes=[b_XSQ])

            def f_stats(t):
                for g in range(NG):
                    t.matmul(pb[6][:, 0:TC], t_om[:], xg(src_t, g), start=(g == 0), stop=(g == NG - 1))
                last = None
                for g in range(NG):
                    last = t.matmul(pb[6][:, TC:2 * TC], t_om[:], xg(XSQ, g), start=(g == 0), stop=(g == NG - 1))
                return last
            P.emit("pe", f_stats, reads=[b_src, b_XSQ, b_om], writes=[bpb[6]])

            def f_evac(a):
                a.copy(out=MSB[:], in_=pb[6][:, 0:TC])
                return a.copy(out=SCR[:], in_=pb[6][:, TC:2 * TC])
            P.emit("act", f_evac, reads=[bpb[6]], writes=[b_MSB, b_SCR])

            def f_small(v):
                v.tensor_scalar(out=MSB[:], in0=MSB[:], scalar1=1.0 / D, scalar2=None, op0=AL.mult)
                v.tensor_scalar(out=SCR[:], in0=SCR[:], scalar1=1.0 / D, scalar2=None, op0=AL.mult)
                v.tensor_tensor(out=RSTD[:], in0=MSB[:], in1=MSB[:], op=AL.mult)
                v.tensor_tensor(out=SCR[:], in0=SCR[:], in1=RSTD[:], op=AL.subtract)
                return v.tensor_scalar(out=SCR[:], in0=SCR[:], scalar1=EPS, scalar2=None, op0=AL.add)
            P.emit("dve", f_small, reads=[b_MSB, b_SCR], writes=[b_MSB, b_RSTD, b_SCR])
            if R2 is not None:
                def f_u2(v):
                    return v.tensor_scalar(out=R2[:], in0=SCR[:], scalar1=1.0 + EPS,
                                           scalar2=-EPS * EPS, op0=AL.mult, op1=AL.add)
                P.emit("dve", f_u2, reads=[b_SCR], writes=[b_R2])
                P.emit("act", lambda a: a.activation(out=R2[:], in_=R2[:], func=AF.Sqrt),
                       reads=[b_R2], writes=[b_R2])
            P.emit("act", lambda a: a.activation(out=SCR[:], in_=SCR[:], func=AF.Sqrt),
                   reads=[b_SCR], writes=[b_SCR])
            P.emit("dve", lambda v: v.reciprocal(out=RSTD[:], in_=SCR[:]),
                   reads=[b_SCR], writes=[b_RSTD])
            if R2 is not None:
                P.emit("dve", lambda v: v.reciprocal(out=R2[:], in_=R2[:]),
                       reads=[b_R2], writes=[b_R2])

        for b in range(B):
            P.emit("dve", lambda v: (v.memset(car_sh[:], 0.0), v.memset(car_sc[:], 0.0))[-1],
                   writes=[b_csh, b_csc])

            for c in range(NCHUNK):
                cp = (b * NCHUNK + c) % 2
                X, b_X = X2p[cp], b_X2p[cp]
                XSQ, b_XSQ = XSQ2[cp], b_XSQ2[cp]
                Y, b_Y = Y2[cp], b_Y2[cp]
                XN, b_XN = XN2[cp], b_XN2[cp]
                MIX, b_MIX = MIX2[cp], b_MIX2[cp]
                RSB, b_RSB = RSB2[cp], b_RSB2[cp]
                MSB, b_MSB = MSB2[cp], b_MSB2[cp]
                RSTD, b_RSTD = RSTDa2[cp], b_RSTDa2[cp]
                R2T, b_R2T = RSTDb2[cp], b_RSTDb2[cp]

                T0 = c * TC
                src = d_x[b].rearrange("(g p) t -> p g t", p=128)[:, :, T0:T0 + TC]
                P.emit("sync",
                       (lambda src=src, X=X: lambda s: s.dma_start(
                           out=X[:].rearrange("p (g t) -> p g t", t=TC), in_=src))(),
                       writes=[b_X], dma=True)

                P.level = 2
                P.cur_tag = "ln1"
                ln_stats(X, b_X, MSB, b_MSB, RSTD, b_RSTD, R2T, b_R2T, XSQ, b_XSQ)

                def f_ln12(v, X=X, XSQ=XSQ, Y=Y, XN=XN, MSB=MSB, RSTD=RSTD, R2T=R2T):
                    v.tensor_tensor(out=g3(XSQ), in0=g3(X), in1=bc3(MSB), op=AL.subtract)
                    v.tensor_tensor(out=g3(Y), in0=g3(XSQ), in1=bc3(RSTD), op=AL.mult)
                    v.tensor_tensor(out=g3(XN), in0=g3(XSQ), in1=bc3(R2T), op=AL.mult)
                    v.tensor_tensor(out=gt(XN), in0=gt(XN), in1=pv(0), op=AL.mult)
                    return v.tensor_tensor(out=gt(XN), in0=gt(XN), in1=pv(NG), op=AL.add)
                P.emit("dve", f_ln12, reads=[b_X, b_MSB, b_RSTD, b_R2T, b_vec],
                       writes=[b_Y, b_XSQ, b_XN])

                P.level = 4
                P.cur_tag = "mix"

                def f_mix(v, XSQ=XSQ, XN=XN, MIX=MIX):
                    for g in range(NG):
                        v.tensor_tensor(out=xg(XSQ, g)[:, 0:1], in0=xg(XN, g)[:, 0:1], in1=car_sh[:, g:g + 1], op=AL.subtract)
                    for g in range(NG):
                        v.tensor_tensor(out=xg(XSQ, g)[:, 1:], in0=xg(XN, g)[:, 1:], in1=xg(XN, g)[:, :-1], op=AL.subtract)
                    last = None
                    for mi, col in ((0, 2 * NG), (1, 3 * NG), (2, 4 * NG)):
                        mx = MIX[:, mi * NG * TC:(mi + 1) * NG * TC]
                        v.tensor_tensor(out=gt(mx), in0=gt(XSQ), in1=pv(col), op=AL.mult)
                        for g in range(NG):
                            v.tensor_tensor(out=xg(mx, g)[:, 1:], in0=xg(mx, g)[:, 1:], in1=xg(XN, g)[:, :-1], op=AL.add)
                            last = v.tensor_tensor(out=xg(mx, g)[:, 0:1], in0=xg(mx, g)[:, 0:1], in1=car_sh[:, g:g + 1], op=AL.add)
                    for g in range(NG):
                        last = v.tensor_copy(out=car_sh[:, g:g + 1], in_=xg(XN, g)[:, TC - 1:TC])
                    return last
                P.emit("dve", f_mix, reads=[b_XN, b_csh, b_vec], writes=[b_MIX, b_XSQ, b_csh])

                P.level = 5
                P.cur_tag = "kvr"
                for ec in range(NG):
                    par = ec % 2
                    kb, vb, rb = pb[2 * par][:, 0:TC], pb[2 * par][:, TC:2 * TC], pb[2 * par + 1][:, 0:TC]
                    b_kv, b_r = bpb[2 * par], bpb[2 * par + 1]
                    VSB, b_VSB = VSB2[ec % 2], b_VSB2[ec % 2]
                    KVSB, b_KVSB = KVSB2[ec % 2], b_KVSB2[ec % 2]

                    def f_kvr(t, ec=ec, kb=kb, vb=vb, rb=rb, MIX=MIX):
                        for g in range(NG):
                            t.matmul(kb, wslot(0, g)[:, ec * 128:(ec + 1) * 128],
                                     MIX[:, (0 * NG + g) * TC:(0 * NG + g + 1) * TC],
                                     start=(g == 0), stop=(g == NG - 1))
                        for g in range(NG):
                            t.matmul(vb, wslot(1, g)[:, ec * 128:(ec + 1) * 128],
                                     MIX[:, (1 * NG + g) * TC:(1 * NG + g + 1) * TC],
                                     start=(g == 0), stop=(g == NG - 1))
                        last = None
                        for g in range(NG):
                            last = t.matmul(rb, wslot(2, g)[:, ec * 128:(ec + 1) * 128],
                                            MIX[:, (2 * NG + g) * TC:(2 * NG + g + 1) * TC],
                                            start=(g == 0), stop=(g == NG - 1))
                        return last
                    P.emit("pe", f_kvr, reads=[b_MIX, b_wattn], writes=[b_kv, b_r])

                    P.emit("act", (lambda vb=vb, VSB=VSB: lambda a: a.copy(out=VSB[:], in_=vb))(),
                           reads=[b_kv], writes=[b_VSB])
                    P.emit("act", (lambda rb=rb, ec=ec, RSB=RSB: lambda a: a.activation(
                        out=xg(RSB, ec), in_=rb, func=AF.Sigmoid))(),
                        reads=[b_r], writes=[b_RSB])

                    def f_scan(v, ec=ec, kb=kb, VSB=VSB, KVSB=KVSB):
                        v.tensor_tensor(out=KVSB[:], in0=kb, in1=VSB[:], op=AL.mult)
                        v.tensor_tensor_scan(out=xg(SF, ec),
                                             data0=t_decay[:, ec:ec + 1].broadcast_to([128, TC]),
                                             data1=KVSB[:], initial=car_sc[:, ec:ec + 1],
                                             op0=AL.mult, op1=AL.add)
                        v.tensor_copy(out=t_sp[:, 0:256], in_=t_sp[:, 256:512])
                        return v.tensor_copy(out=car_sc[:, ec:ec + 1], in_=xg(SF, ec)[:, TC - 1:TC])
                    P.emit("dve", f_scan, reads=[b_kv, b_VSB, b_decay, b_csc],
                           writes=[b_KVSB, b_SF, b_csc])
                    P.emit("act", (lambda ec=ec: lambda a: a.copy(out=xg(SBF, ec), in_=xg(SF, ec)))(),
                           reads=[b_SF], writes=[b_SBF])

                P.level = 6
                P.cur_tag = "oga"
                for ec in range(NG):
                    ob, b_ob = pb[4 + ec % 2][:, 0:TC], bpb[4 + ec % 2]
                    KVSB, b_KVSB = KVSB2[ec % 2], b_KVSB2[ec % 2]

                    def f_omm(t, ec=ec, ob=ob):
                        last = None
                        for g in range(NG):
                            last = t.matmul(ob, wslot(3, g)[:, ec * 128:(ec + 1) * 128],
                                            xg(SBF, g), start=(g == 0), stop=(g == NG - 1))
                        return last
                    P.emit("pe", f_omm, reads=[b_SBF, b_wattn], writes=[b_ob])

                    def f_gate(v, ec=ec, ob=ob, KVSB=KVSB, RSB=RSB, Y=Y, XN=XN):
                        v.tensor_tensor(out=KVSB[:], in0=ob, in1=xg(RSB, ec), op=AL.mult)
                        return v.tensor_tensor(out=xg(XN, ec), in0=KVSB[:], in1=xg(Y, ec), op=AL.add)
                    P.emit("dve", f_gate, reads=[b_ob, b_RSB, b_Y], writes=[b_KVSB, b_XN])

                P.level = 7
                P.cur_tag = "ln3"
                ln_stats(XN, b_XN, MSB, b_MSB, RSTD, b_RSTD, None, None, XSQ, b_XSQ)

                def f_ln3(v, XSQ=XSQ, XN=XN, X=X, MSB=MSB, RSTD=RSTD):
                    v.tensor_tensor(out=g3(XSQ), in0=g3(XN), in1=bc3(MSB), op=AL.subtract)
                    v.tensor_tensor(out=g3(XSQ), in0=g3(XSQ), in1=bc3(RSTD), op=AL.mult)
                    return v.tensor_tensor(out=X[:], in0=X[:], in1=XSQ[:], op=AL.add)
                P.emit("dve", f_ln3, reads=[b_XN, b_MSB, b_RSTD], writes=[b_X, b_XSQ])

                P.level = 8
                P.cur_tag = "tr"
                for sub in range(TC // 128):
                    for rnd in range(2):
                        def f_tr(t, sub=sub, rnd=rnd, X=X):
                            last = None
                            for j in range(4):
                                g = rnd * 4 + j
                                last = t.transpose(pb[7][:, j * 128:(j + 1) * 128],
                                                   xg(X, g)[:, sub * 128:(sub + 1) * 128], ident)
                            return last
                        P.emit("pe", f_tr, reads=[b_X, b_id], writes=[bpb[7]])

                        def f_ev(a, rnd=rnd):
                            return a.copy(out=STG[:, rnd * 512:(rnd + 1) * 512], in_=pb[7][:, 0:512])
                        P.emit("act", f_ev, reads=[bpb[7]], writes=[b_STG])

                    row0 = b * T + T0 + sub * 128
                    P.emit("sync",
                           (lambda row0=row0: lambda s: s.dma_start(
                               out=d_x2tm[row0:row0 + 128, :], in_=STG[:]))(),
                           reads=[b_STG], dma=True)

        P.level = 1
        P.cur_tag = ""
        A.pop()
        P.barrier()

    if "b" in phases:
        P.cur_tag = "bc"
        A.push()
        w_gup, b_wgup = A.alloc([128, 2 * NG * I], BF16)
        qs = make_qscratch(1024)
        p_qv = pb[6].ap().bitcast(BF16) if hasattr(pb[6], "ap") else pb[6][:, 0:512].bitcast(BF16)
        p_qv = pb[6][:, 0:512].bitcast(BF16)
        b_pq = bpb[6]

        A.push()
        wstage, _ = A.alloc([128, 2 * D])
        b_ws = [Buf(), Buf()]
        wqt, b_wqt = A.alloc([128, D], BF16)
        b_dest = Buf()

        def mk_dest(mi):
            def dest(rt, j):
                return w_gup[:, (mi * NG + j) * I + rt * 128:(mi * NG + j) * I + rt * 128 + 128]
            return dest

        quant_weight(d_gw, I // 128, D, t_gs[:, 4:5], t_gs[:, 5:6], mk_dest(0), qs,
                     wstage, b_ws, wqt, b_wqt, p_qv, b_pq, b_dest)
        quant_weight(d_uw, I // 128, D, t_gs[:, 6:7], t_gs[:, 7:8], mk_dest(1), qs,
                     wstage, b_ws, wqt, b_wqt, p_qv, b_pq, b_dest)
        if "wg" in _dbg:
            d_wg = nc.dram_tensor("wgdump", [128, 2 * NG * I], BF16, kind="ExternalOutput").ap()
            d_gsd = nc.dram_tensor("gsdump", [128, 8], F32, kind="ExternalOutput").ap()
            P.barrier()
            P.emit("sync", lambda s: s.dma_start(out=d_wg, in_=w_gup[:]), reads=[b_wgup], dma=True)
            P.emit("sync", lambda s: s.dma_start(out=d_gsd, in_=t_gs[:]), reads=[b_gs], dma=True)
        A.pop()
        P.barrier()

        _skipB = "wo" in _dbg
        X2, _ = A.alloc([128, 2 * D])
        b_X2 = [Buf(), Buf()]

        def bpair(shape, dt=F32):
            t0, bb0 = A.alloc(shape, dt)
            t1, bb1 = A.alloc(shape, dt)
            return [t0, t1], [bb0, bb1]

        XN42, b_XN42 = bpair([128, D])
        XQ2, b_XQ2 = bpair([128, D], BF16)
        XQF2, b_XQF2 = bpair([128, D], BF16)
        HH2, b_HH2 = bpair([128, 1024])
        SGT2, b_SGT2 = bpair([128, 512])
        SGU2, b_SGU2 = bpair([128, 512])
        HQ2, b_HQ2 = bpair([128, 1024], BF16)
        HQT2, b_HQT2 = bpair([128, 1024], BF16)
        cst2p, b_cst2p = bpair([128, 8])
        qsx = make_qscratch(1024)
        qsh = make_qscratch(1024)

        p_g = [pb[0], pb[1]]
        p_u = [pb[2], pb[3]]
        bg = [bpb[0], bpb[1]]
        bu = [bpb[2], bpb[3]]
        p_tr2 = [pb[6][:, 0:512].bitcast(BF16), pb[7][:, 0:512].bitcast(BF16)]
        b_tr2 = [bpb[6], bpb[7]]

        for tt in range(NTT if not _skipB else 0):
            p2 = tt % 2
            x2sl = X2[:, p2 * D:(p2 + 1) * D]
            XN4, b_XN4 = XN42[p2], b_XN42[p2]
            XQ, b_XQ = XQ2[p2], b_XQ2[p2]
            XQF, b_XQF = XQF2[p2], b_XQF2[p2]
            HH, b_HH = HH2[p2], b_HH2[p2]
            SGT, b_SGT = SGT2[p2], b_SGT2[p2]
            SGU, b_SGU = SGU2[p2], b_SGU2[p2]
            HQ, b_HQ = HQ2[p2], b_HQ2[p2]
            HQT, b_HQT = HQT2[p2], b_HQT2[p2]
            cst, b_cst = cst2p[p2], b_cst2p[p2]
            p_qv2, b_pq2v = p_tr2[p2], b_tr2[p2]

            P.emit("sync", (lambda x2sl=x2sl, src=d_x2tm[tt * 128:(tt + 1) * 128, :]:
                            lambda s: s.dma_start(out=x2sl, in_=src))(),
                   writes=[b_X2[p2]], dma=True)

            P.emit("dve", (lambda x2sl=x2sl, cst=cst: lambda v: v.reduce_sum(out=cst[:, 0:1], in_=x2sl, axis=AX.X))(),
                   reads=[b_X2[p2]], writes=[b_cst])
            P.emit("act", (lambda x2sl=x2sl, cst=cst: lambda a: a.activation(
                out=qsx["asg"][0][:, 0:D], in_=x2sl, func=AF.Square, accum_out=cst[:, 1:2]))(),
                reads=[b_X2[p2]], writes=[qsx["asg"][1], b_cst])

            def f_small(v, cst=cst):
                v.tensor_scalar(out=cst[:, 2:3], in0=cst[:, 0:1], scalar1=1.0 / D, scalar2=None, op0=AL.mult)
                v.tensor_scalar(out=cst[:, 3:4], in0=cst[:, 1:2], scalar1=1.0 / D, scalar2=None, op0=AL.mult)
                v.tensor_copy(out=t_sp[:, 0:256], in_=t_sp[:, 256:512])
                v.tensor_tensor(out=cst[:, 4:5], in0=cst[:, 2:3], in1=cst[:, 2:3], op=AL.mult)
                v.tensor_copy(out=t_sp[:, 0:256], in_=t_sp[:, 256:512])
                v.tensor_tensor(out=cst[:, 3:4], in0=cst[:, 3:4], in1=cst[:, 4:5], op=AL.subtract)
                v.tensor_copy(out=t_sp[:, 0:256], in_=t_sp[:, 256:512])
                return v.tensor_scalar(out=cst[:, 3:4], in0=cst[:, 3:4], scalar1=EPS, scalar2=None, op0=AL.add)
            P.emit("dve", f_small, reads=[b_cst], writes=[b_cst])
            P.emit("act", (lambda cst=cst: lambda a: a.activation(out=cst[:, 4:5], in_=cst[:, 3:4], func=AF.Sqrt))(),
                   reads=[b_cst], writes=[b_cst])
            P.emit("dve", (lambda cst=cst: lambda v: v.reciprocal(out=cst[:, 3:4], in_=cst[:, 4:5]))(),
                   reads=[b_cst], writes=[b_cst])
            P.emit("dve", (lambda x2sl=x2sl, XN4=XN4, cst=cst: lambda v: v.tensor_scalar(
                out=XN4[:], in0=x2sl, scalar1=cst[:, 2:3], scalar2=cst[:, 3:4],
                op0=AL.subtract, op1=AL.mult))(),
                reads=[b_X2[p2], b_cst], writes=[b_XN4])

            emit_quant(XN4[:], b_XN4, XQ[:], b_XQ, t_gs[:, 0:1], t_gs[:, 1:2], D, qsx)

            def f_trx(t, XQ=XQ, p_qv2=p_qv2):
                last = None
                for j in range(8):
                    last = t.transpose(p_qv2[:, j * 128:(j + 1) * 128], XQ[:, j * 128:(j + 1) * 128], t_idbf[:])
                return last
            P.emit("pe", f_trx, reads=[b_XQ, b_idbf], writes=[b_pq2v])
            P.emit("act", (lambda XQF=XQF, p_qv2=p_qv2: lambda a: a.copy(out=XQF[:], in_=p_qv2[:]))(),
                   reads=[b_pq2v], writes=[b_XQF])

            for kk in range(4):
                for half in range(2):
                    hs = kk * 2 + half
                    par = hs % 2

                    def f_gu(t, hs=hs, par=par, XQF=XQF):
                        for hc in range(NG):
                            t.matmul(p_g[par][:, 0:512], XQF[:, hc * 128:(hc + 1) * 128],
                                     w_gup[:, (0 * NG + hc) * I + hs * 512:(0 * NG + hc) * I + hs * 512 + 512],
                                     start=(hc == 0), stop=(hc == NG - 1))
                        last = None
                        for hc in range(NG):
                            last = t.matmul(p_u[par][:, 0:512], XQF[:, hc * 128:(hc + 1) * 128],
                                            w_gup[:, (1 * NG + hc) * I + hs * 512:(1 * NG + hc) * I + hs * 512 + 512],
                                            start=(hc == 0), stop=(hc == NG - 1))
                        return last
                    P.emit("pe", f_gu, reads=[b_XQF, b_wgup], writes=[bg[par], bu[par]])
                    P.emit("act", (lambda par=par, SGT=SGT: lambda a: a.activation(
                        out=SGT[:], in_=p_g[par][:, 0:512], func=AF.Silu))(),
                        reads=[bg[par]], writes=[b_SGT])
                    P.emit("act", (lambda par=par, SGU=SGU: lambda a: a.copy(
                        out=SGU[:], in_=p_u[par][:, 0:512]))(),
                        reads=[bu[par]], writes=[b_SGU])
                    P.emit("gp", (lambda half=half, SGT=SGT, SGU=SGU, HH=HH: lambda g: g.tensor_tensor(
                        out=HH[:, half * 512:(half + 1) * 512], in0=SGT[:], in1=SGU[:], op=AL.mult))(),
                        reads=[b_SGT, b_SGU], writes=[b_HH])

                emit_quant(HH[:], b_HH, HQ[:], b_HQ, t_gs[:, 2:3], t_gs[:, 3:4], 1024, qsh)

                def f_trh(t, HQ=HQ, p_qv2=p_qv2):
                    last = None
                    for j in range(8):
                        last = t.transpose(p_qv2[:, j * 128:(j + 1) * 128], HQ[:, j * 128:(j + 1) * 128], t_idbf[:])
                    return last
                P.emit("pe", f_trh, reads=[b_HQ, b_idbf], writes=[b_pq2v])
                P.emit("act", (lambda HQT=HQT, p_qv2=p_qv2: lambda a: a.copy(out=HQT[:], in_=p_qv2[:]))(),
                       reads=[b_pq2v], writes=[b_HQT])
                dst = d_hq[kk * 8:(kk + 1) * 8, :, tt * 128:(tt + 1) * 128].rearrange("i p t -> p i t")
                P.emit("sync", (lambda dst=dst, HQT=HQT: lambda s: s.dma_start(
                    out=dst, in_=HQT[:].rearrange("p (i t) -> p i t", t=128)))(),
                    reads=[b_HQT], dma=True)

        A.pop()
        P.barrier()


    if "c" in phases:
        A.push()
        w_dn, b_wdn = A.alloc([128, NIC * D], BF16)

        A.push()
        qs2 = make_qscratch(4096)
        wstage2, _ = A.alloc([128, 2 * I])
        b_ws2 = [Buf(), Buf()]
        wqt2, b_wqt2 = A.alloc([128, I], BF16)
        b_dest2 = Buf()
        p_qv2 = pb[6][:, 0:512].bitcast(BF16)
        b_pq2 = bpb[6]

        def dest_dn(rt, j):
            return w_dn[:, j * D + rt * 128:j * D + rt * 128 + 128]

        quant_weight(d_dw, D // 128, I, t_gs[:, 4:5], t_gs[:, 5:6], dest_dn, qs2,
                     wstage2, b_ws2, wqt2, b_wqt2, p_qv2, b_pq2, b_dest2)
        A.pop()
        P.barrier()

        HQF, _ = A.alloc([128, 2 * NIC * 128], BF16)
        b_HQF = [Buf(), Buf()]
        X2c, _ = A.alloc([128, 2 * D])
        b_X2c = [Buf(), Buf()]
        OC, _ = A.alloc([128, 2 * D])
        b_OC = [Buf(), Buf()]
        _s0, _bs0 = A.alloc([128, D])
        _s1, _bs1 = A.alloc([128, D])
        SQSp, b_SQSp = [_s0, _s1], [_bs0, _bs1]
        _c0, _bc0 = A.alloc([128, 8])
        _c1, _bc1 = A.alloc([128, 8])
        cst2p_, b_cst2p_ = [_c0, _c1], [_bc0, _bc1]

        p_o = [pb[0], pb[1], pb[2], pb[3]]
        bo = [bpb[0], bpb[1], bpb[2], bpb[3]]

        for tt in range(NTT):
            p2 = tt % 2
            SQS, b_SQS = SQSp[p2], b_SQSp[p2]
            cst2, b_cst2 = cst2p_[p2], b_cst2p_[p2]
            hsl = HQF[:, p2 * NIC * 128:(p2 + 1) * NIC * 128]
            x2sl = X2c[:, p2 * D:(p2 + 1) * D]
            osl = OC[:, p2 * D:(p2 + 1) * D]
            srch = d_hq[:, :, tt * 128:(tt + 1) * 128].rearrange("i p t -> p i t")
            P.emit("sync", (lambda hsl=hsl, srch=srch: lambda s: s.dma_start(
                out=hsl.rearrange("p (i t) -> p i t", t=128), in_=srch))(),
                writes=[b_HQF[p2]], dma=True)
            P.emit("sync", (lambda x2sl=x2sl, src=d_x2tm[tt * 128:(tt + 1) * 128, :]:
                            lambda s: s.dma_start(out=x2sl, in_=src))(),
                   writes=[b_X2c[p2]], dma=True)

            for half in range(2):
                pj = p2 * 2 + half

                def f_dn(t, hsl=hsl, half=half, pj=pj):
                    last = None
                    for ic in range(NIC):
                        last = t.matmul(p_o[pj][:, 0:512], hsl[:, ic * 128:(ic + 1) * 128],
                                        w_dn[:, ic * D + half * 512:ic * D + half * 512 + 512],
                                        start=(ic == 0), stop=(ic == NIC - 1))
                    return last
                P.emit("pe", f_dn, reads=[b_HQF[p2], b_wdn], writes=[bo[pj]])
                P.emit("dve", (lambda osl=osl, half=half, pj=pj: lambda v: v.tensor_scalar(
                    out=osl[:, half * 512:(half + 1) * 512], in0=p_o[pj][:, 0:512], scalar1=1.0, scalar2=0.0,
                    op0=AL.mult, op1=AL.add, accum_out=cst2[:, half:half + 1]))(),
                    reads=[bo[pj]], writes=[b_OC[p2], b_cst2])

            P.emit("act", (lambda osl=osl: lambda a: a.activation(
                out=SQS[:], in_=osl, func=AF.Square, accum_out=cst2[:, 2:3]))(),
                reads=[b_OC[p2]], writes=[b_SQS, b_cst2])

            def f_small2(v):
                v.tensor_tensor(out=cst2[:, 0:1], in0=cst2[:, 0:1], in1=cst2[:, 1:2], op=AL.add)
                v.tensor_scalar(out=cst2[:, 3:4], in0=cst2[:, 0:1], scalar1=1.0 / D, scalar2=None, op0=AL.mult)
                v.tensor_scalar(out=cst2[:, 4:5], in0=cst2[:, 2:3], scalar1=1.0 / D, scalar2=None, op0=AL.mult)
                v.tensor_tensor(out=cst2[:, 5:6], in0=cst2[:, 3:4], in1=cst2[:, 3:4], op=AL.mult)
                v.tensor_tensor(out=cst2[:, 4:5], in0=cst2[:, 4:5], in1=cst2[:, 5:6], op=AL.subtract)
                return v.tensor_scalar(out=cst2[:, 4:5], in0=cst2[:, 4:5], scalar1=EPS, scalar2=None, op0=AL.add)
            P.emit("dve", f_small2, reads=[b_cst2], writes=[b_cst2])
            P.emit("act", lambda a: a.activation(out=cst2[:, 5:6], in_=cst2[:, 4:5], func=AF.Sqrt),
                   reads=[b_cst2], writes=[b_cst2])
            P.emit("dve", lambda v: v.reciprocal(out=cst2[:, 4:5], in_=cst2[:, 5:6]),
                   reads=[b_cst2], writes=[b_cst2])

            def f_out(v, osl=osl, x2sl=x2sl):
                v.tensor_scalar(out=osl, in0=osl, scalar1=cst2[:, 3:4], scalar2=cst2[:, 4:5],
                                op0=AL.subtract, op1=AL.mult)
                return v.tensor_tensor(out=osl, in0=osl, in1=x2sl, op=AL.add)
            P.emit("dve", f_out, reads=[b_cst2, b_X2c[p2], b_OC[p2]], writes=[b_OC[p2]])

            P.emit("sync", (lambda osl=osl, dst=d_out[tt * 128:(tt + 1) * 128, :]:
                            lambda s: s.dma_start(out=dst, in_=osl))(),
                   reads=[b_OC[p2]], dma=True)

        A.pop()

    P.barrier()
    P.emit("sync", lambda s: s.nop())

    sems = {e: es.enter_context(nc.semaphore(f"sem_{e}")) for e in ENGS}
    block = es.enter_context(nc.Block())

    @block.tensor
    def _(t):
        P.play("pe", t, sems)

    @block.vector
    def _(v):
        P.play("dve", v, sems)

    @block.scalar
    def _(a):
        P.play("act", a, sems)

    @block.gpsimd
    def _(g):
        P.play("gp", g, sems)

    @block.sync
    def _(s):
        P.play("sync", s, sems)

    es.close()
    mybir.codegen_inst_isa_subclasses(nc)
    return nc


def _prep_inputs(inputs, B_core, T, core, n_cores):
    f32 = np.float32
    x = np.asarray(inputs["L0_attn_x"], f32)
    b0 = core * B_core
    x_fm = np.ascontiguousarray(x[b0:b0 + B_core, :T].transpose(0, 2, 1))
    vecs = np.stack([
        np.ascontiguousarray(np.asarray(inputs[k], f32).reshape(8, 128).T)
        for k in ("L0_attn_ln_w", "L0_attn_ln_b", "L0_attn_tmk", "L0_attn_tmv",
                  "L0_attn_tmr", "L0_attn_decay_w")])
    ident = np.zeros((128, 129), f32)
    ident[:, 0] = 1.0
    ident[:, 1:129] = np.eye(128, dtype=f32)
    return {
        "x_fm": x_fm,
        "wkT": np.ascontiguousarray(np.asarray(inputs["L0_attn_Wk"], f32).T),
        "wvT": np.ascontiguousarray(np.asarray(inputs["L0_attn_Wv"], f32).T),
        "wrT": np.ascontiguousarray(np.asarray(inputs["L0_attn_Wr"], f32).T),
        "woT": np.ascontiguousarray(np.asarray(inputs["L0_attn_Wo"], f32).T),
        "gate_w": np.ascontiguousarray(np.asarray(inputs["L0_ffn_gate_w"], f32)),
        "up_w": np.ascontiguousarray(np.asarray(inputs["L0_ffn_up_w"], f32)),
        "down_w": np.ascontiguousarray(np.asarray(inputs["L0_ffn_down_w"], f32)),
        "vecs": np.ascontiguousarray(vecs),
        "s_in2": np.array([[float(np.asarray(inputs["L0_ffn_s_in"]).reshape(-1)[0]),
                            float(np.asarray(inputs["L0_ffn_s_in_down"]).reshape(-1)[0])]], f32),
        "ident": ident,
        "onesrow": np.ones((1, 128), f32),
        "onesmat": np.ones((128, 128), f32),
    }


_NC_CACHE = {}


def kernel(**inputs):
    B_total = int(np.asarray(inputs["L0_attn_x"]).shape[0])
    T = int(np.asarray(inputs["L0_attn_x"]).shape[1])
    n_cores = 8
    B_core = B_total // n_cores
    key = (B_core, T)
    if key not in _NC_CACHE:
        _NC_CACHE[key] = build_program(B=B_core, T=T)
    nc = _NC_CACHE[key]
    in_maps = [_prep_inputs(inputs, B_core, T, c, n_cores) for c in range(n_cores)]
    res = run_bass_kernel_spmd(nc, in_maps, core_ids=list(range(n_cores)))
    outs = [np.asarray(r["out"]).reshape(B_core, T, 1024) for r in res.results]
    return np.concatenate(outs, axis=0)


# version 04
# speedup vs baseline: 1.0070x; 1.0070x over previous
import contextlib
import numpy as np
import ml_dtypes

import concourse.bass as bass
import concourse.mybir as mybir
from concourse.bass_utils import run_bass_kernel_spmd

F32 = mybir.dt.float32
F32R = mybir.dt.float32r
BF16 = mybir.dt.bfloat16
FP8E4 = mybir.dt.float8e4
U8 = mybir.dt.uint8
U32 = mybir.dt.uint32
AL = mybir.AluOpType
AF = mybir.ActivationFunctionType
AX = mybir.AxisListType

from concourse.dve_spec import Spec, Src0, Src1, C0, C1, C2, maxx, minn, lower
from concourse.dve_op import DveOp
import concourse.dve_ops as dve_ops

MAGIC = float(1.5 * 2 ** 23)


def _register_op(name, spec, reference):
    for op in dve_ops.OPS:
        if op.name == name:
            return op
    shas = {}
    for ver in ("v3", "v4"):
        try:
            shas[ver] = lower(spec, ver=ver, name=name).sha(ver)
        except Exception:
            pass
    op = DveOp(name, spec, subdim=False, uops_sha=shas, reference=reference)
    dve_ops.OPS.append(op)
    idx = len(dve_ops.OPS) - 1
    dve_ops._SUB_OPCODE_FOR_NAME[name] = dve_ops._CUSTOM_DVE_ROW_BASE + idx
    assert dve_ops._SUB_OPCODE_FOR_NAME[name] < 0x20
    dve_ops.CUSTOM_DVE_SPECS[name] = spec
    dve_ops.CUSTOM_DVE_REFS[name] = reference
    return op


def _compand_spec():
    d1 = Src0 + Src0
    d2 = Src0 + C1
    m1 = minn(d1, d2)
    e1 = Src0 * C0
    d3 = e1 + (C1 + C1)
    m2 = minn(m1, d3)
    z = m2 + C2
    return Spec(body=z - C2)


def _compand_ref(in0, in1, s0, s1, imm2):
    G = np.minimum(np.minimum(2.0 * in0, in0 + 2.0), 0.5 * in0 + 4.0)
    return np.round(G)


def _uncompand_spec():
    w1 = Src0 + Src0
    w1b = w1 - C0
    v1 = maxx(Src0, w1b)
    w2 = w1 + w1
    w2b = w2 - C1
    v2 = maxx(v1, w2b)
    capped = minn(v2, C2)
    return Spec(body=capped * Src1)


def _uncompand_ref(in0, in1, s0, s1, imm2):
    v = np.maximum(np.maximum(in0, 2.0 * in0 - 4.0), 4.0 * in0 - 16.0)
    return np.minimum(v, 12.0) * in1


COMPAND = _register_op("NVFP4_COMPAND", _compand_spec(), _compand_ref)
UNCOMPAND = _register_op("NVFP4_UNCOMPAND", _uncompand_spec(), _uncompand_ref)

ENGS = ("pe", "dve", "act", "gp", "sync")


class Buf:
    __slots__ = ("w", "rs")

    def __init__(self):
        self.w = None
        self.rs = []


class Prog:
    def __init__(self):
        self.items = []
        self.count = {e: 0 for e in ENGS}
        self.pending = {e: [] for e in ENGS}
        self.level = 1
        self.maxlev = 99
        self.spacer_fns = {}
        self.serialize = False

    def barrier(self):
        snap = [(e, self.count[e]) for e in ENGS if self.count[e] > 0]
        for e in ENGS:
            self.pending[e] = list(snap)

    def emit(self, eng, fn, reads=(), writes=(), deps=(), dma=False):
        if self.level > self.maxlev:
            return None
        if self.serialize or (getattr(self, "cur_tag", "") in getattr(self, "ser_tags", set())):
            self.barrier()
        dd = {}

        def add(d, keep_same=False):
            if d is not None and (keep_same or d[0] != eng):
                dd[d[0]] = max(dd.get(d[0], 0), d[1])
        for d in deps:
            add(d)
        for d in self.pending[eng]:
            add(d, keep_same=True)
        for b in reads:
            add(b.w)
        for b in writes:
            for d in (b.rs + [b.w]):
                add(d)
        self.pending[eng] = []
        inc = 16 if dma else 1
        hazard = self.serialize or getattr(self, "allspace", False)
        if eng in ("dve", "act", "gp"):
            prev_tok = (eng, self.count[eng])
            for b in reads:
                if b.w == prev_tok:
                    hazard = True
            for b in writes:
                if b.w == prev_tok or (b.rs and b.rs[-1] == prev_tok):
                    hazard = True
        self.items.append((eng, fn, sorted(dd.items()), inc, hazard))
        self.count[eng] += inc
        tok = (eng, self.count[eng])
        for b in reads:
            b.rs.append(tok)
        for b in writes:
            b.w = tok
            b.rs = []
        return tok

    def play(self, engname, obj, sems):
        sp = self.spacer_fns.get(engname)
        for item in self.items:
            eng, fn, deps, inc, hazard = item
            if eng != engname:
                continue
            for de, dc in deps:
                obj.wait_ge(sems[de], dc)
            if hazard and sp is not None:
                sp(obj)
            last = fn(obj)
            if last is None:
                last = obj.nop()
            last.then_inc(sems[eng], inc)


class Arena:

    def __init__(self, tensor, nbytes):
        self.t = tensor
        self.nbytes = nbytes
        self.off = 0
        self.marks = []

    def push(self):
        self.marks.append(self.off)

    def pop(self):
        self.off = self.marks.pop()

    def alloc(self, shape, dt=F32):
        p, n = shape
        assert p <= 128
        esz = {F32: 4, BF16: 2, FP8E4: 1, U32: 4, U8: 1}[dt]
        nb = n * esz
        off = (self.off + 511) & ~511
        assert off + nb <= self.nbytes, f"arena overflow: {off + nb} > {self.nbytes}"
        self.off = off + nb
        view = self.t[:, off:off + nb]
        if dt != U8:
            view = view.bitcast(dt)
        return view[0:p, :], Buf()


def build_program(B=2, T=4096, TC=256, phases="abc", maxlev=99):
    D, I = 1024, 4096
    NG = D // 128
    NCHUNK = T // TC
    NTT = (B * T) // 128
    NIC = I // 128
    EPS = 1e-5

    nc = bass.Bass("TRN2", target_bir_lowering=False,
                   dynamic_dma_scratch_size=4096)

    d_x = nc.dram_tensor("x_fm", [B, D, T], F32, kind="ExternalInput").ap()
    d_w = [nc.dram_tensor(n, [D, D], F32, kind="ExternalInput").ap()
           for n in ("wkT", "wvT", "wrT", "woT")]
    d_gw = nc.dram_tensor("gate_w", [I, D], F32, kind="ExternalInput").ap()
    d_uw = nc.dram_tensor("up_w", [I, D], F32, kind="ExternalInput").ap()
    d_dw = nc.dram_tensor("down_w", [D, I], F32, kind="ExternalInput").ap()
    d_vec = nc.dram_tensor("vecs", [6, 128, NG], F32, kind="ExternalInput").ap()
    d_sin = nc.dram_tensor("s_in2", [1, 2], F32, kind="ExternalInput").ap()
    d_id = nc.dram_tensor("ident", [128, 129], F32, kind="ExternalInput").ap()
    d_orow = nc.dram_tensor("onesrow", [1, 128], F32, kind="ExternalInput").ap()
    d_om = nc.dram_tensor("onesmat", [128, 128], F32, kind="ExternalInput").ap()
    d_out = nc.dram_tensor("out", [B * T, D], F32, kind="ExternalOutput").ap()
    import os as _os
    _dbg = _os.environ.get("KDBG", "")
    d_x2tm = nc.dram_tensor("x2tm", [B * T, D], F32,
                            kind="ExternalOutput" if "x2" in _dbg else "Internal").ap()
    d_hq_kind = "ExternalOutput" if "hq" in _dbg else "Internal"
    d_hq = nc.dram_tensor("hq", [NIC, 128, B * T], BF16, kind=d_hq_kind).ap()

    P = Prog()
    import os as _os2
    _kser = _os2.environ.get("KSER", "")
    P.serialize = _kser == "1"
    P.allspace = _os2.environ.get("KSPACE", "0") == "1"
    P.ser_tags = set(_kser.split(",")) if _kser not in ("", "0", "1") else set()
    P.ser_tags.add("su")
    P.cur_tag = ""
    P.maxlev = maxlev
    es = contextlib.ExitStack()

    ARENA_BYTES = 214 * 1024
    arena_t = es.enter_context(nc.sbuf_tensor("arena", [128, ARENA_BYTES], U8))
    A = Arena(arena_t, ARENA_BYTES)

    def SBP(name, shape, dt=F32):
        return es.enter_context(nc.sbuf_tensor(name, shape, dt)), Buf()

    t_id, b_id = SBP("t_id", [128, 129])
    t_idbf, b_idbf = SBP("t_idbf", [128, 128], BF16)
    t_onebf, b_onebf = SBP("t_onebf", [128, 1], BF16)
    t_or, b_or = SBP("t_or", [1, 128])
    t_om, b_om = SBP("t_om", [128, 128])
    t_ombf, b_ombf = SBP("t_ombf", [128, 128], BF16)
    t_vec, b_vec = SBP("t_vec", [128, 6 * NG])
    t_decay, b_decay = SBP("t_decay", [128, NG])
    t_gs, b_gs = SBP("t_gs", [128, 8])
    car_sh, b_csh = SBP("car_sh", [128, NG])
    car_sc, b_csc = SBP("car_sc", [128, NG])
    wmaxt, b_wmax = SBP("wmaxt", [128, 128])
    gsm, b_gsm = SBP("gsm", [1, 4])

    ones_col = t_id[:, 0:1]
    ident = t_id[:, 1:129]
    t_sp, _bsp = SBP("t_sp", [128, 512])
    P.spacer_fns["dve"] = lambda v: v.tensor_copy(out=t_sp[:, 0:256], in_=t_sp[:, 256:512])
    P.spacer_fns["act"] = lambda a: a.copy(out=t_sp[:, 0:256], in_=t_sp[:, 256:512])
    P.spacer_fns["gp"] = lambda g: g.tensor_copy(out=t_sp[:, 0:256], in_=t_sp[:, 256:512])

    pb = []
    bpb = []
    for i in range(8):
        t = es.enter_context(nc.psum_tensor(f"pb{i}", [128, 512], F32))
        pb.append(t)
        bpb.append(Buf())

    P.cur_tag = "su"
    P.emit("sync", lambda s: s.dma_start(out=t_vec[:].rearrange("p (k g) -> p k g", g=NG), in_=d_vec.transpose([1, 0, 2])),
           writes=[b_vec], dma=True)
    P.emit("sync", lambda s: s.dma_start(out=t_gs[0:1, 0:2], in_=d_sin[:]), writes=[b_gs], dma=True)
    P.emit("sync", lambda s: s.dma_start(out=t_id[:], in_=d_id[:]), writes=[b_id], dma=True)
    P.emit("sync", lambda s: s.dma_start(out=t_or[:], in_=d_orow[:]), writes=[b_or], dma=True)
    P.emit("sync", lambda s: s.dma_start(out=t_om[:], in_=d_om[:]), writes=[b_om], dma=True)

    P.emit("act", lambda a: a.activation(out=t_decay[:], in_=t_vec[:, 5 * NG:6 * NG], func=AF.Sigmoid),
           reads=[b_vec], writes=[b_decay])
    P.emit("act", lambda a: a.copy(out=t_idbf[:], in_=ident), reads=[b_id], writes=[b_idbf])
    P.emit("act", lambda a: a.copy(out=t_ombf[:], in_=t_om[:]), reads=[b_om], writes=[b_ombf])
    P.emit("act", lambda a: a.copy(out=t_onebf[:], in_=ones_col), reads=[b_id], writes=[b_onebf])

    P.emit("pe", lambda t: t.matmul(pb[0][:, 0:2], t_or[:], t_gs[0:1, 0:2],
                                    start=True, stop=True),
           reads=[b_or, b_gs], writes=[bpb[0]])

    def f_gs(v):
        v.tensor_scalar(out=t_gs[:, 0:1], in0=pb[0][:, 0:1], scalar1=1.0 / 6.0, scalar2=None, op0=AL.mult)
        v.tensor_scalar(out=t_gs[:, 2:3], in0=pb[0][:, 1:2], scalar1=1.0 / 6.0, scalar2=None, op0=AL.mult)
        v.reciprocal(out=t_gs[:, 1:2], in_=pb[0][:, 0:1])
        return v.reciprocal(out=t_gs[:, 3:4], in_=pb[0][:, 1:2])
    P.emit("dve", f_gs, reads=[bpb[0]], writes=[b_gs])

    def emit_quant(src_ap, b_src, out_ap, b_out, gs6_ap, rgs_ap, NN, q):
        nb = NN // 16
        amax, b_amax = q["amax"]
        scz, b_scz = q["scz"]
        sgn, b_sgn = q["sgn"]
        asg, b_asg = q["asg"]
        fp8t, b_fp8 = q["fp8"]
        msku, b_msk = q["msk"]

        P.emit("dve", lambda v: v.tensor_reduce(
            out=amax[:, 0:nb][:, :, None], in_=src_ap.rearrange("p (nb w) -> p nb w", w=16),
            axis=AX.X, op=AL.max, apply_absolute_value=True),
            reads=[b_src], writes=[b_amax])

        def f_sc(v):
            v.tensor_scalar(out=scz[:, 0:nb], in0=amax[:, 0:nb], scalar1=gs6_ap, scalar2=None, op0=AL.mult)
            v.tensor_scalar(out=scz[:, 0:nb], in0=scz[:, 0:nb], scalar1=0.0, scalar2=448.0,
                            op0=AL.max, op1=AL.min)
            return v.tensor_scalar(out=scz[:, nb:2 * nb], in0=scz[:, 0:nb],
                                   scalar1=float(2 ** 14), scalar2=float(2 ** 14),
                                   op0=AL.add, op1=AL.subtract)
        P.emit("dve", f_sc, reads=[b_amax, b_gs], writes=[b_scz])

        def f_fp8(a):
            a.mul(out=scz[:, 2 * nb:3 * nb], in_=scz[:, 0:nb], mul=0.5)
            a.copy(out=fp8t[:, 0:nb], in_=scz[:, 2 * nb:3 * nb])
            a.activation(out=scz[:, 2 * nb:3 * nb], in_=fp8t[:, 0:nb], func=AF.Copy, scale=2.0)
            a.activation(out=asg[:, 0:NN], in_=src_ap, func=AF.Abs)
            return a.sign(out=sgn[:, 0:NN], in_=src_ap)
        P.emit("act", f_fp8, reads=[b_scz, b_src], writes=[b_scz, b_fp8, b_sgn, b_asg])

        def f_sel(v):
            v.tensor_scalar(out=msku[:, 0:nb], in0=scz[:, 0:nb], scalar1=float(2.0 ** -5),
                            scalar2=None, op0=AL.is_lt)
            v.select(out=scz[:, 0:nb], mask=msku[:, 0:nb], on_true=scz[:, nb:2 * nb],
                     on_false=scz[:, 2 * nb:3 * nb])
            v.tensor_scalar(out=scz[:, 3 * nb:4 * nb], in0=scz[:, 0:nb], scalar1=rgs_ap, scalar2=None, op0=AL.mult)
            v.tensor_scalar(out=scz[:, 4 * nb:5 * nb], in0=scz[:, 3 * nb:4 * nb], scalar1=0.5, scalar2=None, op0=AL.mult)
            v.tensor_scalar(out=scz[:, 5 * nb:6 * nb], in0=scz[:, 3 * nb:4 * nb], scalar1=1e-38, scalar2=None, op0=AL.max)
            return v.reciprocal(out=scz[:, 5 * nb:6 * nb], in_=scz[:, 5 * nb:6 * nb])
        P.emit("dve", f_sel, reads=[b_scz, b_fp8, b_gs], writes=[b_scz, b_msk])

        def rv(ap_):
            return ap_.rearrange("p (nb w) -> p w nb", w=16)

        def bv(col0):
            return scz[:, col0:col0 + nb][:, None, :].broadcast_to([128, 16, nb])

        def f_ratio(v):
            v.tensor_tensor(out=rv(asg[:, 0:NN]), in0=rv(asg[:, 0:NN]), in1=bv(5 * nb), op=AL.mult)
            return v.tensor_tensor(out=rv(sgn[:, 0:NN]), in0=rv(sgn[:, 0:NN]), in1=bv(4 * nb), op=AL.mult)
        P.emit("dve", f_ratio, reads=[b_scz, b_sgn, b_asg], writes=[b_asg, b_sgn])

        def f_cq(v):
            v._custom_dve(COMPAND, out=asg[:, 0:NN], in0=asg[:, 0:NN], s0=0.5, s1=2.0, imm2=MAGIC)
            return v._custom_dve(UNCOMPAND, out=out_ap, in0=asg[:, 0:NN], in1=sgn[:, 0:NN],
                                 s0=4.0, s1=16.0, imm2=12.0)
        P.emit("dve", f_cq, reads=[b_asg, b_sgn], writes=[b_out, b_asg])

    def make_qscratch(NNmax):
        nbm = NNmax // 16
        return {
            "amax": A.alloc([128, nbm]),
            "scz": A.alloc([128, 6 * nbm]),
            "sgn": A.alloc([128, NNmax]),
            "asg": A.alloc([128, NNmax]),
            "fp8": A.alloc([128, nbm], FP8E4),
            "msk": A.alloc([128, nbm], U32),
        }

    def quant_weight(dram_w, nrt, ncols, gs6_ap, rgs_ap, dest_fn, qs, wstage, b_ws,
                     wqt, b_wqt, p_qv, b_pq, b_dest):
        P.emit("dve", lambda v: v.memset(wmaxt[:, 0:1], 0.0), writes=[b_wmax])
        for rt in range(nrt):
            par = rt % 2
            sl = wstage[:, par * ncols:(par + 1) * ncols]
            P.emit("sync", (lambda sl=sl, src=dram_w[rt * 128:(rt + 1) * 128, :]:
                            lambda s: s.dma_start(out=sl, in_=src))(),
                   writes=[b_ws[par]], dma=True)
            P.emit("dve", (lambda sl=sl: lambda v: v.tensor_reduce(
                out=wmaxt[:, 1:2], in_=sl, axis=AX.X, op=AL.max, apply_absolute_value=True))(),
                reads=[b_ws[par]], writes=[b_wmax])
            P.emit("dve", lambda v: v.tensor_tensor(out=wmaxt[:, 0:1], in0=wmaxt[:, 0:1],
                                                    in1=wmaxt[:, 1:2], op=AL.max),
                   reads=[b_wmax], writes=[b_wmax])
        P.emit("pe", lambda t: t.transpose(pb[7][:, 0:128], wmaxt[:], ident),
               reads=[b_wmax, b_id], writes=[bpb[7]])
        P.emit("act", lambda a: a.copy(out=wmaxt[:], in_=pb[7][:, 0:128]),
               reads=[bpb[7]], writes=[b_wmax])
        P.emit("dve", lambda v: v.reduce_max(out=gsm[:, 0:1], in_=wmaxt[0:1, 0:128], axis=AX.X),
               reads=[b_wmax], writes=[b_gsm])
        P.emit("pe", lambda t: t.matmul(pb[7][:, 0:1], t_or[:],
                                        gsm[:, 0:1], start=True, stop=True),
               reads=[b_gsm, b_or], writes=[bpb[7]])

        def f_gsw(v):
            v.reciprocal(out=wmaxt[:, 2:3], in_=pb[7][:, 0:1])
            v.tensor_copy(out=t_sp[:, 0:256], in_=t_sp[:, 256:512])
            v.tensor_scalar(out=gs6_ap, in0=wmaxt[:, 2:3], scalar1=448.0, scalar2=None, op0=AL.mult)
            return v.tensor_scalar(out=rgs_ap, in0=pb[7][:, 0:1], scalar1=1.0 / 2688.0, scalar2=None, op0=AL.mult)
        P.emit("dve", f_gsw, reads=[bpb[7]], writes=[b_gs, b_wmax])
        if "wd" in _dbg and not getattr(P, "_wd2_done", False):
            P._wd2_done = True
            d_gsm = nc.dram_tensor("gsmd", [1, 4], F32, kind="ExternalOutput").ap()
            d_p7 = nc.dram_tensor("p7d", [128, 4], F32, kind="ExternalOutput").ap()
            d_wmx = nc.dram_tensor("wmxd", [128, 128], F32, kind="ExternalOutput").ap()
            P.emit("sync", lambda s: s.dma_start(out=d_gsm, in_=gsm[:]), reads=[b_gsm], dma=True)
            P.emit("act", lambda a: a.copy(out=wmaxt[:, 4:8], in_=pb[7][:, 0:4]), reads=[bpb[7]], writes=[b_wmax])
            P.emit("sync", lambda s: s.dma_start(out=d_p7, in_=wmaxt[:, 4:8]), reads=[b_wmax], dma=True)
            P.emit("sync", lambda s: s.dma_start(out=d_wmx, in_=wmaxt[:]), reads=[b_wmax], dma=True)

        for rt in range(nrt):
            par = rt % 2
            sl = wstage[:, par * ncols:(par + 1) * ncols]
            P.emit("sync", (lambda sl=sl, src=dram_w[rt * 128:(rt + 1) * 128, :]:
                            lambda s: s.dma_start(out=sl, in_=src))(),
                   writes=[b_ws[par]], dma=True)
            emit_quant(sl, b_ws[par], wqt[:, 0:ncols], b_wqt, gs6_ap, rgs_ap, ncols, qs)
            if "wd" in _dbg and rt == 0 and not getattr(P, "_wd_done", False):
                P._wd_done = True
                d_wq0 = nc.dram_tensor("wq0", [128, ncols], BF16, kind="ExternalOutput").ap()
                d_sz0 = nc.dram_tensor("sz0", [128, 6 * (ncols // 16)], F32, kind="ExternalOutput").ap()
                d_am0 = nc.dram_tensor("am0", [128, ncols // 16], F32, kind="ExternalOutput").ap()
                d_src0 = nc.dram_tensor("src0", [128, ncols], F32, kind="ExternalOutput").ap()
                P.emit("sync", (lambda sl=sl: lambda s: s.dma_start(out=d_src0, in_=sl))(), reads=[b_ws[par]], dma=True)
                P.emit("sync", lambda s: s.dma_start(out=d_wq0, in_=wqt[:, 0:ncols]), reads=[b_wqt], dma=True)
                P.emit("sync", lambda s: s.dma_start(out=d_sz0, in_=qs["scz"][0][:, 0:6 * (ncols // 16)]), reads=[qs["scz"][1]], dma=True)
                P.emit("sync", lambda s: s.dma_start(out=d_am0, in_=qs["amax"][0][:, 0:ncols // 16]), reads=[qs["amax"][1]], dma=True)
            for j0 in range(0, ncols // 128, 8):
                jn = min(j0 + 8, ncols // 128)

                def f_trq(t, j0=j0, jn=jn):
                    last = None
                    for j in range(j0, jn):
                        last = t.transpose(p_qv[:, (j - j0) * 128:(j - j0 + 1) * 128],
                                           wqt[:, j * 128:(j + 1) * 128], t_idbf[:])
                    return last
                P.emit("pe", f_trq, reads=[b_wqt, b_idbf], writes=[b_pq])

                def f_pl(a, rt=rt, j0=j0, jn=jn):
                    last = None
                    for j in range(j0, jn):
                        last = a.copy(out=dest_fn(rt, j), in_=p_qv[:, (j - j0) * 128:(j - j0 + 1) * 128])
                    return last
                tok = P.emit("act", f_pl, reads=[b_pq], writes=[b_dest])
                b_dest.rs = []

    if "a" in phases:
        A.push()
        w_attn, b_wattn = A.alloc([128, 4 * NG * D], BF16)

        def wslot(mi, g):
            return w_attn[:, (mi * NG + g) * D:(mi * NG + g + 1) * D]

        A.push()
        stage_w, _ = A.alloc([128, 2 * D])
        b_stage = [Buf(), Buf()]
        for mi in range(4):
            for g in range(NG):
                par = (mi * NG + g) % 2
                sl = stage_w[:, par * D:(par + 1) * D]
                P.emit("sync", (lambda sl=sl, src=d_w[mi][g * 128:(g + 1) * 128, :]:
                                lambda s: s.dma_start(out=sl, in_=src))(),
                       writes=[b_stage[par]], dma=True)
                P.emit("act", (lambda sl=sl, dst=wslot(mi, g): lambda a: a.copy(out=dst, in_=sl))(),
                       reads=[b_stage[par]], writes=[b_wattn])
        A.pop()

        def pair(shape, dt=F32):
            t0, bb0 = A.alloc(shape, dt)
            t1, bb1 = A.alloc(shape, dt)
            return [t0, t1], [bb0, bb1]

        X2p, b_X2p = pair([128, NG * TC])
        XSQ2, b_XSQ2 = pair([128, NG * TC])
        Y2, b_Y2 = pair([128, NG * TC])
        XN2, b_XN2 = pair([128, NG * TC])
        MIX2, b_MIX2 = pair([128, 3 * NG * TC], BF16)
        RSB2, b_RSB2 = pair([128, NG * TC])
        MSB2, b_MSB2 = pair([128, TC])
        RSTDa2, b_RSTDa2 = pair([128, TC])
        RSTDb2, b_RSTDb2 = pair([128, TC])
        SCR, b_SCR = A.alloc([128, TC])
        SF, b_SF = A.alloc([128, NG * TC])
        SBF, b_SBF = A.alloc([128, NG * TC], BF16)
        VSB2, b_VSB2 = pair([128, TC])
        KVSB2, b_KVSB2 = pair([128, TC])
        STG, b_STG = A.alloc([128, D])

        def xg(buf, g):
            return buf[:, g * TC:(g + 1) * TC]

        def g3(buf):
            return buf[:, 0:NG * TC].rearrange("p (g t) -> p g t", t=TC)

        def gt(buf):
            return buf[:, 0:NG * TC].rearrange("p (g t) -> p t g", t=TC)

        def pv(col0):
            return t_vec[:, col0:col0 + NG][:, None, :].broadcast_to([128, TC, NG])

        def bc3(tile):
            return tile[:, 0:TC][:, None, :].broadcast_to([128, NG, TC])

        def ln_stats(src_t, b_src, MSB, b_MSB, RSTD, b_RSTD, R2=None, b_R2=None, XSQ=None, b_XSQ=None):
            P.emit("act", lambda a: a.activation(out=XSQ[:], in_=src_t[:], func=AF.Square),
                   reads=[b_src], writes=[b_XSQ])

            def f_stats(t):
                for g in range(NG):
                    t.matmul(pb[6][:, 0:TC], t_om[:], xg(src_t, g), start=(g == 0), stop=(g == NG - 1))
                last = None
                for g in range(NG):
                    last = t.matmul(pb[6][:, TC:2 * TC], t_om[:], xg(XSQ, g), start=(g == 0), stop=(g == NG - 1))
                return last
            P.emit("pe", f_stats, reads=[b_src, b_XSQ, b_om], writes=[bpb[6]])

            def f_evac(a):
                a.copy(out=MSB[:], in_=pb[6][:, 0:TC])
                return a.copy(out=SCR[:], in_=pb[6][:, TC:2 * TC])
            P.emit("act", f_evac, reads=[bpb[6]], writes=[b_MSB, b_SCR])

            def f_small(v):
                v.tensor_scalar(out=MSB[:], in0=MSB[:], scalar1=1.0 / D, scalar2=None, op0=AL.mult)
                v.tensor_scalar(out=SCR[:], in0=SCR[:], scalar1=1.0 / D, scalar2=None, op0=AL.mult)
                v.tensor_tensor(out=RSTD[:], in0=MSB[:], in1=MSB[:], op=AL.mult)
                v.tensor_tensor(out=SCR[:], in0=SCR[:], in1=RSTD[:], op=AL.subtract)
                return v.tensor_scalar(out=SCR[:], in0=SCR[:], scalar1=EPS, scalar2=None, op0=AL.add)
            P.emit("dve", f_small, reads=[b_MSB, b_SCR], writes=[b_MSB, b_RSTD, b_SCR])
            if R2 is not None:
                def f_u2(v):
                    return v.tensor_scalar(out=R2[:], in0=SCR[:], scalar1=1.0 + EPS,
                                           scalar2=-EPS * EPS, op0=AL.mult, op1=AL.add)
                P.emit("dve", f_u2, reads=[b_SCR], writes=[b_R2])
                P.emit("act", lambda a: a.activation(out=R2[:], in_=R2[:], func=AF.Sqrt),
                       reads=[b_R2], writes=[b_R2])
            P.emit("act", lambda a: a.activation(out=SCR[:], in_=SCR[:], func=AF.Sqrt),
                   reads=[b_SCR], writes=[b_SCR])
            P.emit("dve", lambda v: v.reciprocal(out=RSTD[:], in_=SCR[:]),
                   reads=[b_SCR], writes=[b_RSTD])
            if R2 is not None:
                P.emit("dve", lambda v: v.reciprocal(out=R2[:], in_=R2[:]),
                       reads=[b_R2], writes=[b_R2])

        t_zero, b_zero = A.alloc([128, NG])
        P.emit("dve", lambda v: v.memset(t_zero[:], 0.0), writes=[b_zero])

        def emit_front(b, c):
                cp = (b * NCHUNK + c) % 2
                X, b_X = X2p[cp], b_X2p[cp]
                XSQ, b_XSQ = XSQ2[cp], b_XSQ2[cp]
                Y, b_Y = Y2[cp], b_Y2[cp]
                XN, b_XN = XN2[cp], b_XN2[cp]
                MIX, b_MIX = MIX2[cp], b_MIX2[cp]
                RSB, b_RSB = RSB2[cp], b_RSB2[cp]
                MSB, b_MSB = MSB2[cp], b_MSB2[cp]
                RSTD, b_RSTD = RSTDa2[cp], b_RSTDa2[cp]
                R2T, b_R2T = RSTDb2[cp], b_RSTDb2[cp]

                T0 = c * TC
                src = d_x[b].rearrange("(g p) t -> p g t", p=128)[:, :, T0:T0 + TC]
                P.emit("sync",
                       (lambda src=src, X=X: lambda s: s.dma_start(
                           out=X[:].rearrange("p (g t) -> p g t", t=TC), in_=src))(),
                       writes=[b_X], dma=True)

                P.level = 2
                P.cur_tag = "ln1"
                ln_stats(X, b_X, MSB, b_MSB, RSTD, b_RSTD, R2T, b_R2T, XSQ, b_XSQ)

                def f_ln12(v, X=X, XSQ=XSQ, Y=Y, XN=XN, MSB=MSB, RSTD=RSTD, R2T=R2T):
                    v.tensor_tensor(out=g3(XSQ), in0=g3(X), in1=bc3(MSB), op=AL.subtract)
                    v.tensor_tensor(out=g3(Y), in0=g3(XSQ), in1=bc3(RSTD), op=AL.mult)
                    v.tensor_tensor(out=g3(XN), in0=g3(XSQ), in1=bc3(R2T), op=AL.mult)
                    v.tensor_tensor(out=gt(XN), in0=gt(XN), in1=pv(0), op=AL.mult)
                    return v.tensor_tensor(out=gt(XN), in0=gt(XN), in1=pv(NG), op=AL.add)
                P.emit("dve", f_ln12, reads=[b_X, b_MSB, b_RSTD, b_R2T, b_vec],
                       writes=[b_Y, b_XSQ, b_XN])

                P.level = 4
                P.cur_tag = "mix"

                csh = t_zero if c == 0 else car_sh
                b_csh_r = b_zero if c == 0 else b_csh

                def f_mix(v, XSQ=XSQ, XN=XN, MIX=MIX, csh=csh):
                    for g in range(NG):
                        v.tensor_tensor(out=xg(XSQ, g)[:, 0:1], in0=xg(XN, g)[:, 0:1], in1=csh[:, g:g + 1], op=AL.subtract)
                    for g in range(NG):
                        v.tensor_tensor(out=xg(XSQ, g)[:, 1:], in0=xg(XN, g)[:, 1:], in1=xg(XN, g)[:, :-1], op=AL.subtract)
                    last = None
                    for mi, col in ((0, 2 * NG), (1, 3 * NG), (2, 4 * NG)):
                        mx = MIX[:, mi * NG * TC:(mi + 1) * NG * TC]
                        v.tensor_tensor(out=gt(mx), in0=gt(XSQ), in1=pv(col), op=AL.mult)
                        for g in range(NG):
                            v.tensor_tensor(out=xg(mx, g)[:, 1:], in0=xg(mx, g)[:, 1:], in1=xg(XN, g)[:, :-1], op=AL.add)
                            last = v.tensor_tensor(out=xg(mx, g)[:, 0:1], in0=xg(mx, g)[:, 0:1], in1=csh[:, g:g + 1], op=AL.add)
                    for g in range(NG):
                        last = v.tensor_copy(out=car_sh[:, g:g + 1], in_=xg(XN, g)[:, TC - 1:TC])
                    return last
                P.emit("dve", f_mix, reads=[b_XN, b_csh_r, b_vec], writes=[b_MIX, b_XSQ, b_csh])

        def emit_back(b, c):
                cp = (b * NCHUNK + c) % 2
                X, b_X = X2p[cp], b_X2p[cp]
                XSQ, b_XSQ = XSQ2[cp], b_XSQ2[cp]
                Y, b_Y = Y2[cp], b_Y2[cp]
                XN, b_XN = XN2[cp], b_XN2[cp]
                MIX, b_MIX = MIX2[cp], b_MIX2[cp]
                RSB, b_RSB = RSB2[cp], b_RSB2[cp]
                MSB, b_MSB = MSB2[cp], b_MSB2[cp]
                RSTD, b_RSTD = RSTDa2[cp], b_RSTDa2[cp]
                R2T, b_R2T = RSTDb2[cp], b_RSTDb2[cp]
                T0 = c * TC
                P.level = 5
                P.cur_tag = "kvr"
                for ec in range(NG):
                    par = ec % 2
                    kb, vb, rb = pb[2 * par][:, 0:TC], pb[2 * par][:, TC:2 * TC], pb[2 * par + 1][:, 0:TC]
                    b_kv, b_r = bpb[2 * par], bpb[2 * par + 1]
                    VSB, b_VSB = VSB2[ec % 2], b_VSB2[ec % 2]
                    KVSB, b_KVSB = KVSB2[ec % 2], b_KVSB2[ec % 2]

                    def f_kvr(t, ec=ec, kb=kb, vb=vb, rb=rb, MIX=MIX):
                        for g in range(NG):
                            t.matmul(kb, wslot(0, g)[:, ec * 128:(ec + 1) * 128],
                                     MIX[:, (0 * NG + g) * TC:(0 * NG + g + 1) * TC],
                                     start=(g == 0), stop=(g == NG - 1))
                        for g in range(NG):
                            t.matmul(vb, wslot(1, g)[:, ec * 128:(ec + 1) * 128],
                                     MIX[:, (1 * NG + g) * TC:(1 * NG + g + 1) * TC],
                                     start=(g == 0), stop=(g == NG - 1))
                        last = None
                        for g in range(NG):
                            last = t.matmul(rb, wslot(2, g)[:, ec * 128:(ec + 1) * 128],
                                            MIX[:, (2 * NG + g) * TC:(2 * NG + g + 1) * TC],
                                            start=(g == 0), stop=(g == NG - 1))
                        return last
                    P.emit("pe", f_kvr, reads=[b_MIX, b_wattn], writes=[b_kv, b_r])

                    P.emit("act", (lambda vb=vb, VSB=VSB: lambda a: a.copy(out=VSB[:], in_=vb))(),
                           reads=[b_kv], writes=[b_VSB])
                    P.emit("act", (lambda rb=rb, ec=ec, RSB=RSB: lambda a: a.activation(
                        out=xg(RSB, ec), in_=rb, func=AF.Sigmoid))(),
                        reads=[b_r], writes=[b_RSB])

                    def f_scan(v, ec=ec, kb=kb, VSB=VSB, KVSB=KVSB, c=c):
                        v.tensor_tensor(out=KVSB[:], in0=kb, in1=VSB[:], op=AL.mult)
                        v.tensor_tensor_scan(out=xg(SF, ec),
                                             data0=t_decay[:, ec:ec + 1].broadcast_to([128, TC]),
                                             data1=KVSB[:],
                                             initial=(0.0 if c == 0 else car_sc[:, ec:ec + 1]),
                                             op0=AL.mult, op1=AL.add)
                        v.tensor_copy(out=t_sp[:, 0:256], in_=t_sp[:, 256:512])
                        return v.tensor_copy(out=car_sc[:, ec:ec + 1], in_=xg(SF, ec)[:, TC - 1:TC])
                    P.emit("dve", f_scan, reads=[b_kv, b_VSB, b_decay, b_csc],
                           writes=[b_KVSB, b_SF, b_csc])
                    P.emit("act", (lambda ec=ec: lambda a: a.copy(out=xg(SBF, ec), in_=xg(SF, ec)))(),
                           reads=[b_SF], writes=[b_SBF])

                P.level = 6
                P.cur_tag = "oga"
                for ec in range(NG):
                    ob, b_ob = pb[4 + ec % 2][:, 0:TC], bpb[4 + ec % 2]
                    KVSB, b_KVSB = KVSB2[ec % 2], b_KVSB2[ec % 2]

                    def f_omm(t, ec=ec, ob=ob):
                        last = None
                        for g in range(NG):
                            last = t.matmul(ob, wslot(3, g)[:, ec * 128:(ec + 1) * 128],
                                            xg(SBF, g), start=(g == 0), stop=(g == NG - 1))
                        return last
                    P.emit("pe", f_omm, reads=[b_SBF, b_wattn], writes=[b_ob])

                    def f_gate(v, ec=ec, ob=ob, KVSB=KVSB, RSB=RSB, Y=Y, XN=XN):
                        v.tensor_tensor(out=KVSB[:], in0=ob, in1=xg(RSB, ec), op=AL.mult)
                        return v.tensor_tensor(out=xg(XN, ec), in0=KVSB[:], in1=xg(Y, ec), op=AL.add)
                    P.emit("dve", f_gate, reads=[b_ob, b_RSB, b_Y], writes=[b_KVSB, b_XN])

                P.level = 7
                P.cur_tag = "ln3"
                ln_stats(XN, b_XN, MSB, b_MSB, RSTD, b_RSTD, None, None, XSQ, b_XSQ)

                def f_ln3(v, XSQ=XSQ, XN=XN, X=X, MSB=MSB, RSTD=RSTD):
                    v.tensor_tensor(out=g3(XSQ), in0=g3(XN), in1=bc3(MSB), op=AL.subtract)
                    v.tensor_tensor(out=g3(XSQ), in0=g3(XSQ), in1=bc3(RSTD), op=AL.mult)
                    return v.tensor_tensor(out=X[:], in0=X[:], in1=XSQ[:], op=AL.add)
                P.emit("dve", f_ln3, reads=[b_XN, b_MSB, b_RSTD], writes=[b_X, b_XSQ])

                P.level = 8
                P.cur_tag = "tr"
                for sub in range(TC // 128):
                    for rnd in range(2):
                        def f_tr(t, sub=sub, rnd=rnd, X=X):
                            last = None
                            for j in range(4):
                                g = rnd * 4 + j
                                last = t.transpose(pb[7][:, j * 128:(j + 1) * 128],
                                                   xg(X, g)[:, sub * 128:(sub + 1) * 128], ident)
                            return last
                        P.emit("pe", f_tr, reads=[b_X, b_id], writes=[bpb[7]])

                        def f_ev(a, rnd=rnd):
                            return a.copy(out=STG[:, rnd * 512:(rnd + 1) * 512], in_=pb[7][:, 0:512])
                        P.emit("act", f_ev, reads=[bpb[7]], writes=[b_STG])

                    row0 = b * T + T0 + sub * 128
                    P.emit("sync",
                           (lambda row0=row0: lambda s: s.dma_start(
                               out=d_x2tm[row0:row0 + 128, :], in_=STG[:]))(),
                           reads=[b_STG], dma=True)

        bc_list = [(b, c) for b in range(B) for c in range(NCHUNK)]
        for i in range(len(bc_list) + 1):
            if i < len(bc_list):
                emit_front(*bc_list[i])
            if i >= 1:
                emit_back(*bc_list[i - 1])

        P.level = 1
        P.cur_tag = ""
        A.pop()
        P.barrier()

    if "b" in phases:
        P.cur_tag = "bc"
        A.push()
        w_gup, b_wgup = A.alloc([128, 2 * NG * I], BF16)
        qs = make_qscratch(1024)
        p_qv = pb[6].ap().bitcast(BF16) if hasattr(pb[6], "ap") else pb[6][:, 0:512].bitcast(BF16)
        p_qv = pb[6][:, 0:512].bitcast(BF16)
        b_pq = bpb[6]

        A.push()
        wstage, _ = A.alloc([128, 2 * D])
        b_ws = [Buf(), Buf()]
        wqt, b_wqt = A.alloc([128, D], BF16)
        b_dest = Buf()

        def mk_dest(mi):
            def dest(rt, j):
                return w_gup[:, (mi * NG + j) * I + rt * 128:(mi * NG + j) * I + rt * 128 + 128]
            return dest

        quant_weight(d_gw, I // 128, D, t_gs[:, 4:5], t_gs[:, 5:6], mk_dest(0), qs,
                     wstage, b_ws, wqt, b_wqt, p_qv, b_pq, b_dest)
        quant_weight(d_uw, I // 128, D, t_gs[:, 6:7], t_gs[:, 7:8], mk_dest(1), qs,
                     wstage, b_ws, wqt, b_wqt, p_qv, b_pq, b_dest)
        if "wg" in _dbg:
            d_wg = nc.dram_tensor("wgdump", [128, 2 * NG * I], BF16, kind="ExternalOutput").ap()
            d_gsd = nc.dram_tensor("gsdump", [128, 8], F32, kind="ExternalOutput").ap()
            P.barrier()
            P.emit("sync", lambda s: s.dma_start(out=d_wg, in_=w_gup[:]), reads=[b_wgup], dma=True)
            P.emit("sync", lambda s: s.dma_start(out=d_gsd, in_=t_gs[:]), reads=[b_gs], dma=True)
        A.pop()
        P.barrier()

        _skipB = "wo" in _dbg
        X2, _ = A.alloc([128, 2 * D])
        b_X2 = [Buf(), Buf()]

        def bpair(shape, dt=F32):
            t0, bb0 = A.alloc(shape, dt)
            t1, bb1 = A.alloc(shape, dt)
            return [t0, t1], [bb0, bb1]

        XN42, b_XN42 = bpair([128, D])
        XQ2, b_XQ2 = bpair([128, D], BF16)
        XQF2, b_XQF2 = bpair([128, D], BF16)
        HH2, b_HH2 = bpair([128, 1024])
        SGT2, b_SGT2 = bpair([128, 512])
        SGU2, b_SGU2 = bpair([128, 512])
        HQ2, b_HQ2 = bpair([128, 1024], BF16)
        HQT2, b_HQT2 = bpair([128, 1024], BF16)
        cst2p, b_cst2p = bpair([128, 8])
        qsx = make_qscratch(1024)
        qsh = make_qscratch(1024)

        p_g = [pb[0], pb[1]]
        p_u = [pb[2], pb[3]]
        bg = [bpb[0], bpb[1]]
        bu = [bpb[2], bpb[3]]
        p_tr2 = [pb[6][:, 0:512].bitcast(BF16), pb[7][:, 0:512].bitcast(BF16)]
        b_tr2 = [bpb[6], bpb[7]]

        def emit_b_front(tt):
            p2 = tt % 2
            x2sl = X2[:, p2 * D:(p2 + 1) * D]
            XN4, b_XN4 = XN42[p2], b_XN42[p2]
            XQ, b_XQ = XQ2[p2], b_XQ2[p2]
            XQF, b_XQF = XQF2[p2], b_XQF2[p2]
            HH, b_HH = HH2[p2], b_HH2[p2]
            SGT, b_SGT = SGT2[p2], b_SGT2[p2]
            SGU, b_SGU = SGU2[p2], b_SGU2[p2]
            HQ, b_HQ = HQ2[p2], b_HQ2[p2]
            HQT, b_HQT = HQT2[p2], b_HQT2[p2]
            cst, b_cst = cst2p[p2], b_cst2p[p2]
            p_qv2, b_pq2v = p_tr2[p2], b_tr2[p2]

            P.emit("sync", (lambda x2sl=x2sl, src=d_x2tm[tt * 128:(tt + 1) * 128, :]:
                            lambda s: s.dma_start(out=x2sl, in_=src))(),
                   writes=[b_X2[p2]], dma=True)

            P.emit("dve", (lambda x2sl=x2sl, cst=cst: lambda v: v.reduce_sum(out=cst[:, 0:1], in_=x2sl, axis=AX.X))(),
                   reads=[b_X2[p2]], writes=[b_cst])
            P.emit("act", (lambda x2sl=x2sl, cst=cst: lambda a: a.activation(
                out=qsx["asg"][0][:, 0:D], in_=x2sl, func=AF.Square, accum_out=cst[:, 1:2]))(),
                reads=[b_X2[p2]], writes=[qsx["asg"][1], b_cst])

            def f_small(v, cst=cst):
                v.tensor_scalar(out=cst[:, 2:3], in0=cst[:, 0:1], scalar1=1.0 / D, scalar2=None, op0=AL.mult)
                v.tensor_scalar(out=cst[:, 3:4], in0=cst[:, 1:2], scalar1=1.0 / D, scalar2=None, op0=AL.mult)
                v.tensor_copy(out=t_sp[:, 0:256], in_=t_sp[:, 256:512])
                v.tensor_tensor(out=cst[:, 4:5], in0=cst[:, 2:3], in1=cst[:, 2:3], op=AL.mult)
                v.tensor_copy(out=t_sp[:, 0:256], in_=t_sp[:, 256:512])
                v.tensor_tensor(out=cst[:, 3:4], in0=cst[:, 3:4], in1=cst[:, 4:5], op=AL.subtract)
                v.tensor_copy(out=t_sp[:, 0:256], in_=t_sp[:, 256:512])
                return v.tensor_scalar(out=cst[:, 3:4], in0=cst[:, 3:4], scalar1=EPS, scalar2=None, op0=AL.add)
            P.emit("dve", f_small, reads=[b_cst], writes=[b_cst])
            P.emit("act", (lambda cst=cst: lambda a: a.activation(out=cst[:, 4:5], in_=cst[:, 3:4], func=AF.Sqrt))(),
                   reads=[b_cst], writes=[b_cst])
            P.emit("dve", (lambda cst=cst: lambda v: v.reciprocal(out=cst[:, 3:4], in_=cst[:, 4:5]))(),
                   reads=[b_cst], writes=[b_cst])
            P.emit("dve", (lambda x2sl=x2sl, XN4=XN4, cst=cst: lambda v: v.tensor_scalar(
                out=XN4[:], in0=x2sl, scalar1=cst[:, 2:3], scalar2=cst[:, 3:4],
                op0=AL.subtract, op1=AL.mult))(),
                reads=[b_X2[p2], b_cst], writes=[b_XN4])

            emit_quant(XN4[:], b_XN4, XQ[:], b_XQ, t_gs[:, 0:1], t_gs[:, 1:2], D, qsx)

            def f_trx(t, XQ=XQ, p_qv2=p_qv2):
                last = None
                for j in range(8):
                    last = t.transpose(p_qv2[:, j * 128:(j + 1) * 128], XQ[:, j * 128:(j + 1) * 128], t_idbf[:])
                return last
            P.emit("pe", f_trx, reads=[b_XQ, b_idbf], writes=[b_pq2v])
            P.emit("act", (lambda XQF=XQF, p_qv2=p_qv2: lambda a: a.copy(out=XQF[:], in_=p_qv2[:]))(),
                   reads=[b_pq2v], writes=[b_XQF])

        def emit_b_back(tt):
            p2 = tt % 2
            XQF, b_XQF = XQF2[p2], b_XQF2[p2]
            HH, b_HH = HH2[p2], b_HH2[p2]
            SGT, b_SGT = SGT2[p2], b_SGT2[p2]
            SGU, b_SGU = SGU2[p2], b_SGU2[p2]
            HQ, b_HQ = HQ2[p2], b_HQ2[p2]
            HQT, b_HQT = HQT2[p2], b_HQT2[p2]
            p_qv2, b_pq2v = p_tr2[p2], b_tr2[p2]
            for kk in range(4):
                for half in range(2):
                    hs = kk * 2 + half
                    par = hs % 2

                    def f_gu(t, hs=hs, par=par, XQF=XQF):
                        for hc in range(NG):
                            t.matmul(p_g[par][:, 0:512], XQF[:, hc * 128:(hc + 1) * 128],
                                     w_gup[:, (0 * NG + hc) * I + hs * 512:(0 * NG + hc) * I + hs * 512 + 512],
                                     start=(hc == 0), stop=(hc == NG - 1))
                        last = None
                        for hc in range(NG):
                            last = t.matmul(p_u[par][:, 0:512], XQF[:, hc * 128:(hc + 1) * 128],
                                            w_gup[:, (1 * NG + hc) * I + hs * 512:(1 * NG + hc) * I + hs * 512 + 512],
                                            start=(hc == 0), stop=(hc == NG - 1))
                        return last
                    P.emit("pe", f_gu, reads=[b_XQF, b_wgup], writes=[bg[par], bu[par]])
                    P.emit("act", (lambda par=par, SGT=SGT: lambda a: a.activation(
                        out=SGT[:], in_=p_g[par][:, 0:512], func=AF.Silu))(),
                        reads=[bg[par]], writes=[b_SGT])
                    P.emit("act", (lambda par=par, SGU=SGU: lambda a: a.copy(
                        out=SGU[:], in_=p_u[par][:, 0:512]))(),
                        reads=[bu[par]], writes=[b_SGU])
                    P.emit("gp", (lambda half=half, SGT=SGT, SGU=SGU, HH=HH: lambda g: g.tensor_tensor(
                        out=HH[:, half * 512:(half + 1) * 512], in0=SGT[:], in1=SGU[:], op=AL.mult))(),
                        reads=[b_SGT, b_SGU], writes=[b_HH])

                emit_quant(HH[:], b_HH, HQ[:], b_HQ, t_gs[:, 2:3], t_gs[:, 3:4], 1024, qsh)

                def f_trh(t, HQ=HQ, p_qv2=p_qv2):
                    last = None
                    for j in range(8):
                        last = t.transpose(p_qv2[:, j * 128:(j + 1) * 128], HQ[:, j * 128:(j + 1) * 128], t_idbf[:])
                    return last
                P.emit("pe", f_trh, reads=[b_HQ, b_idbf], writes=[b_pq2v])
                P.emit("act", (lambda HQT=HQT, p_qv2=p_qv2: lambda a: a.copy(out=HQT[:], in_=p_qv2[:]))(),
                       reads=[b_pq2v], writes=[b_HQT])
                dst = d_hq[kk * 8:(kk + 1) * 8, :, tt * 128:(tt + 1) * 128].rearrange("i p t -> p i t")
                P.emit("sync", (lambda dst=dst, HQT=HQT: lambda s: s.dma_start(
                    out=dst, in_=HQT[:].rearrange("p (i t) -> p i t", t=128)))(),
                    reads=[b_HQT], dma=True)

        nbt = NTT if not _skipB else 0
        for i in range(nbt + 1):
            if i < nbt:
                emit_b_front(i)
            if i >= 1:
                emit_b_back(i - 1)

        A.pop()
        P.barrier()


    if "c" in phases:
        A.push()
        w_dn, b_wdn = A.alloc([128, NIC * D], BF16)

        A.push()
        qs2 = make_qscratch(4096)
        wstage2, _ = A.alloc([128, 2 * I])
        b_ws2 = [Buf(), Buf()]
        wqt2, b_wqt2 = A.alloc([128, I], BF16)
        b_dest2 = Buf()
        p_qv2 = pb[6][:, 0:512].bitcast(BF16)
        b_pq2 = bpb[6]

        def dest_dn(rt, j):
            return w_dn[:, j * D + rt * 128:j * D + rt * 128 + 128]

        quant_weight(d_dw, D // 128, I, t_gs[:, 4:5], t_gs[:, 5:6], dest_dn, qs2,
                     wstage2, b_ws2, wqt2, b_wqt2, p_qv2, b_pq2, b_dest2)
        A.pop()
        P.barrier()

        HQF, _ = A.alloc([128, 2 * NIC * 128], BF16)
        b_HQF = [Buf(), Buf()]
        X2c, _ = A.alloc([128, 2 * D])
        b_X2c = [Buf(), Buf()]
        OC, _ = A.alloc([128, 2 * D])
        b_OC = [Buf(), Buf()]
        _s0, _bs0 = A.alloc([128, D])
        _s1, _bs1 = A.alloc([128, D])
        SQSp, b_SQSp = [_s0, _s1], [_bs0, _bs1]
        _c0, _bc0 = A.alloc([128, 8])
        _c1, _bc1 = A.alloc([128, 8])
        cst2p_, b_cst2p_ = [_c0, _c1], [_bc0, _bc1]

        p_o = [pb[0], pb[1], pb[2], pb[3]]
        bo = [bpb[0], bpb[1], bpb[2], bpb[3]]

        for tt in range(NTT):
            p2 = tt % 2
            SQS, b_SQS = SQSp[p2], b_SQSp[p2]
            cst2, b_cst2 = cst2p_[p2], b_cst2p_[p2]
            hsl = HQF[:, p2 * NIC * 128:(p2 + 1) * NIC * 128]
            x2sl = X2c[:, p2 * D:(p2 + 1) * D]
            osl = OC[:, p2 * D:(p2 + 1) * D]
            srch = d_hq[:, :, tt * 128:(tt + 1) * 128].rearrange("i p t -> p i t")
            P.emit("sync", (lambda hsl=hsl, srch=srch: lambda s: s.dma_start(
                out=hsl.rearrange("p (i t) -> p i t", t=128), in_=srch))(),
                writes=[b_HQF[p2]], dma=True)
            P.emit("sync", (lambda x2sl=x2sl, src=d_x2tm[tt * 128:(tt + 1) * 128, :]:
                            lambda s: s.dma_start(out=x2sl, in_=src))(),
                   writes=[b_X2c[p2]], dma=True)

            for half in range(2):
                pj = p2 * 2 + half

                def f_dn(t, hsl=hsl, half=half, pj=pj):
                    last = None
                    for ic in range(NIC):
                        last = t.matmul(p_o[pj][:, 0:512], hsl[:, ic * 128:(ic + 1) * 128],
                                        w_dn[:, ic * D + half * 512:ic * D + half * 512 + 512],
                                        start=(ic == 0), stop=(ic == NIC - 1))
                    return last
                P.emit("pe", f_dn, reads=[b_HQF[p2], b_wdn], writes=[bo[pj]])
                P.emit("dve", (lambda osl=osl, half=half, pj=pj: lambda v: v.tensor_scalar(
                    out=osl[:, half * 512:(half + 1) * 512], in0=p_o[pj][:, 0:512], scalar1=1.0, scalar2=0.0,
                    op0=AL.mult, op1=AL.add, accum_out=cst2[:, half:half + 1]))(),
                    reads=[bo[pj]], writes=[b_OC[p2], b_cst2])

            P.emit("act", (lambda osl=osl: lambda a: a.activation(
                out=SQS[:], in_=osl, func=AF.Square, accum_out=cst2[:, 2:3]))(),
                reads=[b_OC[p2]], writes=[b_SQS, b_cst2])

            def f_small2(v):
                v.tensor_tensor(out=cst2[:, 0:1], in0=cst2[:, 0:1], in1=cst2[:, 1:2], op=AL.add)
                v.tensor_scalar(out=cst2[:, 3:4], in0=cst2[:, 0:1], scalar1=1.0 / D, scalar2=None, op0=AL.mult)
                v.tensor_scalar(out=cst2[:, 4:5], in0=cst2[:, 2:3], scalar1=1.0 / D, scalar2=None, op0=AL.mult)
                v.tensor_tensor(out=cst2[:, 5:6], in0=cst2[:, 3:4], in1=cst2[:, 3:4], op=AL.mult)
                v.tensor_tensor(out=cst2[:, 4:5], in0=cst2[:, 4:5], in1=cst2[:, 5:6], op=AL.subtract)
                return v.tensor_scalar(out=cst2[:, 4:5], in0=cst2[:, 4:5], scalar1=EPS, scalar2=None, op0=AL.add)
            P.emit("dve", f_small2, reads=[b_cst2], writes=[b_cst2])
            P.emit("act", lambda a: a.activation(out=cst2[:, 5:6], in_=cst2[:, 4:5], func=AF.Sqrt),
                   reads=[b_cst2], writes=[b_cst2])
            P.emit("dve", lambda v: v.reciprocal(out=cst2[:, 4:5], in_=cst2[:, 5:6]),
                   reads=[b_cst2], writes=[b_cst2])

            def f_out(v, osl=osl, x2sl=x2sl):
                v.tensor_scalar(out=osl, in0=osl, scalar1=cst2[:, 3:4], scalar2=cst2[:, 4:5],
                                op0=AL.subtract, op1=AL.mult)
                return v.tensor_tensor(out=osl, in0=osl, in1=x2sl, op=AL.add)
            P.emit("dve", f_out, reads=[b_cst2, b_X2c[p2], b_OC[p2]], writes=[b_OC[p2]])

            P.emit("sync", (lambda osl=osl, dst=d_out[tt * 128:(tt + 1) * 128, :]:
                            lambda s: s.dma_start(out=dst, in_=osl))(),
                   reads=[b_OC[p2]], dma=True)

        A.pop()

    P.barrier()
    P.emit("sync", lambda s: s.nop())

    sems = {e: es.enter_context(nc.semaphore(f"sem_{e}")) for e in ENGS}
    block = es.enter_context(nc.Block())

    @block.tensor
    def _(t):
        P.play("pe", t, sems)

    @block.vector
    def _(v):
        P.play("dve", v, sems)

    @block.scalar
    def _(a):
        P.play("act", a, sems)

    @block.gpsimd
    def _(g):
        P.play("gp", g, sems)

    @block.sync
    def _(s):
        P.play("sync", s, sems)

    es.close()
    mybir.codegen_inst_isa_subclasses(nc)
    return nc


def _prep_inputs(inputs, B_core, T, core, n_cores):
    f32 = np.float32
    x = np.asarray(inputs["L0_attn_x"], f32)
    b0 = core * B_core
    x_fm = np.ascontiguousarray(x[b0:b0 + B_core, :T].transpose(0, 2, 1))
    vecs = np.stack([
        np.ascontiguousarray(np.asarray(inputs[k], f32).reshape(8, 128).T)
        for k in ("L0_attn_ln_w", "L0_attn_ln_b", "L0_attn_tmk", "L0_attn_tmv",
                  "L0_attn_tmr", "L0_attn_decay_w")])
    ident = np.zeros((128, 129), f32)
    ident[:, 0] = 1.0
    ident[:, 1:129] = np.eye(128, dtype=f32)
    return {
        "x_fm": x_fm,
        "wkT": np.ascontiguousarray(np.asarray(inputs["L0_attn_Wk"], f32).T),
        "wvT": np.ascontiguousarray(np.asarray(inputs["L0_attn_Wv"], f32).T),
        "wrT": np.ascontiguousarray(np.asarray(inputs["L0_attn_Wr"], f32).T),
        "woT": np.ascontiguousarray(np.asarray(inputs["L0_attn_Wo"], f32).T),
        "gate_w": np.ascontiguousarray(np.asarray(inputs["L0_ffn_gate_w"], f32)),
        "up_w": np.ascontiguousarray(np.asarray(inputs["L0_ffn_up_w"], f32)),
        "down_w": np.ascontiguousarray(np.asarray(inputs["L0_ffn_down_w"], f32)),
        "vecs": np.ascontiguousarray(vecs),
        "s_in2": np.array([[float(np.asarray(inputs["L0_ffn_s_in"]).reshape(-1)[0]),
                            float(np.asarray(inputs["L0_ffn_s_in_down"]).reshape(-1)[0])]], f32),
        "ident": ident,
        "onesrow": np.ones((1, 128), f32),
        "onesmat": np.ones((128, 128), f32),
    }


_NC_CACHE = {}


def kernel(**inputs):
    B_total = int(np.asarray(inputs["L0_attn_x"]).shape[0])
    T = int(np.asarray(inputs["L0_attn_x"]).shape[1])
    n_cores = 8
    B_core = B_total // n_cores
    key = (B_core, T)
    if key not in _NC_CACHE:
        _NC_CACHE[key] = build_program(B=B_core, T=T)
    nc = _NC_CACHE[key]
    in_maps = [_prep_inputs(inputs, B_core, T, c, n_cores) for c in range(n_cores)]
    res = run_bass_kernel_spmd(nc, in_maps, core_ids=list(range(n_cores)))
    outs = [np.asarray(r["out"]).reshape(B_core, T, 1024) for r in res.results]
    return np.concatenate(outs, axis=0)


# version 05
# speedup vs baseline: 1.1252x; 1.1173x over previous
import contextlib
import numpy as np
import ml_dtypes

import concourse.bass as bass
import concourse.mybir as mybir
from concourse.bass_utils import run_bass_kernel_spmd

F32 = mybir.dt.float32
F32R = mybir.dt.float32r
BF16 = mybir.dt.bfloat16
FP8E4 = mybir.dt.float8e4
U8 = mybir.dt.uint8
U32 = mybir.dt.uint32
AL = mybir.AluOpType
AF = mybir.ActivationFunctionType
AX = mybir.AxisListType

from concourse.dve_spec import Spec, Src0, Src1, C0, C1, C2, maxx, minn, lower
from concourse.dve_op import DveOp
import concourse.dve_ops as dve_ops

MAGIC = float(1.5 * 2 ** 23)


def _register_op(name, spec, reference):
    for op in dve_ops.OPS:
        if op.name == name:
            return op
    shas = {}
    for ver in ("v3", "v4"):
        try:
            shas[ver] = lower(spec, ver=ver, name=name).sha(ver)
        except Exception:
            pass
    op = DveOp(name, spec, subdim=False, uops_sha=shas, reference=reference)
    dve_ops.OPS.append(op)
    idx = len(dve_ops.OPS) - 1
    dve_ops._SUB_OPCODE_FOR_NAME[name] = dve_ops._CUSTOM_DVE_ROW_BASE + idx
    assert dve_ops._SUB_OPCODE_FOR_NAME[name] < 0x20
    dve_ops.CUSTOM_DVE_SPECS[name] = spec
    dve_ops.CUSTOM_DVE_REFS[name] = reference
    return op


def _compand_spec():
    d1 = Src0 + Src0
    d2 = Src0 + C1
    m1 = minn(d1, d2)
    e1 = Src0 * C0
    d3 = e1 + (C1 + C1)
    m2 = minn(m1, d3)
    z = m2 + C2
    return Spec(body=z - C2)


def _compand_ref(in0, in1, s0, s1, imm2):
    G = np.minimum(np.minimum(2.0 * in0, in0 + 2.0), 0.5 * in0 + 4.0)
    return np.round(G)


def _uncompand_spec():
    w1 = Src0 + Src0
    w1b = w1 - C0
    v1 = maxx(Src0, w1b)
    w2 = w1 + w1
    w2b = w2 - C1
    v2 = maxx(v1, w2b)
    capped = minn(v2, C2)
    return Spec(body=capped * Src1)


def _uncompand_ref(in0, in1, s0, s1, imm2):
    v = np.maximum(np.maximum(in0, 2.0 * in0 - 4.0), 4.0 * in0 - 16.0)
    return np.minimum(v, 12.0) * in1


COMPAND = _register_op("NVFP4_COMPAND", _compand_spec(), _compand_ref)
UNCOMPAND = _register_op("NVFP4_UNCOMPAND", _uncompand_spec(), _uncompand_ref)

ENGS = ("pe", "dve", "act", "gp", "sync")


class Buf:
    __slots__ = ("w", "rs")

    def __init__(self):
        self.w = None
        self.rs = []


class Prog:
    def __init__(self):
        self.items = []
        self.count = {e: 0 for e in ENGS}
        self.pending = {e: [] for e in ENGS}
        self.level = 1
        self.maxlev = 99
        self.spacer_fns = {}
        self.serialize = False

    def barrier(self):
        snap = [(e, self.count[e]) for e in ENGS if self.count[e] > 0]
        for e in ENGS:
            self.pending[e] = list(snap)

    def emit(self, eng, fn, reads=(), writes=(), deps=(), dma=False):
        if self.level > self.maxlev:
            return None
        if self.serialize or (getattr(self, "cur_tag", "") in getattr(self, "ser_tags", set())):
            self.barrier()
        dd = {}

        def add(d, keep_same=False):
            if d is not None and (keep_same or d[0] != eng):
                dd[d[0]] = max(dd.get(d[0], 0), d[1])
        for d in deps:
            add(d)
        for d in self.pending[eng]:
            add(d, keep_same=True)
        for b in reads:
            add(b.w)
        for b in writes:
            for d in (b.rs + [b.w]):
                add(d)
        self.pending[eng] = []
        inc = 16 if dma else 1
        hazard = self.serialize or getattr(self, "allspace", False)
        if eng in ("dve", "act", "gp"):
            prev_tok = (eng, self.count[eng])
            for b in reads:
                if b.w == prev_tok:
                    hazard = True
            for b in writes:
                if b.w == prev_tok or (b.rs and b.rs[-1] == prev_tok):
                    hazard = True
        self.items.append((eng, fn, sorted(dd.items()), inc, hazard))
        self.count[eng] += inc
        tok = (eng, self.count[eng])
        for b in reads:
            b.rs.append(tok)
        for b in writes:
            b.w = tok
            b.rs = []
        return tok

    def play(self, engname, obj, sems):
        sp = self.spacer_fns.get(engname)
        for item in self.items:
            eng, fn, deps, inc, hazard = item
            if eng != engname:
                continue
            for de, dc in deps:
                obj.wait_ge(sems[de], dc)
            if hazard and sp is not None:
                sp(obj)
            last = fn(obj)
            if last is None:
                last = obj.nop()
            last.then_inc(sems[eng], inc)


class Arena:

    def __init__(self, tensor, nbytes):
        self.t = tensor
        self.nbytes = nbytes
        self.off = 0
        self.marks = []

    def push(self):
        self.marks.append(self.off)

    def pop(self):
        self.off = self.marks.pop()

    def alloc(self, shape, dt=F32):
        p, n = shape
        assert p <= 128
        esz = {F32: 4, BF16: 2, FP8E4: 1, U32: 4, U8: 1}[dt]
        nb = n * esz
        off = (self.off + 511) & ~511
        assert off + nb <= self.nbytes, f"arena overflow: {off + nb} > {self.nbytes}"
        self.off = off + nb
        view = self.t[:, off:off + nb]
        if dt != U8:
            view = view.bitcast(dt)
        return view[0:p, :], Buf()


def build_program(B=2, T=4096, TC=256, phases="abc", maxlev=99):
    D, I = 1024, 4096
    NG = D // 128
    NCHUNK = T // TC
    NTT = (B * T) // 128
    NIC = I // 128
    EPS = 1e-5

    nc = bass.Bass("TRN2", target_bir_lowering=False,
                   dynamic_dma_scratch_size=4096)

    d_x = nc.dram_tensor("x_fm", [B, D, T], F32, kind="ExternalInput").ap()
    d_w = [nc.dram_tensor(n, [D, D], F32, kind="ExternalInput").ap()
           for n in ("wkT", "wvT", "wrT", "woT")]
    d_gw = nc.dram_tensor("gate_w", [I, D], F32, kind="ExternalInput").ap()
    d_uw = nc.dram_tensor("up_w", [I, D], F32, kind="ExternalInput").ap()
    d_dw = nc.dram_tensor("down_w", [D, I], F32, kind="ExternalInput").ap()
    d_vec = nc.dram_tensor("vecs", [6, 128, NG], F32, kind="ExternalInput").ap()
    d_sin = nc.dram_tensor("s_in2", [1, 2], F32, kind="ExternalInput").ap()
    d_id = nc.dram_tensor("ident", [128, 129], F32, kind="ExternalInput").ap()
    d_orow = nc.dram_tensor("onesrow", [1, 128], F32, kind="ExternalInput").ap()
    d_om = nc.dram_tensor("onesmat", [128, 128], F32, kind="ExternalInput").ap()
    d_out = nc.dram_tensor("out", [B * T, D], F32, kind="ExternalOutput").ap()
    import os as _os
    _dbg = _os.environ.get("KDBG", "")
    d_x2tm = nc.dram_tensor("x2tm", [B * T, D], F32,
                            kind="ExternalOutput" if "x2" in _dbg else "Internal").ap()
    d_hq_kind = "ExternalOutput" if "hq" in _dbg else "Internal"
    d_hq = nc.dram_tensor("hq", [NIC, 128, B * T], BF16, kind=d_hq_kind).ap()

    P = Prog()
    import os as _os2
    _kser = _os2.environ.get("KSER", "")
    P.serialize = _kser == "1"
    P.allspace = _os2.environ.get("KSPACE", "0") == "1"
    P.ser_tags = set(_kser.split(",")) if _kser not in ("", "0", "1") else set()
    P.ser_tags.add("su")
    P.cur_tag = ""
    P.maxlev = maxlev
    es = contextlib.ExitStack()

    ARENA_BYTES = 216 * 1024 - 2048
    arena_t = es.enter_context(nc.sbuf_tensor("arena", [128, ARENA_BYTES], U8))
    A = Arena(arena_t, ARENA_BYTES)

    def SBP(name, shape, dt=F32):
        return es.enter_context(nc.sbuf_tensor(name, shape, dt)), Buf()

    t_id, b_id = SBP("t_id", [128, 129])
    t_idbf, b_idbf = SBP("t_idbf", [128, 128], BF16)
    t_onebf, b_onebf = SBP("t_onebf", [128, 1], BF16)
    t_or, b_or = SBP("t_or", [1, 128])
    t_om, b_om = SBP("t_om", [128, 128])
    t_ombf, b_ombf = SBP("t_ombf", [128, 128], BF16)
    t_vec, b_vec = SBP("t_vec", [128, 6 * NG])
    t_decay, b_decay = SBP("t_decay", [128, NG])
    t_gs, b_gs = SBP("t_gs", [128, 8])
    car_sh, b_csh = SBP("car_sh", [128, NG])
    car_sc, b_csc = SBP("car_sc", [128, NG])
    wmaxt, b_wmax = SBP("wmaxt", [128, 128])
    gsm, b_gsm = SBP("gsm", [1, 4])

    ones_col = t_id[:, 0:1]
    ident = t_id[:, 1:129]
    t_sp, _bsp = SBP("t_sp", [128, 256])
    P.spacer_fns["dve"] = lambda v: v.tensor_copy(out=t_sp[:, 0:128], in_=t_sp[:, 128:256])
    P.spacer_fns["act"] = lambda a: a.copy(out=t_sp[:, 0:128], in_=t_sp[:, 128:256])
    P.spacer_fns["gp"] = lambda g: g.tensor_copy(out=t_sp[:, 0:128], in_=t_sp[:, 128:256])

    pb = []
    bpb = []
    for i in range(8):
        t = es.enter_context(nc.psum_tensor(f"pb{i}", [128, 512], F32))
        pb.append(t)
        bpb.append(Buf())

    P.cur_tag = "su"
    P.emit("sync", lambda s: s.dma_start(out=t_vec[:].rearrange("p (k g) -> p k g", g=NG), in_=d_vec.transpose([1, 0, 2])),
           writes=[b_vec], dma=True)
    P.emit("sync", lambda s: s.dma_start(out=t_gs[0:1, 0:2], in_=d_sin[:]), writes=[b_gs], dma=True)
    P.emit("sync", lambda s: s.dma_start(out=t_id[:], in_=d_id[:]), writes=[b_id], dma=True)
    P.emit("sync", lambda s: s.dma_start(out=t_or[:], in_=d_orow[:]), writes=[b_or], dma=True)
    P.emit("sync", lambda s: s.dma_start(out=t_om[:], in_=d_om[:]), writes=[b_om], dma=True)

    P.emit("act", lambda a: a.activation(out=t_decay[:], in_=t_vec[:, 5 * NG:6 * NG], func=AF.Sigmoid),
           reads=[b_vec], writes=[b_decay])
    P.emit("act", lambda a: a.copy(out=t_idbf[:], in_=ident), reads=[b_id], writes=[b_idbf])
    P.emit("act", lambda a: a.copy(out=t_ombf[:], in_=t_om[:]), reads=[b_om], writes=[b_ombf])
    P.emit("act", lambda a: a.copy(out=t_onebf[:], in_=ones_col), reads=[b_id], writes=[b_onebf])

    P.emit("pe", lambda t: t.matmul(pb[0][:, 0:2], t_or[:], t_gs[0:1, 0:2],
                                    start=True, stop=True),
           reads=[b_or, b_gs], writes=[bpb[0]])

    def f_gs(v):
        v.tensor_scalar(out=t_gs[:, 0:1], in0=pb[0][:, 0:1], scalar1=1.0 / 6.0, scalar2=None, op0=AL.mult)
        v.tensor_scalar(out=t_gs[:, 2:3], in0=pb[0][:, 1:2], scalar1=1.0 / 6.0, scalar2=None, op0=AL.mult)
        v.reciprocal(out=t_gs[:, 1:2], in_=pb[0][:, 0:1])
        return v.reciprocal(out=t_gs[:, 3:4], in_=pb[0][:, 1:2])
    P.emit("dve", f_gs, reads=[bpb[0]], writes=[b_gs])

    def emit_quant(src_ap, b_src, out_ap, b_out, gs6_ap, rgs_ap, NN, q):
        nb = NN // 16
        amax, b_amax = q["amax"]
        scz, b_scz = q["scz"]
        sgn, b_sgn = q["sgn"]
        asg, b_asg = q["asg"]
        fp8t, b_fp8 = q["fp8"]
        msku, b_msk = q["msk"]

        P.emit("dve", lambda v: v.tensor_reduce(
            out=amax[:, 0:nb][:, :, None], in_=src_ap.rearrange("p (nb w) -> p nb w", w=16),
            axis=AX.X, op=AL.max, apply_absolute_value=True),
            reads=[b_src], writes=[b_amax])

        def f_sc(v):
            v.tensor_scalar(out=scz[:, 0:nb], in0=amax[:, 0:nb], scalar1=gs6_ap, scalar2=None, op0=AL.mult)
            v.tensor_scalar(out=scz[:, 0:nb], in0=scz[:, 0:nb], scalar1=0.0, scalar2=448.0,
                            op0=AL.max, op1=AL.min)
            return v.tensor_scalar(out=scz[:, nb:2 * nb], in0=scz[:, 0:nb],
                                   scalar1=float(2 ** 14), scalar2=float(2 ** 14),
                                   op0=AL.add, op1=AL.subtract)
        P.emit("dve", f_sc, reads=[b_amax, b_gs], writes=[b_scz])

        def f_fp8(a):
            a.mul(out=scz[:, 2 * nb:3 * nb], in_=scz[:, 0:nb], mul=0.5)
            a.copy(out=fp8t[:, 0:nb], in_=scz[:, 2 * nb:3 * nb])
            a.activation(out=scz[:, 2 * nb:3 * nb], in_=fp8t[:, 0:nb], func=AF.Copy, scale=2.0)
            a.activation(out=asg[:, 0:NN], in_=src_ap, func=AF.Abs)
            return a.sign(out=sgn[:, 0:NN], in_=src_ap)
        P.emit("act", f_fp8, reads=[b_scz, b_src], writes=[b_scz, b_fp8, b_sgn, b_asg])

        def f_sel(v):
            v.tensor_scalar(out=msku[:, 0:nb], in0=scz[:, 0:nb], scalar1=float(2.0 ** -5),
                            scalar2=None, op0=AL.is_lt)
            v.select(out=scz[:, 0:nb], mask=msku[:, 0:nb], on_true=scz[:, nb:2 * nb],
                     on_false=scz[:, 2 * nb:3 * nb])
            v.tensor_scalar(out=scz[:, 3 * nb:4 * nb], in0=scz[:, 0:nb], scalar1=rgs_ap, scalar2=None, op0=AL.mult)
            v.tensor_scalar(out=scz[:, 4 * nb:5 * nb], in0=scz[:, 3 * nb:4 * nb], scalar1=0.5, scalar2=None, op0=AL.mult)
            v.tensor_scalar(out=scz[:, 5 * nb:6 * nb], in0=scz[:, 3 * nb:4 * nb], scalar1=1e-38, scalar2=None, op0=AL.max)
            return v.reciprocal(out=scz[:, 5 * nb:6 * nb], in_=scz[:, 5 * nb:6 * nb])
        P.emit("dve", f_sel, reads=[b_scz, b_fp8, b_gs], writes=[b_scz, b_msk])

        def rv(ap_):
            return ap_.rearrange("p (nb w) -> p w nb", w=16)

        def bv(col0):
            return scz[:, col0:col0 + nb][:, None, :].broadcast_to([128, 16, nb])

        def f_ratio(v):
            v.tensor_tensor(out=rv(asg[:, 0:NN]), in0=rv(asg[:, 0:NN]), in1=bv(5 * nb), op=AL.mult)
            return v.tensor_tensor(out=rv(sgn[:, 0:NN]), in0=rv(sgn[:, 0:NN]), in1=bv(4 * nb), op=AL.mult)
        P.emit("dve", f_ratio, reads=[b_scz, b_sgn, b_asg], writes=[b_asg, b_sgn])

        def f_cq(v):
            v._custom_dve(COMPAND, out=asg[:, 0:NN], in0=asg[:, 0:NN], s0=0.5, s1=2.0, imm2=MAGIC)
            return v._custom_dve(UNCOMPAND, out=out_ap, in0=asg[:, 0:NN], in1=sgn[:, 0:NN],
                                 s0=4.0, s1=16.0, imm2=12.0)
        P.emit("dve", f_cq, reads=[b_asg, b_sgn], writes=[b_out, b_asg])

    def make_qscratch(NNmax):
        nbm = NNmax // 16
        return {
            "amax": A.alloc([128, nbm]),
            "scz": A.alloc([128, 6 * nbm]),
            "sgn": A.alloc([128, NNmax]),
            "asg": A.alloc([128, NNmax]),
            "fp8": A.alloc([128, nbm], FP8E4),
            "msk": A.alloc([128, nbm], U32),
        }

    def quant_weight(dram_w, nrt, ncols, gs6_ap, rgs_ap, dest_fn, qs, wstage, b_ws,
                     wqt, b_wqt, p_qv, b_pq, b_dest):
        P.emit("dve", lambda v: v.memset(wmaxt[:, 0:1], 0.0), writes=[b_wmax])
        for rt in range(nrt):
            par = rt % 2
            sl = wstage[:, par * ncols:(par + 1) * ncols]
            P.emit("sync", (lambda sl=sl, src=dram_w[rt * 128:(rt + 1) * 128, :]:
                            lambda s: s.dma_start(out=sl, in_=src))(),
                   writes=[b_ws[par]], dma=True)
            P.emit("dve", (lambda sl=sl: lambda v: v.tensor_reduce(
                out=wmaxt[:, 1:2], in_=sl, axis=AX.X, op=AL.max, apply_absolute_value=True))(),
                reads=[b_ws[par]], writes=[b_wmax])
            P.emit("dve", lambda v: v.tensor_tensor(out=wmaxt[:, 0:1], in0=wmaxt[:, 0:1],
                                                    in1=wmaxt[:, 1:2], op=AL.max),
                   reads=[b_wmax], writes=[b_wmax])
        P.emit("pe", lambda t: t.transpose(pb[7][:, 0:128], wmaxt[:], ident),
               reads=[b_wmax, b_id], writes=[bpb[7]])
        P.emit("act", lambda a: a.copy(out=wmaxt[:], in_=pb[7][:, 0:128]),
               reads=[bpb[7]], writes=[b_wmax])
        P.emit("dve", lambda v: v.reduce_max(out=gsm[:, 0:1], in_=wmaxt[0:1, 0:128], axis=AX.X),
               reads=[b_wmax], writes=[b_gsm])
        P.emit("pe", lambda t: t.matmul(pb[7][:, 0:1], t_or[:],
                                        gsm[:, 0:1], start=True, stop=True),
               reads=[b_gsm, b_or], writes=[bpb[7]])

        def f_gsw(v):
            v.reciprocal(out=wmaxt[:, 2:3], in_=pb[7][:, 0:1])
            v.tensor_copy(out=t_sp[:, 0:128], in_=t_sp[:, 128:256])
            v.tensor_scalar(out=gs6_ap, in0=wmaxt[:, 2:3], scalar1=448.0, scalar2=None, op0=AL.mult)
            return v.tensor_scalar(out=rgs_ap, in0=pb[7][:, 0:1], scalar1=1.0 / 2688.0, scalar2=None, op0=AL.mult)
        P.emit("dve", f_gsw, reads=[bpb[7]], writes=[b_gs, b_wmax])
        if "wd" in _dbg and not getattr(P, "_wd2_done", False):
            P._wd2_done = True
            d_gsm = nc.dram_tensor("gsmd", [1, 4], F32, kind="ExternalOutput").ap()
            d_p7 = nc.dram_tensor("p7d", [128, 4], F32, kind="ExternalOutput").ap()
            d_wmx = nc.dram_tensor("wmxd", [128, 128], F32, kind="ExternalOutput").ap()
            P.emit("sync", lambda s: s.dma_start(out=d_gsm, in_=gsm[:]), reads=[b_gsm], dma=True)
            P.emit("act", lambda a: a.copy(out=wmaxt[:, 4:8], in_=pb[7][:, 0:4]), reads=[bpb[7]], writes=[b_wmax])
            P.emit("sync", lambda s: s.dma_start(out=d_p7, in_=wmaxt[:, 4:8]), reads=[b_wmax], dma=True)
            P.emit("sync", lambda s: s.dma_start(out=d_wmx, in_=wmaxt[:]), reads=[b_wmax], dma=True)

        for rt in range(nrt):
            par = rt % 2
            sl = wstage[:, par * ncols:(par + 1) * ncols]
            P.emit("sync", (lambda sl=sl, src=dram_w[rt * 128:(rt + 1) * 128, :]:
                            lambda s: s.dma_start(out=sl, in_=src))(),
                   writes=[b_ws[par]], dma=True)
            emit_quant(sl, b_ws[par], wqt[:, 0:ncols], b_wqt, gs6_ap, rgs_ap, ncols, qs)
            if "wd" in _dbg and rt == 0 and not getattr(P, "_wd_done", False):
                P._wd_done = True
                d_wq0 = nc.dram_tensor("wq0", [128, ncols], BF16, kind="ExternalOutput").ap()
                d_sz0 = nc.dram_tensor("sz0", [128, 6 * (ncols // 16)], F32, kind="ExternalOutput").ap()
                d_am0 = nc.dram_tensor("am0", [128, ncols // 16], F32, kind="ExternalOutput").ap()
                d_src0 = nc.dram_tensor("src0", [128, ncols], F32, kind="ExternalOutput").ap()
                P.emit("sync", (lambda sl=sl: lambda s: s.dma_start(out=d_src0, in_=sl))(), reads=[b_ws[par]], dma=True)
                P.emit("sync", lambda s: s.dma_start(out=d_wq0, in_=wqt[:, 0:ncols]), reads=[b_wqt], dma=True)
                P.emit("sync", lambda s: s.dma_start(out=d_sz0, in_=qs["scz"][0][:, 0:6 * (ncols // 16)]), reads=[qs["scz"][1]], dma=True)
                P.emit("sync", lambda s: s.dma_start(out=d_am0, in_=qs["amax"][0][:, 0:ncols // 16]), reads=[qs["amax"][1]], dma=True)
            for j0 in range(0, ncols // 128, 8):
                jn = min(j0 + 8, ncols // 128)

                def f_trq(t, j0=j0, jn=jn):
                    last = None
                    for j in range(j0, jn):
                        last = t.transpose(p_qv[:, (j - j0) * 128:(j - j0 + 1) * 128],
                                           wqt[:, j * 128:(j + 1) * 128], t_idbf[:])
                    return last
                P.emit("pe", f_trq, reads=[b_wqt, b_idbf], writes=[b_pq])

                def f_pl(a, rt=rt, j0=j0, jn=jn):
                    last = None
                    for j in range(j0, jn):
                        last = a.copy(out=dest_fn(rt, j), in_=p_qv[:, (j - j0) * 128:(j - j0 + 1) * 128])
                    return last
                tok = P.emit("act", f_pl, reads=[b_pq], writes=[b_dest])
                b_dest.rs = []

    if "a" in phases:
        A.push()
        w_attn, b_wattn = A.alloc([128, 4 * NG * D], BF16)

        def wslot(mi, g):
            return w_attn[:, (mi * NG + g) * D:(mi * NG + g + 1) * D]

        A.push()
        stage_w, _ = A.alloc([128, 2 * D])
        b_stage = [Buf(), Buf()]
        for mi in range(4):
            for g in range(NG):
                par = (mi * NG + g) % 2
                sl = stage_w[:, par * D:(par + 1) * D]
                P.emit("sync", (lambda sl=sl, src=d_w[mi][g * 128:(g + 1) * 128, :]:
                                lambda s: s.dma_start(out=sl, in_=src))(),
                       writes=[b_stage[par]], dma=True)
                P.emit("act", (lambda sl=sl, dst=wslot(mi, g): lambda a: a.copy(out=dst, in_=sl))(),
                       reads=[b_stage[par]], writes=[b_wattn])
        A.pop()

        def pair(shape, dt=F32):
            t0, bb0 = A.alloc(shape, dt)
            t1, bb1 = A.alloc(shape, dt)
            return [t0, t1], [bb0, bb1]

        X2p, b_X2p = pair([128, NG * TC])
        XSQ2, b_XSQ2 = pair([128, NG * TC])
        Y2, b_Y2 = pair([128, NG * TC])
        XN2, b_XN2 = pair([128, NG * TC])
        MIX2, b_MIX2 = pair([128, 3 * NG * TC], BF16)
        _rsb, _brsb = A.alloc([128, NG * TC])
        RSB2, b_RSB2 = [_rsb, _rsb], [_brsb, _brsb]
        MSB2, b_MSB2 = pair([128, TC])
        RSTDa2, b_RSTDa2 = pair([128, TC])
        RSTDb2, b_RSTDb2 = pair([128, TC])
        SCR, b_SCR = A.alloc([128, TC])
        SF, b_SF = A.alloc([128, NG * TC])
        SBF, b_SBF = A.alloc([128, NG * TC], BF16)
        VSB2, b_VSB2 = pair([128, TC])
        KVSB2, b_KVSB2 = pair([128, TC])
        STG, b_STG = A.alloc([128, D])

        def xg(buf, g):
            return buf[:, g * TC:(g + 1) * TC]

        def g3(buf):
            return buf[:, 0:NG * TC].rearrange("p (g t) -> p g t", t=TC)

        def gt(buf):
            return buf[:, 0:NG * TC].rearrange("p (g t) -> p t g", t=TC)

        def pv(col0):
            return t_vec[:, col0:col0 + NG][:, None, :].broadcast_to([128, TC, NG])

        def bc3(tile):
            return tile[:, 0:TC][:, None, :].broadcast_to([128, NG, TC])

        def ln_stats(src_t, b_src, MSB, b_MSB, RSTD, b_RSTD, R2=None, b_R2=None, XSQ=None, b_XSQ=None):
            P.emit("act", lambda a: a.activation(out=XSQ[:], in_=src_t[:], func=AF.Square),
                   reads=[b_src], writes=[b_XSQ])

            def f_stats(t):
                for g in range(NG):
                    t.matmul(pb[6][:, 0:TC], t_om[:], xg(src_t, g), start=(g == 0), stop=(g == NG - 1))
                last = None
                for g in range(NG):
                    last = t.matmul(pb[6][:, TC:2 * TC], t_om[:], xg(XSQ, g), start=(g == 0), stop=(g == NG - 1))
                return last
            P.emit("pe", f_stats, reads=[b_src, b_XSQ, b_om], writes=[bpb[6]])

            def f_evac(a):
                a.copy(out=MSB[:], in_=pb[6][:, 0:TC])
                return a.copy(out=SCR[:], in_=pb[6][:, TC:2 * TC])
            P.emit("act", f_evac, reads=[bpb[6]], writes=[b_MSB, b_SCR])

            def f_small(v):
                v.tensor_scalar(out=MSB[:], in0=MSB[:], scalar1=1.0 / D, scalar2=None, op0=AL.mult)
                v.tensor_scalar(out=SCR[:], in0=SCR[:], scalar1=1.0 / D, scalar2=None, op0=AL.mult)
                v.tensor_tensor(out=RSTD[:], in0=MSB[:], in1=MSB[:], op=AL.mult)
                v.tensor_tensor(out=SCR[:], in0=SCR[:], in1=RSTD[:], op=AL.subtract)
                return v.tensor_scalar(out=SCR[:], in0=SCR[:], scalar1=EPS, scalar2=None, op0=AL.add)
            P.emit("dve", f_small, reads=[b_MSB, b_SCR], writes=[b_MSB, b_RSTD, b_SCR])
            if R2 is not None:
                def f_u2(v):
                    return v.tensor_scalar(out=R2[:], in0=SCR[:], scalar1=1.0 + EPS,
                                           scalar2=-EPS * EPS, op0=AL.mult, op1=AL.add)
                P.emit("dve", f_u2, reads=[b_SCR], writes=[b_R2])
                P.emit("act", lambda a: a.activation(out=R2[:], in_=R2[:], func=AF.Sqrt),
                       reads=[b_R2], writes=[b_R2])
            P.emit("act", lambda a: a.activation(out=SCR[:], in_=SCR[:], func=AF.Sqrt),
                   reads=[b_SCR], writes=[b_SCR])
            P.emit("dve", lambda v: v.reciprocal(out=RSTD[:], in_=SCR[:]),
                   reads=[b_SCR], writes=[b_RSTD])
            if R2 is not None:
                P.emit("dve", lambda v: v.reciprocal(out=R2[:], in_=R2[:]),
                       reads=[b_R2], writes=[b_R2])

        t_zero, b_zero = A.alloc([128, NG])
        P.emit("dve", lambda v: v.memset(t_zero[:], 0.0), writes=[b_zero])

        def emit_front(b, c):
                cp = (b * NCHUNK + c) % 2
                X, b_X = X2p[cp], b_X2p[cp]
                XSQ, b_XSQ = XSQ2[cp], b_XSQ2[cp]
                Y, b_Y = Y2[cp], b_Y2[cp]
                XN, b_XN = XN2[cp], b_XN2[cp]
                MIX, b_MIX = MIX2[cp], b_MIX2[cp]
                RSB, b_RSB = RSB2[cp], b_RSB2[cp]
                MSB, b_MSB = MSB2[cp], b_MSB2[cp]
                RSTD, b_RSTD = RSTDa2[cp], b_RSTDa2[cp]
                R2T, b_R2T = RSTDb2[cp], b_RSTDb2[cp]

                T0 = c * TC
                src = d_x[b].rearrange("(g p) t -> p g t", p=128)[:, :, T0:T0 + TC]
                P.emit("sync",
                       (lambda src=src, X=X: lambda s: s.dma_start(
                           out=X[:].rearrange("p (g t) -> p g t", t=TC), in_=src))(),
                       writes=[b_X], dma=True)

                P.level = 2
                P.cur_tag = "ln1"
                ln_stats(X, b_X, MSB, b_MSB, RSTD, b_RSTD, R2T, b_R2T, XSQ, b_XSQ)

                def f_ln12(v, X=X, XSQ=XSQ, Y=Y, XN=XN, MSB=MSB, RSTD=RSTD, R2T=R2T):
                    v.tensor_tensor(out=g3(XSQ), in0=g3(X), in1=bc3(MSB), op=AL.subtract)
                    v.tensor_tensor(out=g3(Y), in0=g3(XSQ), in1=bc3(RSTD), op=AL.mult)
                    v.tensor_tensor(out=g3(XN), in0=g3(XSQ), in1=bc3(R2T), op=AL.mult)
                    v.tensor_tensor(out=gt(XN), in0=gt(XN), in1=pv(0), op=AL.mult)
                    return v.tensor_tensor(out=gt(XN), in0=gt(XN), in1=pv(NG), op=AL.add)
                P.emit("dve", f_ln12, reads=[b_X, b_MSB, b_RSTD, b_R2T, b_vec],
                       writes=[b_Y, b_XSQ, b_XN])

                P.level = 4
                P.cur_tag = "mix"

                csh = t_zero if c == 0 else car_sh
                b_csh_r = b_zero if c == 0 else b_csh

                def f_mix(v, XSQ=XSQ, XN=XN, MIX=MIX, csh=csh):
                    for g in range(NG):
                        v.tensor_tensor(out=xg(XSQ, g)[:, 0:1], in0=xg(XN, g)[:, 0:1], in1=csh[:, g:g + 1], op=AL.subtract)
                    for g in range(NG):
                        v.tensor_tensor(out=xg(XSQ, g)[:, 1:], in0=xg(XN, g)[:, 1:], in1=xg(XN, g)[:, :-1], op=AL.subtract)
                    last = None
                    for mi, col in ((0, 2 * NG), (1, 3 * NG), (2, 4 * NG)):
                        mx = MIX[:, mi * NG * TC:(mi + 1) * NG * TC]
                        v.tensor_tensor(out=gt(mx), in0=gt(XSQ), in1=pv(col), op=AL.mult)
                        for g in range(NG):
                            v.tensor_tensor(out=xg(mx, g)[:, 1:], in0=xg(mx, g)[:, 1:], in1=xg(XN, g)[:, :-1], op=AL.add)
                            last = v.tensor_tensor(out=xg(mx, g)[:, 0:1], in0=xg(mx, g)[:, 0:1], in1=csh[:, g:g + 1], op=AL.add)
                    for g in range(NG):
                        last = v.tensor_copy(out=car_sh[:, g:g + 1], in_=xg(XN, g)[:, TC - 1:TC])
                    return last
                P.emit("dve", f_mix, reads=[b_XN, b_csh_r, b_vec], writes=[b_MIX, b_XSQ, b_csh])

        def emit_back(b, c):
                cp = (b * NCHUNK + c) % 2
                X, b_X = X2p[cp], b_X2p[cp]
                XSQ, b_XSQ = XSQ2[cp], b_XSQ2[cp]
                Y, b_Y = Y2[cp], b_Y2[cp]
                XN, b_XN = XN2[cp], b_XN2[cp]
                MIX, b_MIX = MIX2[cp], b_MIX2[cp]
                RSB, b_RSB = RSB2[cp], b_RSB2[cp]
                MSB, b_MSB = MSB2[cp], b_MSB2[cp]
                RSTD, b_RSTD = RSTDa2[cp], b_RSTDa2[cp]
                R2T, b_R2T = RSTDb2[cp], b_RSTDb2[cp]
                T0 = c * TC
                P.level = 5
                P.cur_tag = "kvr"
                for ec in range(NG):
                    par = ec % 2
                    kb, vb, rb = pb[2 * par][:, 0:TC], pb[2 * par][:, TC:2 * TC], pb[2 * par + 1][:, 0:TC]
                    b_kv, b_r = bpb[2 * par], bpb[2 * par + 1]
                    VSB, b_VSB = VSB2[ec % 2], b_VSB2[ec % 2]
                    KVSB, b_KVSB = KVSB2[ec % 2], b_KVSB2[ec % 2]

                    def f_kvr(t, ec=ec, kb=kb, vb=vb, rb=rb, MIX=MIX):
                        for g in range(NG):
                            t.matmul(kb, wslot(0, g)[:, ec * 128:(ec + 1) * 128],
                                     MIX[:, (0 * NG + g) * TC:(0 * NG + g + 1) * TC],
                                     start=(g == 0), stop=(g == NG - 1))
                        for g in range(NG):
                            t.matmul(vb, wslot(1, g)[:, ec * 128:(ec + 1) * 128],
                                     MIX[:, (1 * NG + g) * TC:(1 * NG + g + 1) * TC],
                                     start=(g == 0), stop=(g == NG - 1))
                        last = None
                        for g in range(NG):
                            last = t.matmul(rb, wslot(2, g)[:, ec * 128:(ec + 1) * 128],
                                            MIX[:, (2 * NG + g) * TC:(2 * NG + g + 1) * TC],
                                            start=(g == 0), stop=(g == NG - 1))
                        return last
                    P.emit("pe", f_kvr, reads=[b_MIX, b_wattn], writes=[b_kv, b_r])

                    P.emit("act", (lambda vb=vb, VSB=VSB: lambda a: a.copy(out=VSB[:], in_=vb))(),
                           reads=[b_kv], writes=[b_VSB])
                    P.emit("act", (lambda rb=rb, ec=ec, RSB=RSB: lambda a: a.activation(
                        out=xg(RSB, ec), in_=rb, func=AF.Sigmoid))(),
                        reads=[b_r], writes=[b_RSB])

                    def f_scan(v, ec=ec, kb=kb, VSB=VSB, KVSB=KVSB, c=c):
                        v.tensor_tensor(out=KVSB[:], in0=kb, in1=VSB[:], op=AL.mult)
                        v.tensor_tensor_scan(out=xg(SF, ec),
                                             data0=t_decay[:, ec:ec + 1].broadcast_to([128, TC]),
                                             data1=KVSB[:],
                                             initial=(0.0 if c == 0 else car_sc[:, ec:ec + 1]),
                                             op0=AL.mult, op1=AL.add)
                        v.tensor_copy(out=t_sp[:, 0:128], in_=t_sp[:, 128:256])
                        return v.tensor_copy(out=car_sc[:, ec:ec + 1], in_=xg(SF, ec)[:, TC - 1:TC])
                    P.emit("dve", f_scan, reads=[b_kv, b_VSB, b_decay, b_csc],
                           writes=[b_KVSB, b_SF, b_csc])
                    P.emit("act", (lambda ec=ec: lambda a: a.copy(out=xg(SBF, ec), in_=xg(SF, ec)))(),
                           reads=[b_SF], writes=[b_SBF])

                P.level = 6
                P.cur_tag = "oga"
                for ec in range(NG):
                    ob, b_ob = pb[4 + ec % 2][:, 0:TC], bpb[4 + ec % 2]
                    KVSB, b_KVSB = KVSB2[ec % 2], b_KVSB2[ec % 2]

                    def f_omm(t, ec=ec, ob=ob):
                        last = None
                        for g in range(NG):
                            last = t.matmul(ob, wslot(3, g)[:, ec * 128:(ec + 1) * 128],
                                            xg(SBF, g), start=(g == 0), stop=(g == NG - 1))
                        return last
                    P.emit("pe", f_omm, reads=[b_SBF, b_wattn], writes=[b_ob])

                    def f_gate(v, ec=ec, ob=ob, KVSB=KVSB, RSB=RSB, Y=Y, XN=XN):
                        v.tensor_tensor(out=KVSB[:], in0=ob, in1=xg(RSB, ec), op=AL.mult)
                        return v.tensor_tensor(out=xg(XN, ec), in0=KVSB[:], in1=xg(Y, ec), op=AL.add)
                    P.emit("dve", f_gate, reads=[b_ob, b_RSB, b_Y], writes=[b_KVSB, b_XN])

                P.level = 7
                P.cur_tag = "ln3"
                ln_stats(XN, b_XN, MSB, b_MSB, RSTD, b_RSTD, None, None, XSQ, b_XSQ)

                def f_ln3(v, XSQ=XSQ, XN=XN, X=X, MSB=MSB, RSTD=RSTD):
                    v.tensor_tensor(out=g3(XSQ), in0=g3(XN), in1=bc3(MSB), op=AL.subtract)
                    v.tensor_tensor(out=g3(XSQ), in0=g3(XSQ), in1=bc3(RSTD), op=AL.mult)
                    return v.tensor_tensor(out=X[:], in0=X[:], in1=XSQ[:], op=AL.add)
                P.emit("dve", f_ln3, reads=[b_XN, b_MSB, b_RSTD], writes=[b_X, b_XSQ])

                P.level = 8
                P.cur_tag = "tr"
                for sub in range(TC // 128):
                    for rnd in range(2):
                        def f_tr(t, sub=sub, rnd=rnd, X=X):
                            last = None
                            for j in range(4):
                                g = rnd * 4 + j
                                last = t.transpose(pb[7][:, j * 128:(j + 1) * 128],
                                                   xg(X, g)[:, sub * 128:(sub + 1) * 128], ident)
                            return last
                        P.emit("pe", f_tr, reads=[b_X, b_id], writes=[bpb[7]])

                        def f_ev(a, rnd=rnd):
                            return a.copy(out=STG[:, rnd * 512:(rnd + 1) * 512], in_=pb[7][:, 0:512])
                        P.emit("act", f_ev, reads=[bpb[7]], writes=[b_STG])

                    row0 = b * T + T0 + sub * 128
                    P.emit("sync",
                           (lambda row0=row0: lambda s: s.dma_start(
                               out=d_x2tm[row0:row0 + 128, :], in_=STG[:]))(),
                           reads=[b_STG], dma=True)

        bc_list = [(b, c) for b in range(B) for c in range(NCHUNK)]
        for i in range(len(bc_list) + 1):
            if i < len(bc_list):
                emit_front(*bc_list[i])
            if i >= 1:
                emit_back(*bc_list[i - 1])

        P.level = 1
        P.cur_tag = ""
        A.pop()
        P.barrier()

    if "b" in phases:
        P.cur_tag = "bc"
        A.push()
        w_gup, b_wgup = A.alloc([128, 2 * NG * I], BF16)
        qs = make_qscratch(1024)
        p_qv = pb[6].ap().bitcast(BF16) if hasattr(pb[6], "ap") else pb[6][:, 0:512].bitcast(BF16)
        p_qv = pb[6][:, 0:512].bitcast(BF16)
        b_pq = bpb[6]

        A.push()
        wstage, _ = A.alloc([128, 2 * D])
        b_ws = [Buf(), Buf()]
        wqt, b_wqt = A.alloc([128, D], BF16)
        b_dest = Buf()

        def mk_dest(mi):
            def dest(rt, j):
                return w_gup[:, (mi * NG + j) * I + rt * 128:(mi * NG + j) * I + rt * 128 + 128]
            return dest

        quant_weight(d_gw, I // 128, D, t_gs[:, 4:5], t_gs[:, 5:6], mk_dest(0), qs,
                     wstage, b_ws, wqt, b_wqt, p_qv, b_pq, b_dest)
        quant_weight(d_uw, I // 128, D, t_gs[:, 6:7], t_gs[:, 7:8], mk_dest(1), qs,
                     wstage, b_ws, wqt, b_wqt, p_qv, b_pq, b_dest)
        if "wg" in _dbg:
            d_wg = nc.dram_tensor("wgdump", [128, 2 * NG * I], BF16, kind="ExternalOutput").ap()
            d_gsd = nc.dram_tensor("gsdump", [128, 8], F32, kind="ExternalOutput").ap()
            P.barrier()
            P.emit("sync", lambda s: s.dma_start(out=d_wg, in_=w_gup[:]), reads=[b_wgup], dma=True)
            P.emit("sync", lambda s: s.dma_start(out=d_gsd, in_=t_gs[:]), reads=[b_gs], dma=True)
        A.pop()
        P.barrier()

        _skipB = "wo" in _dbg
        X2, _ = A.alloc([128, 2 * D])
        b_X2 = [Buf(), Buf()]

        def bpair(shape, dt=F32):
            t0, bb0 = A.alloc(shape, dt)
            t1, bb1 = A.alloc(shape, dt)
            return [t0, t1], [bb0, bb1]

        _xn4, _bxn4 = A.alloc([128, D])
        XN42, b_XN42 = [_xn4, _xn4], [_bxn4, _bxn4]
        _xq, _bxq = A.alloc([128, D], BF16)
        XQ2, b_XQ2 = [_xq, _xq], [_bxq, _bxq]
        XQF2, b_XQF2 = bpair([128, D], BF16)
        HH2, b_HH2 = bpair([128, 1024])
        SGT2, b_SGT2 = bpair([128, 512])
        _sgu0, _bsgu0 = A.alloc([128, 512])
        SGU2, b_SGU2 = [_sgu0, _sgu0], [_bsgu0, _bsgu0]
        HQ2, b_HQ2 = bpair([128, 1024], BF16)
        HQT2, b_HQT2 = bpair([128, 1024], BF16)
        cst2p, b_cst2p = bpair([128, 8])
        qsx = make_qscratch(1024)
        qsh2 = [make_qscratch(1024), make_qscratch(1024)]

        p_g = [pb[0], pb[1]]
        p_u = [pb[2], pb[3]]
        bg = [bpb[0], bpb[1]]
        bu = [bpb[2], bpb[3]]
        p_tr2 = [pb[6][:, 0:512].bitcast(BF16), pb[7][:, 0:512].bitcast(BF16)]
        b_tr2 = [bpb[6], bpb[7]]

        def emit_b_front(tt):
            p2 = tt % 2
            x2sl = X2[:, p2 * D:(p2 + 1) * D]
            XN4, b_XN4 = XN42[p2], b_XN42[p2]
            XQ, b_XQ = XQ2[p2], b_XQ2[p2]
            XQF, b_XQF = XQF2[p2], b_XQF2[p2]
            HH, b_HH = HH2[p2], b_HH2[p2]
            SGT, b_SGT = SGT2[p2], b_SGT2[p2]
            SGU, b_SGU = SGU2[p2], b_SGU2[p2]
            HQ, b_HQ = HQ2[p2], b_HQ2[p2]
            HQT, b_HQT = HQT2[p2], b_HQT2[p2]
            cst, b_cst = cst2p[p2], b_cst2p[p2]
            p_qv2, b_pq2v = p_tr2[p2], b_tr2[p2]

            P.emit("sync", (lambda x2sl=x2sl, src=d_x2tm[tt * 128:(tt + 1) * 128, :]:
                            lambda s: s.dma_start(out=x2sl, in_=src))(),
                   writes=[b_X2[p2]], dma=True)

            P.emit("dve", (lambda x2sl=x2sl, cst=cst: lambda v: v.reduce_sum(out=cst[:, 0:1], in_=x2sl, axis=AX.X))(),
                   reads=[b_X2[p2]], writes=[b_cst])
            P.emit("act", (lambda x2sl=x2sl, cst=cst: lambda a: a.activation(
                out=qsx["asg"][0][:, 0:D], in_=x2sl, func=AF.Square, accum_out=cst[:, 1:2]))(),
                reads=[b_X2[p2]], writes=[qsx["asg"][1], b_cst])

            def f_small(v, cst=cst):
                v.tensor_scalar(out=cst[:, 2:3], in0=cst[:, 0:1], scalar1=1.0 / D, scalar2=None, op0=AL.mult)
                v.tensor_scalar(out=cst[:, 3:4], in0=cst[:, 1:2], scalar1=1.0 / D, scalar2=None, op0=AL.mult)
                v.tensor_copy(out=t_sp[:, 0:128], in_=t_sp[:, 128:256])
                v.tensor_tensor(out=cst[:, 4:5], in0=cst[:, 2:3], in1=cst[:, 2:3], op=AL.mult)
                v.tensor_copy(out=t_sp[:, 0:128], in_=t_sp[:, 128:256])
                v.tensor_tensor(out=cst[:, 3:4], in0=cst[:, 3:4], in1=cst[:, 4:5], op=AL.subtract)
                v.tensor_copy(out=t_sp[:, 0:128], in_=t_sp[:, 128:256])
                return v.tensor_scalar(out=cst[:, 3:4], in0=cst[:, 3:4], scalar1=EPS, scalar2=None, op0=AL.add)
            P.emit("dve", f_small, reads=[b_cst], writes=[b_cst])
            P.emit("act", (lambda cst=cst: lambda a: a.activation(out=cst[:, 4:5], in_=cst[:, 3:4], func=AF.Sqrt))(),
                   reads=[b_cst], writes=[b_cst])
            P.emit("dve", (lambda cst=cst: lambda v: v.reciprocal(out=cst[:, 3:4], in_=cst[:, 4:5]))(),
                   reads=[b_cst], writes=[b_cst])
            P.emit("dve", (lambda x2sl=x2sl, XN4=XN4, cst=cst: lambda v: v.tensor_scalar(
                out=XN4[:], in0=x2sl, scalar1=cst[:, 2:3], scalar2=cst[:, 3:4],
                op0=AL.subtract, op1=AL.mult))(),
                reads=[b_X2[p2], b_cst], writes=[b_XN4])

            emit_quant(XN4[:], b_XN4, XQ[:], b_XQ, t_gs[:, 0:1], t_gs[:, 1:2], D, qsx)

            def f_trx(t, XQ=XQ, p_qv2=p_qv2):
                last = None
                for j in range(8):
                    last = t.transpose(p_qv2[:, j * 128:(j + 1) * 128], XQ[:, j * 128:(j + 1) * 128], t_idbf[:])
                return last
            P.emit("pe", f_trx, reads=[b_XQ, b_idbf], writes=[b_pq2v])
            P.emit("act", (lambda XQF=XQF, p_qv2=p_qv2: lambda a: a.copy(out=XQF[:], in_=p_qv2[:]))(),
                   reads=[b_pq2v], writes=[b_XQF])

        def emit_b_back(tt):
            p2 = tt % 2
            XQF, b_XQF = XQF2[p2], b_XQF2[p2]

            def mm(kk):
                for half in range(2):
                    hs = kk * 2 + half
                    par = hs % 2

                    def f_gu(t, hs=hs, par=par, XQF=XQF):
                        for hc in range(NG):
                            t.matmul(p_g[par][:, 0:512], XQF[:, hc * 128:(hc + 1) * 128],
                                     w_gup[:, (0 * NG + hc) * I + hs * 512:(0 * NG + hc) * I + hs * 512 + 512],
                                     start=(hc == 0), stop=(hc == NG - 1))
                        last = None
                        for hc in range(NG):
                            last = t.matmul(p_u[par][:, 0:512], XQF[:, hc * 128:(hc + 1) * 128],
                                            w_gup[:, (1 * NG + hc) * I + hs * 512:(1 * NG + hc) * I + hs * 512 + 512],
                                            start=(hc == 0), stop=(hc == NG - 1))
                        return last
                    P.emit("pe", f_gu, reads=[b_XQF, b_wgup], writes=[bg[par], bu[par]])

            def evac(kk):
                kp = kk % 2
                HH, b_HH = HH2[kp], b_HH2[kp]
                for half in range(2):
                    hs = kk * 2 + half
                    par = hs % 2
                    SGT, b_SGT = SGT2[half], b_SGT2[half]
                    SGU, b_SGU = SGU2[half], b_SGU2[half]
                    P.emit("act", (lambda par=par, SGT=SGT: lambda a: a.activation(
                        out=SGT[:], in_=p_g[par][:, 0:512], func=AF.Silu))(),
                        reads=[bg[par]], writes=[b_SGT])
                    P.emit("act", (lambda par=par, SGU=SGU: lambda a: a.copy(
                        out=SGU[:], in_=p_u[par][:, 0:512]))(),
                        reads=[bu[par]], writes=[b_SGU])
                    P.emit("gp", (lambda half=half, SGT=SGT, SGU=SGU, HH=HH: lambda g: g.tensor_tensor(
                        out=HH[:, half * 512:(half + 1) * 512], in0=SGT[:], in1=SGU[:], op=AL.mult))(),
                        reads=[b_SGT, b_SGU], writes=[b_HH])

            def quant(kk):
                kp = kk % 2
                HH, b_HH = HH2[kp], b_HH2[kp]
                HQ, b_HQ = HQ2[kp], b_HQ2[kp]
                HQT, b_HQT = HQT2[kp], b_HQT2[kp]
                p_qv2, b_pq2v = p_tr2[kp], b_tr2[kp]
                qsh = qsh2[kp]
                emit_quant(HH[:], b_HH, HQ[:], b_HQ, t_gs[:, 2:3], t_gs[:, 3:4], 1024, qsh)

                def f_trh(t, HQ=HQ, p_qv2=p_qv2):
                    last = None
                    for j in range(8):
                        last = t.transpose(p_qv2[:, j * 128:(j + 1) * 128], HQ[:, j * 128:(j + 1) * 128], t_idbf[:])
                    return last
                P.emit("pe", f_trh, reads=[b_HQ, b_idbf], writes=[b_pq2v])
                P.emit("act", (lambda HQT=HQT, p_qv2=p_qv2: lambda a: a.copy(out=HQT[:], in_=p_qv2[:]))(),
                       reads=[b_pq2v], writes=[b_HQT])
                dst = d_hq[kk * 8:(kk + 1) * 8, :, tt * 128:(tt + 1) * 128].rearrange("i p t -> p i t")
                P.emit("sync", (lambda dst=dst, HQT=HQT: lambda s: s.dma_start(
                    out=dst, in_=HQT[:].rearrange("p (i t) -> p i t", t=128)))(),
                    reads=[b_HQT], dma=True)

            mm(0)
            evac(0)
            mm(1)
            evac(1)
            quant(0)
            mm(2)
            evac(2)
            quant(1)
            mm(3)
            evac(3)
            quant(2)
            quant(3)

        nbt = NTT if not _skipB else 0
        for i in range(nbt + 1):
            if i < nbt:
                emit_b_front(i)
            if i >= 1:
                emit_b_back(i - 1)

        A.pop()
        P.barrier()


    if "c" in phases:
        A.push()
        w_dn, b_wdn = A.alloc([128, NIC * D], BF16)

        A.push()
        qs2 = make_qscratch(4096)
        wstage2, _ = A.alloc([128, 2 * I])
        b_ws2 = [Buf(), Buf()]
        wqt2, b_wqt2 = A.alloc([128, I], BF16)
        b_dest2 = Buf()
        p_qv2 = pb[6][:, 0:512].bitcast(BF16)
        b_pq2 = bpb[6]

        def dest_dn(rt, j):
            return w_dn[:, j * D + rt * 128:j * D + rt * 128 + 128]

        quant_weight(d_dw, D // 128, I, t_gs[:, 4:5], t_gs[:, 5:6], dest_dn, qs2,
                     wstage2, b_ws2, wqt2, b_wqt2, p_qv2, b_pq2, b_dest2)
        A.pop()
        P.barrier()

        HQF, _ = A.alloc([128, 2 * NIC * 128], BF16)
        b_HQF = [Buf(), Buf()]
        X2c, _ = A.alloc([128, 2 * D])
        b_X2c = [Buf(), Buf()]
        OC, _ = A.alloc([128, 2 * D])
        b_OC = [Buf(), Buf()]
        _s0, _bs0 = A.alloc([128, D])
        SQSp, b_SQSp = [_s0, _s0], [_bs0, _bs0]
        _c0, _bc0 = A.alloc([128, 8])
        _c1, _bc1 = A.alloc([128, 8])
        cst2p_, b_cst2p_ = [_c0, _c1], [_bc0, _bc1]

        p_o = [pb[0], pb[1], pb[2], pb[3]]
        bo = [bpb[0], bpb[1], bpb[2], bpb[3]]

        for tt in range(NTT):
            p2 = tt % 2
            SQS, b_SQS = SQSp[p2], b_SQSp[p2]
            cst2, b_cst2 = cst2p_[p2], b_cst2p_[p2]
            hsl = HQF[:, p2 * NIC * 128:(p2 + 1) * NIC * 128]
            x2sl = X2c[:, p2 * D:(p2 + 1) * D]
            osl = OC[:, p2 * D:(p2 + 1) * D]
            srch = d_hq[:, :, tt * 128:(tt + 1) * 128].rearrange("i p t -> p i t")
            P.emit("sync", (lambda hsl=hsl, srch=srch: lambda s: s.dma_start(
                out=hsl.rearrange("p (i t) -> p i t", t=128), in_=srch))(),
                writes=[b_HQF[p2]], dma=True)
            P.emit("sync", (lambda x2sl=x2sl, src=d_x2tm[tt * 128:(tt + 1) * 128, :]:
                            lambda s: s.dma_start(out=x2sl, in_=src))(),
                   writes=[b_X2c[p2]], dma=True)

            for half in range(2):
                pj = p2 * 2 + half

                def f_dn(t, hsl=hsl, half=half, pj=pj):
                    last = None
                    for ic in range(NIC):
                        last = t.matmul(p_o[pj][:, 0:512], hsl[:, ic * 128:(ic + 1) * 128],
                                        w_dn[:, ic * D + half * 512:ic * D + half * 512 + 512],
                                        start=(ic == 0), stop=(ic == NIC - 1))
                    return last
                P.emit("pe", f_dn, reads=[b_HQF[p2], b_wdn], writes=[bo[pj]])
                P.emit("dve", (lambda osl=osl, half=half, pj=pj: lambda v: v.tensor_scalar(
                    out=osl[:, half * 512:(half + 1) * 512], in0=p_o[pj][:, 0:512], scalar1=1.0, scalar2=0.0,
                    op0=AL.mult, op1=AL.add, accum_out=cst2[:, half:half + 1]))(),
                    reads=[bo[pj]], writes=[b_OC[p2], b_cst2])

            P.emit("act", (lambda osl=osl: lambda a: a.activation(
                out=SQS[:], in_=osl, func=AF.Square, accum_out=cst2[:, 2:3]))(),
                reads=[b_OC[p2]], writes=[b_SQS, b_cst2])

            def f_small2(v):
                v.tensor_tensor(out=cst2[:, 0:1], in0=cst2[:, 0:1], in1=cst2[:, 1:2], op=AL.add)
                v.tensor_scalar(out=cst2[:, 3:4], in0=cst2[:, 0:1], scalar1=1.0 / D, scalar2=None, op0=AL.mult)
                v.tensor_scalar(out=cst2[:, 4:5], in0=cst2[:, 2:3], scalar1=1.0 / D, scalar2=None, op0=AL.mult)
                v.tensor_tensor(out=cst2[:, 5:6], in0=cst2[:, 3:4], in1=cst2[:, 3:4], op=AL.mult)
                v.tensor_tensor(out=cst2[:, 4:5], in0=cst2[:, 4:5], in1=cst2[:, 5:6], op=AL.subtract)
                return v.tensor_scalar(out=cst2[:, 4:5], in0=cst2[:, 4:5], scalar1=EPS, scalar2=None, op0=AL.add)
            P.emit("dve", f_small2, reads=[b_cst2], writes=[b_cst2])
            P.emit("act", lambda a: a.activation(out=cst2[:, 5:6], in_=cst2[:, 4:5], func=AF.Sqrt),
                   reads=[b_cst2], writes=[b_cst2])
            P.emit("dve", lambda v: v.reciprocal(out=cst2[:, 4:5], in_=cst2[:, 5:6]),
                   reads=[b_cst2], writes=[b_cst2])

            def f_out(v, osl=osl, x2sl=x2sl):
                v.tensor_scalar(out=osl, in0=osl, scalar1=cst2[:, 3:4], scalar2=cst2[:, 4:5],
                                op0=AL.subtract, op1=AL.mult)
                return v.tensor_tensor(out=osl, in0=osl, in1=x2sl, op=AL.add)
            P.emit("dve", f_out, reads=[b_cst2, b_X2c[p2], b_OC[p2]], writes=[b_OC[p2]])

            P.emit("sync", (lambda osl=osl, dst=d_out[tt * 128:(tt + 1) * 128, :]:
                            lambda s: s.dma_start(out=dst, in_=osl))(),
                   reads=[b_OC[p2]], dma=True)

        A.pop()

    P.barrier()
    P.emit("sync", lambda s: s.nop())

    sems = {e: es.enter_context(nc.semaphore(f"sem_{e}")) for e in ENGS}
    block = es.enter_context(nc.Block())

    @block.tensor
    def _(t):
        P.play("pe", t, sems)

    @block.vector
    def _(v):
        P.play("dve", v, sems)

    @block.scalar
    def _(a):
        P.play("act", a, sems)

    @block.gpsimd
    def _(g):
        P.play("gp", g, sems)

    @block.sync
    def _(s):
        P.play("sync", s, sems)

    es.close()
    mybir.codegen_inst_isa_subclasses(nc)
    return nc


def _prep_inputs(inputs, B_core, T, core, n_cores):
    f32 = np.float32
    x = np.asarray(inputs["L0_attn_x"], f32)
    b0 = core * B_core
    x_fm = np.ascontiguousarray(x[b0:b0 + B_core, :T].transpose(0, 2, 1))
    vecs = np.stack([
        np.ascontiguousarray(np.asarray(inputs[k], f32).reshape(8, 128).T)
        for k in ("L0_attn_ln_w", "L0_attn_ln_b", "L0_attn_tmk", "L0_attn_tmv",
                  "L0_attn_tmr", "L0_attn_decay_w")])
    ident = np.zeros((128, 129), f32)
    ident[:, 0] = 1.0
    ident[:, 1:129] = np.eye(128, dtype=f32)
    return {
        "x_fm": x_fm,
        "wkT": np.ascontiguousarray(np.asarray(inputs["L0_attn_Wk"], f32).T),
        "wvT": np.ascontiguousarray(np.asarray(inputs["L0_attn_Wv"], f32).T),
        "wrT": np.ascontiguousarray(np.asarray(inputs["L0_attn_Wr"], f32).T),
        "woT": np.ascontiguousarray(np.asarray(inputs["L0_attn_Wo"], f32).T),
        "gate_w": np.ascontiguousarray(np.asarray(inputs["L0_ffn_gate_w"], f32)),
        "up_w": np.ascontiguousarray(np.asarray(inputs["L0_ffn_up_w"], f32)),
        "down_w": np.ascontiguousarray(np.asarray(inputs["L0_ffn_down_w"], f32)),
        "vecs": np.ascontiguousarray(vecs),
        "s_in2": np.array([[float(np.asarray(inputs["L0_ffn_s_in"]).reshape(-1)[0]),
                            float(np.asarray(inputs["L0_ffn_s_in_down"]).reshape(-1)[0])]], f32),
        "ident": ident,
        "onesrow": np.ones((1, 128), f32),
        "onesmat": np.ones((128, 128), f32),
    }


_NC_CACHE = {}


def kernel(**inputs):
    B_total = int(np.asarray(inputs["L0_attn_x"]).shape[0])
    T = int(np.asarray(inputs["L0_attn_x"]).shape[1])
    n_cores = 8
    B_core = B_total // n_cores
    key = (B_core, T)
    if key not in _NC_CACHE:
        _NC_CACHE[key] = build_program(B=B_core, T=T)
    nc = _NC_CACHE[key]
    in_maps = [_prep_inputs(inputs, B_core, T, c, n_cores) for c in range(n_cores)]
    res = run_bass_kernel_spmd(nc, in_maps, core_ids=list(range(n_cores)))
    outs = [np.asarray(r["out"]).reshape(B_core, T, 1024) for r in res.results]
    return np.concatenate(outs, axis=0)


# version 06 (final)
# speedup vs baseline: 1.1336x; 1.0075x over previous
import contextlib
import numpy as np
import ml_dtypes

import concourse.bass as bass
import concourse.mybir as mybir
from concourse.bass_utils import run_bass_kernel_spmd

F32 = mybir.dt.float32
F32R = mybir.dt.float32r
BF16 = mybir.dt.bfloat16
FP8E4 = mybir.dt.float8e4
U8 = mybir.dt.uint8
U32 = mybir.dt.uint32
AL = mybir.AluOpType
AF = mybir.ActivationFunctionType
AX = mybir.AxisListType

from concourse.dve_spec import Spec, Src0, Src1, C0, C1, C2, maxx, minn, lower
from concourse.dve_op import DveOp
import concourse.dve_ops as dve_ops

MAGIC = float(1.5 * 2 ** 23)


def _register_op(name, spec, reference):
    for op in dve_ops.OPS:
        if op.name == name:
            return op
    shas = {}
    for ver in ("v3", "v4"):
        try:
            shas[ver] = lower(spec, ver=ver, name=name).sha(ver)
        except Exception:
            pass
    op = DveOp(name, spec, subdim=False, uops_sha=shas, reference=reference)
    dve_ops.OPS.append(op)
    idx = len(dve_ops.OPS) - 1
    dve_ops._SUB_OPCODE_FOR_NAME[name] = dve_ops._CUSTOM_DVE_ROW_BASE + idx
    assert dve_ops._SUB_OPCODE_FOR_NAME[name] < 0x20
    dve_ops.CUSTOM_DVE_SPECS[name] = spec
    dve_ops.CUSTOM_DVE_REFS[name] = reference
    return op


def _compand_spec():
    d1 = Src0 + Src0
    d2 = Src0 + C1
    m1 = minn(d1, d2)
    e1 = Src0 * C0
    d3 = e1 + (C1 + C1)
    m2 = minn(m1, d3)
    z = m2 + C2
    return Spec(body=z - C2)


def _compand_ref(in0, in1, s0, s1, imm2):
    G = np.minimum(np.minimum(2.0 * in0, in0 + 2.0), 0.5 * in0 + 4.0)
    return np.round(G)


def _uncompand_spec():
    w1 = Src0 + Src0
    w1b = w1 - C0
    v1 = maxx(Src0, w1b)
    w2 = w1 + w1
    w2b = w2 - C1
    v2 = maxx(v1, w2b)
    capped = minn(v2, C2)
    return Spec(body=capped * Src1)


def _uncompand_ref(in0, in1, s0, s1, imm2):
    v = np.maximum(np.maximum(in0, 2.0 * in0 - 4.0), 4.0 * in0 - 16.0)
    return np.minimum(v, 12.0) * in1


COMPAND = _register_op("NVFP4_COMPAND", _compand_spec(), _compand_ref)
UNCOMPAND = _register_op("NVFP4_UNCOMPAND", _uncompand_spec(), _uncompand_ref)

ENGS = ("pe", "dve", "act", "gp", "sync")


class Buf:
    __slots__ = ("w", "rs")

    def __init__(self):
        self.w = None
        self.rs = []


class Prog:
    def __init__(self):
        self.items = []
        self.count = {e: 0 for e in ENGS}
        self.pending = {e: [] for e in ENGS}
        self.level = 1
        self.maxlev = 99
        self.spacer_fns = {}
        self.serialize = False

    def barrier(self):
        snap = [(e, self.count[e]) for e in ENGS if self.count[e] > 0]
        for e in ENGS:
            self.pending[e] = list(snap)

    def emit(self, eng, fn, reads=(), writes=(), deps=(), dma=False):
        if self.level > self.maxlev:
            return None
        if self.serialize or (getattr(self, "cur_tag", "") in getattr(self, "ser_tags", set())):
            self.barrier()
        dd = {}

        def add(d, keep_same=False):
            if d is not None and (keep_same or d[0] != eng):
                dd[d[0]] = max(dd.get(d[0], 0), d[1])
        for d in deps:
            add(d)
        for d in self.pending[eng]:
            add(d, keep_same=True)
        for b in reads:
            add(b.w)
        for b in writes:
            for d in (b.rs + [b.w]):
                add(d)
        self.pending[eng] = []
        inc = 16 if dma else 1
        hazard = self.serialize or getattr(self, "allspace", False)
        if eng in ("dve", "act", "gp"):
            prev_tok = (eng, self.count[eng])
            for b in reads:
                if b.w == prev_tok:
                    hazard = True
            for b in writes:
                if b.w == prev_tok or (b.rs and b.rs[-1] == prev_tok):
                    hazard = True
        self.items.append((eng, fn, sorted(dd.items()), inc, hazard))
        self.count[eng] += inc
        tok = (eng, self.count[eng])
        for b in reads:
            b.rs.append(tok)
        for b in writes:
            b.w = tok
            b.rs = []
        return tok

    def play(self, engname, obj, sems):
        sp = self.spacer_fns.get(engname)
        for item in self.items:
            eng, fn, deps, inc, hazard = item
            if eng != engname:
                continue
            for de, dc in deps:
                obj.wait_ge(sems[de], dc)
            if hazard and sp is not None:
                sp(obj)
            last = fn(obj)
            if last is None:
                last = obj.nop()
            last.then_inc(sems[eng], inc)


class Arena:

    def __init__(self, tensor, nbytes):
        self.t = tensor
        self.nbytes = nbytes
        self.off = 0
        self.marks = []

    def push(self):
        self.marks.append(self.off)

    def pop(self):
        self.off = self.marks.pop()

    def alloc(self, shape, dt=F32):
        p, n = shape
        assert p <= 128
        esz = {F32: 4, BF16: 2, FP8E4: 1, U32: 4, U8: 1}[dt]
        nb = n * esz
        off = (self.off + 511) & ~511
        assert off + nb <= self.nbytes, f"arena overflow: {off + nb} > {self.nbytes}"
        self.off = off + nb
        view = self.t[:, off:off + nb]
        if dt != U8:
            view = view.bitcast(dt)
        return view[0:p, :], Buf()


def build_program(B=2, T=4096, TC=256, phases="abc", maxlev=99):
    D, I = 1024, 4096
    NG = D // 128
    NCHUNK = T // TC
    NTT = (B * T) // 128
    NIC = I // 128
    EPS = 1e-5

    nc = bass.Bass("TRN2", target_bir_lowering=False,
                   dynamic_dma_scratch_size=4096)

    d_x = nc.dram_tensor("x_fm", [B, D, T], F32, kind="ExternalInput").ap()
    d_w = [nc.dram_tensor(n, [D, D], F32, kind="ExternalInput").ap()
           for n in ("wkT", "wvT", "wrT", "woT")]
    d_gw = nc.dram_tensor("gate_w", [I, D], F32, kind="ExternalInput").ap()
    d_uw = nc.dram_tensor("up_w", [I, D], F32, kind="ExternalInput").ap()
    d_dw = nc.dram_tensor("down_w", [D, I], F32, kind="ExternalInput").ap()
    d_vec = nc.dram_tensor("vecs", [6, 128, NG], F32, kind="ExternalInput").ap()
    d_sin = nc.dram_tensor("s_in2", [1, 2], F32, kind="ExternalInput").ap()
    d_id = nc.dram_tensor("ident", [128, 129], F32, kind="ExternalInput").ap()
    d_orow = nc.dram_tensor("onesrow", [1, 128], F32, kind="ExternalInput").ap()
    d_om = nc.dram_tensor("onesmat", [128, 128], F32, kind="ExternalInput").ap()
    d_out = nc.dram_tensor("out", [B * T, D], F32, kind="ExternalOutput").ap()
    import os as _os
    _dbg = _os.environ.get("KDBG", "")
    d_x2tm = nc.dram_tensor("x2tm", [B * T, D], F32,
                            kind="ExternalOutput" if "x2" in _dbg else "Internal").ap()
    d_hq_kind = "ExternalOutput" if "hq" in _dbg else "Internal"
    d_hq = nc.dram_tensor("hq", [NIC, 128, B * T], BF16, kind=d_hq_kind).ap()

    P = Prog()
    import os as _os2
    _kser = _os2.environ.get("KSER", "")
    P.serialize = _kser == "1"
    P.allspace = _os2.environ.get("KSPACE", "0") == "1"
    P.ser_tags = set(_kser.split(",")) if _kser not in ("", "0", "1") else set()
    P.ser_tags.add("su")
    P.cur_tag = ""
    P.maxlev = maxlev
    es = contextlib.ExitStack()

    ARENA_BYTES = 216 * 1024 - 2048
    arena_t = es.enter_context(nc.sbuf_tensor("arena", [128, ARENA_BYTES], U8))
    A = Arena(arena_t, ARENA_BYTES)

    def SBP(name, shape, dt=F32):
        return es.enter_context(nc.sbuf_tensor(name, shape, dt)), Buf()

    t_id, b_id = SBP("t_id", [128, 129])
    t_idbf, b_idbf = SBP("t_idbf", [128, 128], BF16)
    t_onebf, b_onebf = SBP("t_onebf", [128, 1], BF16)
    t_or, b_or = SBP("t_or", [1, 128])
    t_om, b_om = SBP("t_om", [128, 128])
    t_ombf, b_ombf = SBP("t_ombf", [128, 128], BF16)
    t_vec, b_vec = SBP("t_vec", [128, 6 * NG])
    t_decay, b_decay = SBP("t_decay", [128, NG])
    t_gs, b_gs = SBP("t_gs", [128, 8])
    car_sh, b_csh = SBP("car_sh", [128, NG])
    car_sc, b_csc = SBP("car_sc", [128, NG])
    wmaxt, b_wmax = SBP("wmaxt", [128, 128])
    gsm, b_gsm = SBP("gsm", [1, 4])

    ones_col = t_id[:, 0:1]
    ident = t_id[:, 1:129]
    t_sp, _bsp = SBP("t_sp", [128, 256])
    P.spacer_fns["dve"] = lambda v: v.tensor_copy(out=t_sp[:, 0:128], in_=t_sp[:, 128:256])
    P.spacer_fns["act"] = lambda a: a.copy(out=t_sp[:, 0:128], in_=t_sp[:, 128:256])
    P.spacer_fns["gp"] = lambda g: g.tensor_copy(out=t_sp[:, 0:128], in_=t_sp[:, 128:256])

    pb = []
    bpb = []
    for i in range(8):
        t = es.enter_context(nc.psum_tensor(f"pb{i}", [128, 512], F32))
        pb.append(t)
        bpb.append(Buf())

    P.cur_tag = "su"
    P.emit("sync", lambda s: s.dma_start(out=t_vec[:].rearrange("p (k g) -> p k g", g=NG), in_=d_vec.transpose([1, 0, 2])),
           writes=[b_vec], dma=True)
    P.emit("sync", lambda s: s.dma_start(out=t_gs[0:1, 0:2], in_=d_sin[:]), writes=[b_gs], dma=True)
    P.emit("sync", lambda s: s.dma_start(out=t_id[:], in_=d_id[:]), writes=[b_id], dma=True)
    P.emit("sync", lambda s: s.dma_start(out=t_or[:], in_=d_orow[:]), writes=[b_or], dma=True)
    P.emit("sync", lambda s: s.dma_start(out=t_om[:], in_=d_om[:]), writes=[b_om], dma=True)

    P.emit("act", lambda a: a.activation(out=t_decay[:], in_=t_vec[:, 5 * NG:6 * NG], func=AF.Sigmoid),
           reads=[b_vec], writes=[b_decay])
    P.emit("act", lambda a: a.copy(out=t_idbf[:], in_=ident), reads=[b_id], writes=[b_idbf])
    P.emit("act", lambda a: a.copy(out=t_ombf[:], in_=t_om[:]), reads=[b_om], writes=[b_ombf])
    P.emit("act", lambda a: a.copy(out=t_onebf[:], in_=ones_col), reads=[b_id], writes=[b_onebf])

    P.emit("pe", lambda t: t.matmul(pb[0][:, 0:2], t_or[:], t_gs[0:1, 0:2],
                                    start=True, stop=True),
           reads=[b_or, b_gs], writes=[bpb[0]])

    def f_gs(v):
        v.tensor_scalar(out=t_gs[:, 0:1], in0=pb[0][:, 0:1], scalar1=1.0 / 6.0, scalar2=None, op0=AL.mult)
        v.tensor_scalar(out=t_gs[:, 2:3], in0=pb[0][:, 1:2], scalar1=1.0 / 6.0, scalar2=None, op0=AL.mult)
        v.reciprocal(out=t_gs[:, 1:2], in_=pb[0][:, 0:1])
        return v.reciprocal(out=t_gs[:, 3:4], in_=pb[0][:, 1:2])
    P.emit("dve", f_gs, reads=[bpb[0]], writes=[b_gs])

    def emit_quant(src_ap, b_src, out_ap, b_out, gs6_ap, rgs_ap, NN, q):
        nb = NN // 16
        amax, b_amax = q["amax"]
        scz, b_scz = q["scz"]
        sgn, b_sgn = q["sgn"]
        asg, b_asg = q["asg"]
        fp8t, b_fp8 = q["fp8"]
        msku, b_msk = q["msk"]

        P.emit("dve", lambda v: v.tensor_reduce(
            out=amax[:, 0:nb][:, :, None], in_=src_ap.rearrange("p (nb w) -> p nb w", w=16),
            axis=AX.X, op=AL.max, apply_absolute_value=True),
            reads=[b_src], writes=[b_amax])

        def f_sc(v):
            v.tensor_scalar(out=scz[:, 0:nb], in0=amax[:, 0:nb], scalar1=gs6_ap, scalar2=None, op0=AL.mult)
            v.tensor_scalar(out=scz[:, 0:nb], in0=scz[:, 0:nb], scalar1=0.0, scalar2=448.0,
                            op0=AL.max, op1=AL.min)
            return v.tensor_scalar(out=scz[:, nb:2 * nb], in0=scz[:, 0:nb],
                                   scalar1=float(2 ** 14), scalar2=float(2 ** 14),
                                   op0=AL.add, op1=AL.subtract)
        P.emit("dve", f_sc, reads=[b_amax, b_gs], writes=[b_scz])

        def f_fp8(a):
            a.mul(out=scz[:, 2 * nb:3 * nb], in_=scz[:, 0:nb], mul=0.5)
            a.copy(out=fp8t[:, 0:nb], in_=scz[:, 2 * nb:3 * nb])
            a.activation(out=scz[:, 2 * nb:3 * nb], in_=fp8t[:, 0:nb], func=AF.Copy, scale=2.0)
            a.activation(out=asg[:, 0:NN], in_=src_ap, func=AF.Abs)
            return a.sign(out=sgn[:, 0:NN], in_=src_ap)
        P.emit("act", f_fp8, reads=[b_scz, b_src], writes=[b_scz, b_fp8, b_sgn, b_asg])

        def f_sel(v):
            v.tensor_scalar(out=msku[:, 0:nb], in0=scz[:, 0:nb], scalar1=float(2.0 ** -5),
                            scalar2=None, op0=AL.is_lt)
            v.select(out=scz[:, 0:nb], mask=msku[:, 0:nb], on_true=scz[:, nb:2 * nb],
                     on_false=scz[:, 2 * nb:3 * nb])
            v.tensor_scalar(out=scz[:, 3 * nb:4 * nb], in0=scz[:, 0:nb], scalar1=rgs_ap, scalar2=None, op0=AL.mult)
            v.tensor_scalar(out=scz[:, 4 * nb:5 * nb], in0=scz[:, 3 * nb:4 * nb], scalar1=0.5, scalar2=None, op0=AL.mult)
            v.tensor_scalar(out=scz[:, 5 * nb:6 * nb], in0=scz[:, 3 * nb:4 * nb], scalar1=1e-38, scalar2=None, op0=AL.max)
            return v.reciprocal(out=scz[:, 5 * nb:6 * nb], in_=scz[:, 5 * nb:6 * nb])
        P.emit("dve", f_sel, reads=[b_scz, b_fp8, b_gs], writes=[b_scz, b_msk])

        def rv(ap_):
            return ap_.rearrange("p (nb w) -> p w nb", w=16)

        def bv(col0):
            return scz[:, col0:col0 + nb][:, None, :].broadcast_to([128, 16, nb])

        def f_ratio(v):
            v.tensor_tensor(out=rv(asg[:, 0:NN]), in0=rv(asg[:, 0:NN]), in1=bv(5 * nb), op=AL.mult)
            return v.tensor_tensor(out=rv(sgn[:, 0:NN]), in0=rv(sgn[:, 0:NN]), in1=bv(4 * nb), op=AL.mult)
        P.emit("dve", f_ratio, reads=[b_scz, b_sgn, b_asg], writes=[b_asg, b_sgn])

        def f_cq(v):
            v._custom_dve(COMPAND, out=asg[:, 0:NN], in0=asg[:, 0:NN], s0=0.5, s1=2.0, imm2=MAGIC)
            return v._custom_dve(UNCOMPAND, out=out_ap, in0=asg[:, 0:NN], in1=sgn[:, 0:NN],
                                 s0=4.0, s1=16.0, imm2=12.0)
        P.emit("dve", f_cq, reads=[b_asg, b_sgn], writes=[b_out, b_asg])

    def make_qscratch(NNmax):
        nbm = NNmax // 16
        return {
            "amax": A.alloc([128, nbm]),
            "scz": A.alloc([128, 6 * nbm]),
            "sgn": A.alloc([128, NNmax]),
            "asg": A.alloc([128, NNmax]),
            "fp8": A.alloc([128, nbm], FP8E4),
            "msk": A.alloc([128, nbm], U32),
        }

    def quant_weight(dram_w, nrt, ncols, gs6_ap, rgs_ap, dest_fn, qs, wstage, b_ws,
                     wqt, b_wqt, p_qv, b_pq, b_dest):
        P.emit("dve", lambda v: v.memset(wmaxt[:, 0:1], 0.0), writes=[b_wmax])
        for rt in range(nrt):
            par = rt % 2
            sl = wstage[:, par * ncols:(par + 1) * ncols]
            P.emit("sync", (lambda sl=sl, src=dram_w[rt * 128:(rt + 1) * 128, :]:
                            lambda s: s.dma_start(out=sl, in_=src))(),
                   writes=[b_ws[par]], dma=True)
            P.emit("dve", (lambda sl=sl: lambda v: v.tensor_reduce(
                out=wmaxt[:, 1:2], in_=sl, axis=AX.X, op=AL.max, apply_absolute_value=True))(),
                reads=[b_ws[par]], writes=[b_wmax])
            P.emit("dve", lambda v: v.tensor_tensor(out=wmaxt[:, 0:1], in0=wmaxt[:, 0:1],
                                                    in1=wmaxt[:, 1:2], op=AL.max),
                   reads=[b_wmax], writes=[b_wmax])
        P.emit("pe", lambda t: t.transpose(pb[7][:, 0:128], wmaxt[:], ident),
               reads=[b_wmax, b_id], writes=[bpb[7]])
        P.emit("act", lambda a: a.copy(out=wmaxt[:], in_=pb[7][:, 0:128]),
               reads=[bpb[7]], writes=[b_wmax])
        P.emit("dve", lambda v: v.reduce_max(out=gsm[:, 0:1], in_=wmaxt[0:1, 0:128], axis=AX.X),
               reads=[b_wmax], writes=[b_gsm])
        P.emit("pe", lambda t: t.matmul(pb[7][:, 0:1], t_or[:],
                                        gsm[:, 0:1], start=True, stop=True),
               reads=[b_gsm, b_or], writes=[bpb[7]])

        def f_gsw(v):
            v.reciprocal(out=wmaxt[:, 2:3], in_=pb[7][:, 0:1])
            v.tensor_copy(out=t_sp[:, 0:128], in_=t_sp[:, 128:256])
            v.tensor_scalar(out=gs6_ap, in0=wmaxt[:, 2:3], scalar1=448.0, scalar2=None, op0=AL.mult)
            return v.tensor_scalar(out=rgs_ap, in0=pb[7][:, 0:1], scalar1=1.0 / 2688.0, scalar2=None, op0=AL.mult)
        P.emit("dve", f_gsw, reads=[bpb[7]], writes=[b_gs, b_wmax])
        if "wd" in _dbg and not getattr(P, "_wd2_done", False):
            P._wd2_done = True
            d_gsm = nc.dram_tensor("gsmd", [1, 4], F32, kind="ExternalOutput").ap()
            d_p7 = nc.dram_tensor("p7d", [128, 4], F32, kind="ExternalOutput").ap()
            d_wmx = nc.dram_tensor("wmxd", [128, 128], F32, kind="ExternalOutput").ap()
            P.emit("sync", lambda s: s.dma_start(out=d_gsm, in_=gsm[:]), reads=[b_gsm], dma=True)
            P.emit("act", lambda a: a.copy(out=wmaxt[:, 4:8], in_=pb[7][:, 0:4]), reads=[bpb[7]], writes=[b_wmax])
            P.emit("sync", lambda s: s.dma_start(out=d_p7, in_=wmaxt[:, 4:8]), reads=[b_wmax], dma=True)
            P.emit("sync", lambda s: s.dma_start(out=d_wmx, in_=wmaxt[:]), reads=[b_wmax], dma=True)

        def wq_front(rt):
            par = rt % 2
            sl = wstage[:, par * ncols:(par + 1) * ncols]
            P.emit("sync", (lambda sl=sl, src=dram_w[rt * 128:(rt + 1) * 128, :]:
                            lambda s: s.dma_start(out=sl, in_=src))(),
                   writes=[b_ws[par]], dma=True)
            emit_quant(sl, b_ws[par], wqt[par][:, 0:ncols], b_wqt[par], gs6_ap, rgs_ap, ncols, qs[par])

        def wq_back(rt):
            par = rt % 2
            for j0 in range(0, ncols // 128, 8):
                jn = min(j0 + 8, ncols // 128)

                def f_trq(t, j0=j0, jn=jn, par=par):
                    last = None
                    for j in range(j0, jn):
                        last = t.transpose(p_qv[par][:, (j - j0) * 128:(j - j0 + 1) * 128],
                                           wqt[par][:, j * 128:(j + 1) * 128], t_idbf[:])
                    return last
                P.emit("pe", f_trq, reads=[b_wqt[par], b_idbf], writes=[b_pq[par]])

                def f_pl(a, rt=rt, j0=j0, jn=jn, par=par):
                    last = None
                    for j in range(j0, jn):
                        last = a.copy(out=dest_fn(rt, j), in_=p_qv[par][:, (j - j0) * 128:(j - j0 + 1) * 128])
                    return last
                P.emit("act", f_pl, reads=[b_pq[par]], writes=[b_dest])
                b_dest.rs = []

        for i in range(nrt + 1):
            if i < nrt:
                wq_front(i)
            if i >= 1:
                wq_back(i - 1)

    if "a" in phases:
        A.push()
        w_attn, b_wattn = A.alloc([128, 4 * NG * D], BF16)

        def wslot(mi, g):
            return w_attn[:, (mi * NG + g) * D:(mi * NG + g + 1) * D]

        A.push()
        stage_w, _ = A.alloc([128, 2 * D])
        b_stage = [Buf(), Buf()]
        for mi in range(4):
            for g in range(NG):
                par = (mi * NG + g) % 2
                sl = stage_w[:, par * D:(par + 1) * D]
                P.emit("sync", (lambda sl=sl, src=d_w[mi][g * 128:(g + 1) * 128, :]:
                                lambda s: s.dma_start(out=sl, in_=src))(),
                       writes=[b_stage[par]], dma=True)
                P.emit("act", (lambda sl=sl, dst=wslot(mi, g): lambda a: a.copy(out=dst, in_=sl))(),
                       reads=[b_stage[par]], writes=[b_wattn])
        A.pop()

        def pair(shape, dt=F32):
            t0, bb0 = A.alloc(shape, dt)
            t1, bb1 = A.alloc(shape, dt)
            return [t0, t1], [bb0, bb1]

        X2p, b_X2p = pair([128, NG * TC])
        XSQ2, b_XSQ2 = pair([128, NG * TC])
        Y2, b_Y2 = pair([128, NG * TC])
        XN2, b_XN2 = pair([128, NG * TC])
        MIX2, b_MIX2 = pair([128, 3 * NG * TC], BF16)
        _rsb, _brsb = A.alloc([128, NG * TC])
        RSB2, b_RSB2 = [_rsb, _rsb], [_brsb, _brsb]
        MSB2, b_MSB2 = pair([128, TC])
        RSTDa2, b_RSTDa2 = pair([128, TC])
        RSTDb2, b_RSTDb2 = pair([128, TC])
        SCR, b_SCR = A.alloc([128, TC])
        SF, b_SF = A.alloc([128, NG * TC])
        SBF, b_SBF = A.alloc([128, NG * TC], BF16)
        VSB2, b_VSB2 = pair([128, TC])
        KVSB2, b_KVSB2 = pair([128, TC])
        STG, b_STG = A.alloc([128, D])

        def xg(buf, g):
            return buf[:, g * TC:(g + 1) * TC]

        def g3(buf):
            return buf[:, 0:NG * TC].rearrange("p (g t) -> p g t", t=TC)

        def gt(buf):
            return buf[:, 0:NG * TC].rearrange("p (g t) -> p t g", t=TC)

        def pv(col0):
            return t_vec[:, col0:col0 + NG][:, None, :].broadcast_to([128, TC, NG])

        def bc3(tile):
            return tile[:, 0:TC][:, None, :].broadcast_to([128, NG, TC])

        def ln_stats(src_t, b_src, MSB, b_MSB, RSTD, b_RSTD, R2=None, b_R2=None, XSQ=None, b_XSQ=None):
            P.emit("act", lambda a: a.activation(out=XSQ[:], in_=src_t[:], func=AF.Square),
                   reads=[b_src], writes=[b_XSQ])

            def f_stats(t):
                for g in range(NG):
                    t.matmul(pb[6][:, 0:TC], t_om[:], xg(src_t, g), start=(g == 0), stop=(g == NG - 1))
                last = None
                for g in range(NG):
                    last = t.matmul(pb[6][:, TC:2 * TC], t_om[:], xg(XSQ, g), start=(g == 0), stop=(g == NG - 1))
                return last
            P.emit("pe", f_stats, reads=[b_src, b_XSQ, b_om], writes=[bpb[6]])

            def f_evac(a):
                a.copy(out=MSB[:], in_=pb[6][:, 0:TC])
                return a.copy(out=SCR[:], in_=pb[6][:, TC:2 * TC])
            P.emit("act", f_evac, reads=[bpb[6]], writes=[b_MSB, b_SCR])

            def f_small(v):
                v.tensor_scalar(out=MSB[:], in0=MSB[:], scalar1=1.0 / D, scalar2=None, op0=AL.mult)
                v.tensor_scalar(out=SCR[:], in0=SCR[:], scalar1=1.0 / D, scalar2=None, op0=AL.mult)
                v.tensor_tensor(out=RSTD[:], in0=MSB[:], in1=MSB[:], op=AL.mult)
                v.tensor_tensor(out=SCR[:], in0=SCR[:], in1=RSTD[:], op=AL.subtract)
                return v.tensor_scalar(out=SCR[:], in0=SCR[:], scalar1=EPS, scalar2=None, op0=AL.add)
            P.emit("dve", f_small, reads=[b_MSB, b_SCR], writes=[b_MSB, b_RSTD, b_SCR])
            if R2 is not None:
                def f_u2(v):
                    return v.tensor_scalar(out=R2[:], in0=SCR[:], scalar1=1.0 + EPS,
                                           scalar2=-EPS * EPS, op0=AL.mult, op1=AL.add)
                P.emit("dve", f_u2, reads=[b_SCR], writes=[b_R2])
                P.emit("act", lambda a: a.activation(out=R2[:], in_=R2[:], func=AF.Sqrt),
                       reads=[b_R2], writes=[b_R2])
            P.emit("act", lambda a: a.activation(out=SCR[:], in_=SCR[:], func=AF.Sqrt),
                   reads=[b_SCR], writes=[b_SCR])
            P.emit("dve", lambda v: v.reciprocal(out=RSTD[:], in_=SCR[:]),
                   reads=[b_SCR], writes=[b_RSTD])
            if R2 is not None:
                P.emit("dve", lambda v: v.reciprocal(out=R2[:], in_=R2[:]),
                       reads=[b_R2], writes=[b_R2])

        t_zero, b_zero = A.alloc([128, NG])
        P.emit("dve", lambda v: v.memset(t_zero[:], 0.0), writes=[b_zero])

        def emit_front(b, c):
                cp = (b * NCHUNK + c) % 2
                X, b_X = X2p[cp], b_X2p[cp]
                XSQ, b_XSQ = XSQ2[cp], b_XSQ2[cp]
                Y, b_Y = Y2[cp], b_Y2[cp]
                XN, b_XN = XN2[cp], b_XN2[cp]
                MIX, b_MIX = MIX2[cp], b_MIX2[cp]
                RSB, b_RSB = RSB2[cp], b_RSB2[cp]
                MSB, b_MSB = MSB2[cp], b_MSB2[cp]
                RSTD, b_RSTD = RSTDa2[cp], b_RSTDa2[cp]
                R2T, b_R2T = RSTDb2[cp], b_RSTDb2[cp]

                T0 = c * TC
                src = d_x[b].rearrange("(g p) t -> p g t", p=128)[:, :, T0:T0 + TC]
                P.emit("sync",
                       (lambda src=src, X=X: lambda s: s.dma_start(
                           out=X[:].rearrange("p (g t) -> p g t", t=TC), in_=src))(),
                       writes=[b_X], dma=True)

                P.level = 2
                P.cur_tag = "ln1"
                ln_stats(X, b_X, MSB, b_MSB, RSTD, b_RSTD, R2T, b_R2T, XSQ, b_XSQ)

                def f_ln12(v, X=X, XSQ=XSQ, Y=Y, XN=XN, MSB=MSB, RSTD=RSTD, R2T=R2T):
                    v.tensor_tensor(out=g3(XSQ), in0=g3(X), in1=bc3(MSB), op=AL.subtract)
                    v.tensor_tensor(out=g3(Y), in0=g3(XSQ), in1=bc3(RSTD), op=AL.mult)
                    v.tensor_tensor(out=g3(XN), in0=g3(XSQ), in1=bc3(R2T), op=AL.mult)
                    v.tensor_tensor(out=gt(XN), in0=gt(XN), in1=pv(0), op=AL.mult)
                    return v.tensor_tensor(out=gt(XN), in0=gt(XN), in1=pv(NG), op=AL.add)
                P.emit("dve", f_ln12, reads=[b_X, b_MSB, b_RSTD, b_R2T, b_vec],
                       writes=[b_Y, b_XSQ, b_XN])

                P.level = 4
                P.cur_tag = "mix"

                csh = t_zero if c == 0 else car_sh
                b_csh_r = b_zero if c == 0 else b_csh

                def f_mix(v, XSQ=XSQ, XN=XN, MIX=MIX, csh=csh):
                    for g in range(NG):
                        v.tensor_tensor(out=xg(XSQ, g)[:, 0:1], in0=xg(XN, g)[:, 0:1], in1=csh[:, g:g + 1], op=AL.subtract)
                    for g in range(NG):
                        v.tensor_tensor(out=xg(XSQ, g)[:, 1:], in0=xg(XN, g)[:, 1:], in1=xg(XN, g)[:, :-1], op=AL.subtract)
                    last = None
                    for mi, col in ((0, 2 * NG), (1, 3 * NG), (2, 4 * NG)):
                        mx = MIX[:, mi * NG * TC:(mi + 1) * NG * TC]
                        v.tensor_tensor(out=gt(mx), in0=gt(XSQ), in1=pv(col), op=AL.mult)
                        for g in range(NG):
                            v.tensor_tensor(out=xg(mx, g)[:, 1:], in0=xg(mx, g)[:, 1:], in1=xg(XN, g)[:, :-1], op=AL.add)
                            last = v.tensor_tensor(out=xg(mx, g)[:, 0:1], in0=xg(mx, g)[:, 0:1], in1=csh[:, g:g + 1], op=AL.add)
                    for g in range(NG):
                        last = v.tensor_copy(out=car_sh[:, g:g + 1], in_=xg(XN, g)[:, TC - 1:TC])
                    return last
                P.emit("dve", f_mix, reads=[b_XN, b_csh_r, b_vec], writes=[b_MIX, b_XSQ, b_csh])

        def emit_back(b, c):
                cp = (b * NCHUNK + c) % 2
                X, b_X = X2p[cp], b_X2p[cp]
                XSQ, b_XSQ = XSQ2[cp], b_XSQ2[cp]
                Y, b_Y = Y2[cp], b_Y2[cp]
                XN, b_XN = XN2[cp], b_XN2[cp]
                MIX, b_MIX = MIX2[cp], b_MIX2[cp]
                RSB, b_RSB = RSB2[cp], b_RSB2[cp]
                MSB, b_MSB = MSB2[cp], b_MSB2[cp]
                RSTD, b_RSTD = RSTDa2[cp], b_RSTDa2[cp]
                R2T, b_R2T = RSTDb2[cp], b_RSTDb2[cp]
                T0 = c * TC
                P.level = 5
                P.cur_tag = "kvr"
                for ec in range(NG):
                    par = ec % 2
                    kb, vb, rb = pb[2 * par][:, 0:TC], pb[2 * par][:, TC:2 * TC], pb[2 * par + 1][:, 0:TC]
                    b_kv, b_r = bpb[2 * par], bpb[2 * par + 1]
                    VSB, b_VSB = VSB2[ec % 2], b_VSB2[ec % 2]
                    KVSB, b_KVSB = KVSB2[ec % 2], b_KVSB2[ec % 2]

                    def f_kvr(t, ec=ec, kb=kb, vb=vb, rb=rb, MIX=MIX):
                        for g in range(NG):
                            t.matmul(kb, wslot(0, g)[:, ec * 128:(ec + 1) * 128],
                                     MIX[:, (0 * NG + g) * TC:(0 * NG + g + 1) * TC],
                                     start=(g == 0), stop=(g == NG - 1))
                        for g in range(NG):
                            t.matmul(vb, wslot(1, g)[:, ec * 128:(ec + 1) * 128],
                                     MIX[:, (1 * NG + g) * TC:(1 * NG + g + 1) * TC],
                                     start=(g == 0), stop=(g == NG - 1))
                        last = None
                        for g in range(NG):
                            last = t.matmul(rb, wslot(2, g)[:, ec * 128:(ec + 1) * 128],
                                            MIX[:, (2 * NG + g) * TC:(2 * NG + g + 1) * TC],
                                            start=(g == 0), stop=(g == NG - 1))
                        return last
                    P.emit("pe", f_kvr, reads=[b_MIX, b_wattn], writes=[b_kv, b_r])

                    P.emit("act", (lambda vb=vb, VSB=VSB: lambda a: a.copy(out=VSB[:], in_=vb))(),
                           reads=[b_kv], writes=[b_VSB])
                    P.emit("act", (lambda rb=rb, ec=ec, RSB=RSB: lambda a: a.activation(
                        out=xg(RSB, ec), in_=rb, func=AF.Sigmoid))(),
                        reads=[b_r], writes=[b_RSB])

                    def f_scan(v, ec=ec, kb=kb, VSB=VSB, KVSB=KVSB, c=c):
                        v.tensor_tensor(out=KVSB[:], in0=kb, in1=VSB[:], op=AL.mult)
                        v.tensor_tensor_scan(out=xg(SF, ec),
                                             data0=t_decay[:, ec:ec + 1].broadcast_to([128, TC]),
                                             data1=KVSB[:],
                                             initial=(0.0 if c == 0 else car_sc[:, ec:ec + 1]),
                                             op0=AL.mult, op1=AL.add)
                        v.tensor_copy(out=t_sp[:, 0:128], in_=t_sp[:, 128:256])
                        return v.tensor_copy(out=car_sc[:, ec:ec + 1], in_=xg(SF, ec)[:, TC - 1:TC])
                    P.emit("dve", f_scan, reads=[b_kv, b_VSB, b_decay, b_csc],
                           writes=[b_KVSB, b_SF, b_csc])
                    P.emit("act", (lambda ec=ec: lambda a: a.copy(out=xg(SBF, ec), in_=xg(SF, ec)))(),
                           reads=[b_SF], writes=[b_SBF])

                P.level = 6
                P.cur_tag = "oga"
                for ec in range(NG):
                    ob, b_ob = pb[4 + ec % 2][:, 0:TC], bpb[4 + ec % 2]
                    KVSB, b_KVSB = KVSB2[ec % 2], b_KVSB2[ec % 2]

                    def f_omm(t, ec=ec, ob=ob):
                        last = None
                        for g in range(NG):
                            last = t.matmul(ob, wslot(3, g)[:, ec * 128:(ec + 1) * 128],
                                            xg(SBF, g), start=(g == 0), stop=(g == NG - 1))
                        return last
                    P.emit("pe", f_omm, reads=[b_SBF, b_wattn], writes=[b_ob])

                    def f_gate(v, ec=ec, ob=ob, KVSB=KVSB, RSB=RSB, Y=Y, XN=XN):
                        v.tensor_tensor(out=KVSB[:], in0=ob, in1=xg(RSB, ec), op=AL.mult)
                        return v.tensor_tensor(out=xg(XN, ec), in0=KVSB[:], in1=xg(Y, ec), op=AL.add)
                    P.emit("dve", f_gate, reads=[b_ob, b_RSB, b_Y], writes=[b_KVSB, b_XN])

                P.level = 7
                P.cur_tag = "ln3"
                ln_stats(XN, b_XN, MSB, b_MSB, RSTD, b_RSTD, None, None, XSQ, b_XSQ)

                def f_ln3(v, XSQ=XSQ, XN=XN, X=X, MSB=MSB, RSTD=RSTD):
                    v.tensor_tensor(out=g3(XSQ), in0=g3(XN), in1=bc3(MSB), op=AL.subtract)
                    v.tensor_tensor(out=g3(XSQ), in0=g3(XSQ), in1=bc3(RSTD), op=AL.mult)
                    return v.tensor_tensor(out=X[:], in0=X[:], in1=XSQ[:], op=AL.add)
                P.emit("dve", f_ln3, reads=[b_XN, b_MSB, b_RSTD], writes=[b_X, b_XSQ])

                P.level = 8
                P.cur_tag = "tr"
                for sub in range(TC // 128):
                    for rnd in range(2):
                        def f_tr(t, sub=sub, rnd=rnd, X=X):
                            last = None
                            for j in range(4):
                                g = rnd * 4 + j
                                last = t.transpose(pb[7][:, j * 128:(j + 1) * 128],
                                                   xg(X, g)[:, sub * 128:(sub + 1) * 128], ident)
                            return last
                        P.emit("pe", f_tr, reads=[b_X, b_id], writes=[bpb[7]])

                        def f_ev(a, rnd=rnd):
                            return a.copy(out=STG[:, rnd * 512:(rnd + 1) * 512], in_=pb[7][:, 0:512])
                        P.emit("act", f_ev, reads=[bpb[7]], writes=[b_STG])

                    row0 = b * T + T0 + sub * 128
                    P.emit("sync",
                           (lambda row0=row0: lambda s: s.dma_start(
                               out=d_x2tm[row0:row0 + 128, :], in_=STG[:]))(),
                           reads=[b_STG], dma=True)

        bc_list = [(b, c) for b in range(B) for c in range(NCHUNK)]
        for i in range(len(bc_list) + 1):
            if i < len(bc_list):
                emit_front(*bc_list[i])
            if i >= 1:
                emit_back(*bc_list[i - 1])

        P.level = 1
        P.cur_tag = ""
        A.pop()
        P.barrier()

    if "b" in phases:
        P.cur_tag = "bc"
        A.push()
        w_gup, b_wgup = A.alloc([128, 2 * NG * I], BF16)
        qsW = [make_qscratch(1024), make_qscratch(1024)]
        p_qvW = [pb[6][:, 0:512].bitcast(BF16), pb[7][:, 0:512].bitcast(BF16)]
        b_pqW = [bpb[6], bpb[7]]

        A.push()
        wstage, _ = A.alloc([128, 2 * D])
        b_ws = [Buf(), Buf()]
        _wq0, _bwq0 = A.alloc([128, D], BF16)
        _wq1, _bwq1 = A.alloc([128, D], BF16)
        wqtW, b_wqtW = [_wq0, _wq1], [_bwq0, _bwq1]
        b_dest = Buf()

        def mk_dest(mi):
            def dest(rt, j):
                return w_gup[:, (mi * NG + j) * I + rt * 128:(mi * NG + j) * I + rt * 128 + 128]
            return dest

        quant_weight(d_gw, I // 128, D, t_gs[:, 4:5], t_gs[:, 5:6], mk_dest(0), qsW,
                     wstage, b_ws, wqtW, b_wqtW, p_qvW, b_pqW, b_dest)
        quant_weight(d_uw, I // 128, D, t_gs[:, 6:7], t_gs[:, 7:8], mk_dest(1), qsW,
                     wstage, b_ws, wqtW, b_wqtW, p_qvW, b_pqW, b_dest)
        if "wg" in _dbg:
            d_wg = nc.dram_tensor("wgdump", [128, 2 * NG * I], BF16, kind="ExternalOutput").ap()
            d_gsd = nc.dram_tensor("gsdump", [128, 8], F32, kind="ExternalOutput").ap()
            P.barrier()
            P.emit("sync", lambda s: s.dma_start(out=d_wg, in_=w_gup[:]), reads=[b_wgup], dma=True)
            P.emit("sync", lambda s: s.dma_start(out=d_gsd, in_=t_gs[:]), reads=[b_gs], dma=True)
        A.pop()
        P.barrier()

        _skipB = "wo" in _dbg
        X2, _ = A.alloc([128, 2 * D])
        b_X2 = [Buf(), Buf()]

        def bpair(shape, dt=F32):
            t0, bb0 = A.alloc(shape, dt)
            t1, bb1 = A.alloc(shape, dt)
            return [t0, t1], [bb0, bb1]

        _xn4, _bxn4 = A.alloc([128, D])
        XN42, b_XN42 = [_xn4, _xn4], [_bxn4, _bxn4]
        _xq, _bxq = A.alloc([128, D], BF16)
        XQ2, b_XQ2 = [_xq, _xq], [_bxq, _bxq]
        XQF2, b_XQF2 = bpair([128, D], BF16)
        HH2, b_HH2 = bpair([128, 1024])
        SGT2, b_SGT2 = bpair([128, 512])
        _sgu0, _bsgu0 = A.alloc([128, 512])
        SGU2, b_SGU2 = [_sgu0, _sgu0], [_bsgu0, _bsgu0]
        HQ2, b_HQ2 = bpair([128, 1024], BF16)
        HQT2, b_HQT2 = bpair([128, 1024], BF16)
        cst2p, b_cst2p = bpair([128, 8])
        qsx = make_qscratch(1024)
        _qsh = make_qscratch(1024)
        qsh2 = [_qsh, _qsh]

        p_g = [pb[0], pb[1]]
        p_u = [pb[2], pb[3]]
        bg = [bpb[0], bpb[1]]
        bu = [bpb[2], bpb[3]]
        p_tr2 = [pb[6][:, 0:512].bitcast(BF16), pb[7][:, 0:512].bitcast(BF16)]
        b_tr2 = [bpb[6], bpb[7]]

        def emit_b_front(tt):
            p2 = tt % 2
            x2sl = X2[:, p2 * D:(p2 + 1) * D]
            XN4, b_XN4 = XN42[p2], b_XN42[p2]
            XQ, b_XQ = XQ2[p2], b_XQ2[p2]
            XQF, b_XQF = XQF2[p2], b_XQF2[p2]
            HH, b_HH = HH2[p2], b_HH2[p2]
            SGT, b_SGT = SGT2[p2], b_SGT2[p2]
            SGU, b_SGU = SGU2[p2], b_SGU2[p2]
            HQ, b_HQ = HQ2[p2], b_HQ2[p2]
            HQT, b_HQT = HQT2[p2], b_HQT2[p2]
            cst, b_cst = cst2p[p2], b_cst2p[p2]
            p_qv2, b_pq2v = p_tr2[p2], b_tr2[p2]

            P.emit("sync", (lambda x2sl=x2sl, src=d_x2tm[tt * 128:(tt + 1) * 128, :]:
                            lambda s: s.dma_start(out=x2sl, in_=src))(),
                   writes=[b_X2[p2]], dma=True)

            P.emit("dve", (lambda x2sl=x2sl, cst=cst: lambda v: v.reduce_sum(out=cst[:, 0:1], in_=x2sl, axis=AX.X))(),
                   reads=[b_X2[p2]], writes=[b_cst])
            P.emit("act", (lambda x2sl=x2sl, cst=cst: lambda a: a.activation(
                out=qsx["asg"][0][:, 0:D], in_=x2sl, func=AF.Square, accum_out=cst[:, 1:2]))(),
                reads=[b_X2[p2]], writes=[qsx["asg"][1], b_cst])

            def f_small(v, cst=cst):
                v.tensor_scalar(out=cst[:, 2:3], in0=cst[:, 0:1], scalar1=1.0 / D, scalar2=None, op0=AL.mult)
                v.tensor_scalar(out=cst[:, 3:4], in0=cst[:, 1:2], scalar1=1.0 / D, scalar2=None, op0=AL.mult)
                v.tensor_copy(out=t_sp[:, 0:128], in_=t_sp[:, 128:256])
                v.tensor_tensor(out=cst[:, 4:5], in0=cst[:, 2:3], in1=cst[:, 2:3], op=AL.mult)
                v.tensor_copy(out=t_sp[:, 0:128], in_=t_sp[:, 128:256])
                v.tensor_tensor(out=cst[:, 3:4], in0=cst[:, 3:4], in1=cst[:, 4:5], op=AL.subtract)
                v.tensor_copy(out=t_sp[:, 0:128], in_=t_sp[:, 128:256])
                return v.tensor_scalar(out=cst[:, 3:4], in0=cst[:, 3:4], scalar1=EPS, scalar2=None, op0=AL.add)
            P.emit("dve", f_small, reads=[b_cst], writes=[b_cst])
            P.emit("act", (lambda cst=cst: lambda a: a.activation(out=cst[:, 4:5], in_=cst[:, 3:4], func=AF.Sqrt))(),
                   reads=[b_cst], writes=[b_cst])
            P.emit("dve", (lambda cst=cst: lambda v: v.reciprocal(out=cst[:, 3:4], in_=cst[:, 4:5]))(),
                   reads=[b_cst], writes=[b_cst])
            P.emit("dve", (lambda x2sl=x2sl, XN4=XN4, cst=cst: lambda v: v.tensor_scalar(
                out=XN4[:], in0=x2sl, scalar1=cst[:, 2:3], scalar2=cst[:, 3:4],
                op0=AL.subtract, op1=AL.mult))(),
                reads=[b_X2[p2], b_cst], writes=[b_XN4])

            emit_quant(XN4[:], b_XN4, XQ[:], b_XQ, t_gs[:, 0:1], t_gs[:, 1:2], D, qsx)

            def f_trx(t, XQ=XQ, p_qv2=p_qv2):
                last = None
                for j in range(8):
                    last = t.transpose(p_qv2[:, j * 128:(j + 1) * 128], XQ[:, j * 128:(j + 1) * 128], t_idbf[:])
                return last
            P.emit("pe", f_trx, reads=[b_XQ, b_idbf], writes=[b_pq2v])
            P.emit("act", (lambda XQF=XQF, p_qv2=p_qv2: lambda a: a.copy(out=XQF[:], in_=p_qv2[:]))(),
                   reads=[b_pq2v], writes=[b_XQF])

        def emit_b_back(tt):
            p2 = tt % 2
            XQF, b_XQF = XQF2[p2], b_XQF2[p2]

            def mm(kk):
                for half in range(2):
                    hs = kk * 2 + half
                    par = hs % 2

                    def f_gu(t, hs=hs, par=par, XQF=XQF):
                        for hc in range(NG):
                            t.matmul(p_g[par][:, 0:512], XQF[:, hc * 128:(hc + 1) * 128],
                                     w_gup[:, (0 * NG + hc) * I + hs * 512:(0 * NG + hc) * I + hs * 512 + 512],
                                     start=(hc == 0), stop=(hc == NG - 1))
                        last = None
                        for hc in range(NG):
                            last = t.matmul(p_u[par][:, 0:512], XQF[:, hc * 128:(hc + 1) * 128],
                                            w_gup[:, (1 * NG + hc) * I + hs * 512:(1 * NG + hc) * I + hs * 512 + 512],
                                            start=(hc == 0), stop=(hc == NG - 1))
                        return last
                    P.emit("pe", f_gu, reads=[b_XQF, b_wgup], writes=[bg[par], bu[par]])

            def evac(kk):
                kp = kk % 2
                HH, b_HH = HH2[kp], b_HH2[kp]
                for half in range(2):
                    hs = kk * 2 + half
                    par = hs % 2
                    SGT, b_SGT = SGT2[half], b_SGT2[half]
                    SGU, b_SGU = SGU2[half], b_SGU2[half]
                    P.emit("act", (lambda par=par, SGT=SGT: lambda a: a.activation(
                        out=SGT[:], in_=p_g[par][:, 0:512], func=AF.Silu))(),
                        reads=[bg[par]], writes=[b_SGT])
                    P.emit("act", (lambda par=par, SGU=SGU: lambda a: a.copy(
                        out=SGU[:], in_=p_u[par][:, 0:512]))(),
                        reads=[bu[par]], writes=[b_SGU])
                    P.emit("gp", (lambda half=half, SGT=SGT, SGU=SGU, HH=HH: lambda g: g.tensor_tensor(
                        out=HH[:, half * 512:(half + 1) * 512], in0=SGT[:], in1=SGU[:], op=AL.mult))(),
                        reads=[b_SGT, b_SGU], writes=[b_HH])

            def quant(kk):
                kp = kk % 2
                HH, b_HH = HH2[kp], b_HH2[kp]
                HQ, b_HQ = HQ2[kp], b_HQ2[kp]
                HQT, b_HQT = HQT2[kp], b_HQT2[kp]
                p_qv2, b_pq2v = p_tr2[kp], b_tr2[kp]
                qsh = qsh2[kp]
                emit_quant(HH[:], b_HH, HQ[:], b_HQ, t_gs[:, 2:3], t_gs[:, 3:4], 1024, qsh)

                def f_trh(t, HQ=HQ, p_qv2=p_qv2):
                    last = None
                    for j in range(8):
                        last = t.transpose(p_qv2[:, j * 128:(j + 1) * 128], HQ[:, j * 128:(j + 1) * 128], t_idbf[:])
                    return last
                P.emit("pe", f_trh, reads=[b_HQ, b_idbf], writes=[b_pq2v])
                P.emit("act", (lambda HQT=HQT, p_qv2=p_qv2: lambda a: a.copy(out=HQT[:], in_=p_qv2[:]))(),
                       reads=[b_pq2v], writes=[b_HQT])
                dst = d_hq[kk * 8:(kk + 1) * 8, :, tt * 128:(tt + 1) * 128].rearrange("i p t -> p i t")
                P.emit("sync", (lambda dst=dst, HQT=HQT: lambda s: s.dma_start(
                    out=dst, in_=HQT[:].rearrange("p (i t) -> p i t", t=128)))(),
                    reads=[b_HQT], dma=True)

            mm(0)
            evac(0)
            mm(1)
            evac(1)
            quant(0)
            mm(2)
            evac(2)
            quant(1)
            mm(3)
            evac(3)
            quant(2)
            quant(3)

        nbt = NTT if not _skipB else 0
        for i in range(nbt + 1):
            if i < nbt:
                emit_b_front(i)
            if i >= 1:
                emit_b_back(i - 1)

        A.pop()
        P.barrier()


    if "c" in phases:
        A.push()
        w_dn, b_wdn = A.alloc([128, NIC * D], BF16)

        A.push()
        _qs2 = make_qscratch(4096)
        qs2 = [_qs2, _qs2]
        wstage2, _ = A.alloc([128, 2 * I])
        b_ws2 = [Buf(), Buf()]
        _w20, _bw20 = A.alloc([128, I], BF16)
        _w21, _bw21 = A.alloc([128, I], BF16)
        wqt2W, b_wqt2W = [_w20, _w21], [_bw20, _bw21]
        b_dest2 = Buf()
        p_qv2W = [pb[6][:, 0:512].bitcast(BF16), pb[7][:, 0:512].bitcast(BF16)]
        b_pq2W = [bpb[6], bpb[7]]

        def dest_dn(rt, j):
            return w_dn[:, j * D + rt * 128:j * D + rt * 128 + 128]

        quant_weight(d_dw, D // 128, I, t_gs[:, 4:5], t_gs[:, 5:6], dest_dn, qs2,
                     wstage2, b_ws2, wqt2W, b_wqt2W, p_qv2W, b_pq2W, b_dest2)
        A.pop()
        P.barrier()

        HQF, _ = A.alloc([128, 2 * NIC * 128], BF16)
        b_HQF = [Buf(), Buf()]
        X2c, _ = A.alloc([128, 2 * D])
        b_X2c = [Buf(), Buf()]
        OC, _ = A.alloc([128, 2 * D])
        b_OC = [Buf(), Buf()]
        _s0, _bs0 = A.alloc([128, D])
        SQSp, b_SQSp = [_s0, _s0], [_bs0, _bs0]
        _c0, _bc0 = A.alloc([128, 8])
        _c1, _bc1 = A.alloc([128, 8])
        cst2p_, b_cst2p_ = [_c0, _c1], [_bc0, _bc1]

        p_o = [pb[0], pb[1], pb[2], pb[3]]
        bo = [bpb[0], bpb[1], bpb[2], bpb[3]]

        for tt in range(NTT):
            p2 = tt % 2
            SQS, b_SQS = SQSp[p2], b_SQSp[p2]
            cst2, b_cst2 = cst2p_[p2], b_cst2p_[p2]
            hsl = HQF[:, p2 * NIC * 128:(p2 + 1) * NIC * 128]
            x2sl = X2c[:, p2 * D:(p2 + 1) * D]
            osl = OC[:, p2 * D:(p2 + 1) * D]
            srch = d_hq[:, :, tt * 128:(tt + 1) * 128].rearrange("i p t -> p i t")
            P.emit("sync", (lambda hsl=hsl, srch=srch: lambda s: s.dma_start(
                out=hsl.rearrange("p (i t) -> p i t", t=128), in_=srch))(),
                writes=[b_HQF[p2]], dma=True)
            P.emit("sync", (lambda x2sl=x2sl, src=d_x2tm[tt * 128:(tt + 1) * 128, :]:
                            lambda s: s.dma_start(out=x2sl, in_=src))(),
                   writes=[b_X2c[p2]], dma=True)

            for half in range(2):
                pj = p2 * 2 + half

                def f_dn(t, hsl=hsl, half=half, pj=pj):
                    last = None
                    for ic in range(NIC):
                        last = t.matmul(p_o[pj][:, 0:512], hsl[:, ic * 128:(ic + 1) * 128],
                                        w_dn[:, ic * D + half * 512:ic * D + half * 512 + 512],
                                        start=(ic == 0), stop=(ic == NIC - 1))
                    return last
                P.emit("pe", f_dn, reads=[b_HQF[p2], b_wdn], writes=[bo[pj]])
                P.emit("dve", (lambda osl=osl, half=half, pj=pj: lambda v: v.tensor_scalar(
                    out=osl[:, half * 512:(half + 1) * 512], in0=p_o[pj][:, 0:512], scalar1=1.0, scalar2=0.0,
                    op0=AL.mult, op1=AL.add, accum_out=cst2[:, half:half + 1]))(),
                    reads=[bo[pj]], writes=[b_OC[p2], b_cst2])

            P.emit("act", (lambda osl=osl: lambda a: a.activation(
                out=SQS[:], in_=osl, func=AF.Square, accum_out=cst2[:, 2:3]))(),
                reads=[b_OC[p2]], writes=[b_SQS, b_cst2])

            def f_small2(v):
                v.tensor_tensor(out=cst2[:, 0:1], in0=cst2[:, 0:1], in1=cst2[:, 1:2], op=AL.add)
                v.tensor_scalar(out=cst2[:, 3:4], in0=cst2[:, 0:1], scalar1=1.0 / D, scalar2=None, op0=AL.mult)
                v.tensor_scalar(out=cst2[:, 4:5], in0=cst2[:, 2:3], scalar1=1.0 / D, scalar2=None, op0=AL.mult)
                v.tensor_tensor(out=cst2[:, 5:6], in0=cst2[:, 3:4], in1=cst2[:, 3:4], op=AL.mult)
                v.tensor_tensor(out=cst2[:, 4:5], in0=cst2[:, 4:5], in1=cst2[:, 5:6], op=AL.subtract)
                return v.tensor_scalar(out=cst2[:, 4:5], in0=cst2[:, 4:5], scalar1=EPS, scalar2=None, op0=AL.add)
            P.emit("dve", f_small2, reads=[b_cst2], writes=[b_cst2])
            P.emit("act", lambda a: a.activation(out=cst2[:, 5:6], in_=cst2[:, 4:5], func=AF.Sqrt),
                   reads=[b_cst2], writes=[b_cst2])
            P.emit("dve", lambda v: v.reciprocal(out=cst2[:, 4:5], in_=cst2[:, 5:6]),
                   reads=[b_cst2], writes=[b_cst2])

            def f_out(v, osl=osl, x2sl=x2sl):
                v.tensor_scalar(out=osl, in0=osl, scalar1=cst2[:, 3:4], scalar2=cst2[:, 4:5],
                                op0=AL.subtract, op1=AL.mult)
                return v.tensor_tensor(out=osl, in0=osl, in1=x2sl, op=AL.add)
            P.emit("dve", f_out, reads=[b_cst2, b_X2c[p2], b_OC[p2]], writes=[b_OC[p2]])

            P.emit("sync", (lambda osl=osl, dst=d_out[tt * 128:(tt + 1) * 128, :]:
                            lambda s: s.dma_start(out=dst, in_=osl))(),
                   reads=[b_OC[p2]], dma=True)

        A.pop()

    P.barrier()
    P.emit("sync", lambda s: s.nop())

    sems = {e: es.enter_context(nc.semaphore(f"sem_{e}")) for e in ENGS}
    block = es.enter_context(nc.Block())

    @block.tensor
    def _(t):
        P.play("pe", t, sems)

    @block.vector
    def _(v):
        P.play("dve", v, sems)

    @block.scalar
    def _(a):
        P.play("act", a, sems)

    @block.gpsimd
    def _(g):
        P.play("gp", g, sems)

    @block.sync
    def _(s):
        P.play("sync", s, sems)

    es.close()
    mybir.codegen_inst_isa_subclasses(nc)
    return nc


def _prep_inputs(inputs, B_core, T, core, n_cores):
    f32 = np.float32
    x = np.asarray(inputs["L0_attn_x"], f32)
    b0 = core * B_core
    x_fm = np.ascontiguousarray(x[b0:b0 + B_core, :T].transpose(0, 2, 1))
    vecs = np.stack([
        np.ascontiguousarray(np.asarray(inputs[k], f32).reshape(8, 128).T)
        for k in ("L0_attn_ln_w", "L0_attn_ln_b", "L0_attn_tmk", "L0_attn_tmv",
                  "L0_attn_tmr", "L0_attn_decay_w")])
    ident = np.zeros((128, 129), f32)
    ident[:, 0] = 1.0
    ident[:, 1:129] = np.eye(128, dtype=f32)
    return {
        "x_fm": x_fm,
        "wkT": np.ascontiguousarray(np.asarray(inputs["L0_attn_Wk"], f32).T),
        "wvT": np.ascontiguousarray(np.asarray(inputs["L0_attn_Wv"], f32).T),
        "wrT": np.ascontiguousarray(np.asarray(inputs["L0_attn_Wr"], f32).T),
        "woT": np.ascontiguousarray(np.asarray(inputs["L0_attn_Wo"], f32).T),
        "gate_w": np.ascontiguousarray(np.asarray(inputs["L0_ffn_gate_w"], f32)),
        "up_w": np.ascontiguousarray(np.asarray(inputs["L0_ffn_up_w"], f32)),
        "down_w": np.ascontiguousarray(np.asarray(inputs["L0_ffn_down_w"], f32)),
        "vecs": np.ascontiguousarray(vecs),
        "s_in2": np.array([[float(np.asarray(inputs["L0_ffn_s_in"]).reshape(-1)[0]),
                            float(np.asarray(inputs["L0_ffn_s_in_down"]).reshape(-1)[0])]], f32),
        "ident": ident,
        "onesrow": np.ones((1, 128), f32),
        "onesmat": np.ones((128, 128), f32),
    }


_NC_CACHE = {}


def kernel(**inputs):
    B_total = int(np.asarray(inputs["L0_attn_x"]).shape[0])
    T = int(np.asarray(inputs["L0_attn_x"]).shape[1])
    n_cores = 8
    B_core = B_total // n_cores
    key = (B_core, T)
    if key not in _NC_CACHE:
        _NC_CACHE[key] = build_program(B=B_core, T=T)
    nc = _NC_CACHE[key]
    in_maps = [_prep_inputs(inputs, B_core, T, c, n_cores) for c in range(n_cores)]
    res = run_bass_kernel_spmd(nc, in_maps, core_ids=list(range(n_cores)))
    outs = [np.asarray(r["out"]).reshape(B_core, T, 1024) for r in res.results]
    return np.concatenate(outs, axis=0)
